# version 28
# baseline (speedup 1.0000x reference)
import numpy as np
import ml_dtypes
from contextlib import ExitStack
import concourse.bass as bass
import concourse.mybir as mybir

F32 = mybir.dt.float32
BF16 = mybir.dt.bfloat16
AF = mybir.ActivationFunctionType
ALU = mybir.AluOpType
AX = mybir.AxisListType

D = 1024
HDA = 128
N_EXP = 16
HDF = 64
DHB = 64
LATENT = 256
LN_EPS = 1e-5
SCALE_A = HDA ** -0.5
SCALE_F = HDF ** -0.5
SCALE_B = DHB ** -0.5
SHIFT = 32


class Tracker:
    ENGINES = ("pe", "act", "dve", "pool", "sp")

    def __init__(self, nc):
        self.nc = nc
        self.eng = {"pe": nc.tensor, "act": nc.scalar, "dve": nc.vector,
                    "pool": nc.gpsimd, "sp": nc.sync}
        self.sem = {e: nc.alloc_semaphore(f"prog_{e}") for e in self.ENGINES}
        self.cnt = {e: 0 for e in self.ENGINES}
        self.qeng = {"qsp": "sp", "qpool": "pool"}
        self.qpool_sems = {q: [nc.alloc_semaphore(f"{q}_{i}")
                               for i in range(24 if q == "qpool" else 12)]
                           for q in self.qeng}
        self.qsem_obj = {}
        self.quse = {}
        for q, sems in self.qpool_sems.items():
            for i, sm in enumerate(sems):
                self.qsem_obj[(q, i)] = sm
                self.quse[(q, i)] = 0
        self.qfree = {q: list(range(len(self.qpool_sems[q]))) for q in self.qeng}
        self.buf_sem = {}
        self.waited = {e: {} for e in self.ENGINES}
        self.res = {}

    def R(self, *key):
        r = self.res.get(key)
        if r is None:
            r = {"w": None, "readers": []}
            self.res[key] = r
        return r

    def _sem_obj(self, key):
        return self.sem[key] if key in self.sem else self.qsem_obj[key]

    def _collect(self, e, reads, writes):
        need = {}
        def add(dep):
            skey, val = dep
            if e == "pe" and skey == "pe":
                return
            if val > need.get(skey, 0):
                need[skey] = val
        for r in reads:
            if r["w"] is not None:
                add(r["w"])
        for w in writes:
            if w["w"] is not None:
                add(w["w"])
            for d in w["readers"]:
                add(d)
        return need

    def _emit_waits(self, e, need):
        for skey, val in need.items():
            if self.waited[e].get(skey, 0) >= val:
                continue
            self.eng[e].wait_ge(self._sem_obj(skey), val)
            self.waited[e][skey] = val

    def emit(self, e, fn, reads=(), writes=()):
        self._emit_waits(e, self._collect(e, reads, writes))
        last = fn(self.eng[e])
        assert last is not None
        last.then_inc(self.sem[e], 1)
        self.cnt[e] += 1
        tok = (e, self.cnt[e])
        for w in writes:
            w["w"] = tok
            w["readers"] = []
        for r in reads:
            r["readers"].append(tok)

    def dma(self, q, dst, src, reads=(), writes=(), **kw):
        e = self.qeng[q]
        assert writes, "dma needs a written resource for its sem key"
        bk = id(writes[0])
        skey = self.buf_sem.get(bk)
        if skey is None:
            assert self.qfree[q], f"dma sem pool exhausted on {q}"
            skey = (q, self.qfree[q].pop())
            self.buf_sem[bk] = skey
        self._emit_waits(e, self._collect(e, reads, writes))
        self.eng[e].dma_start(out=dst, in_=src, **kw).then_inc(self.qsem_obj[skey], 16)
        self.quse[skey] += 16
        tok = (skey, self.quse[skey])
        for w in writes:
            w["w"] = tok
            w["readers"] = []
        for r in reads:
            r["readers"].append(tok)

    def phase_end(self):
        for e in self.ENGINES:
            for skey, used in self.quse.items():
                if used > self.waited[e].get(skey, 0):
                    self.eng[e].wait_ge(self.qsem_obj[skey], used)
                    self.waited[e][skey] = used
        self.nc.all_engine_barrier()
        for e in self.ENGINES:
            for f in self.ENGINES:
                self.waited[e][f] = self.cnt[f]
        self.buf_sem = {}
        self.qfree = {q: list(range(len(self.qpool_sems[q]))) for q in self.qeng}


def build_program(T, sublayers=(0, 1, 2, 3), dbg=()):
    NT = T // 128
    NS = NT // 2
    nc = bass.Bass("TRN2", target_bir_lowering=False)
    tk = Tracker(nc)

    d_x = nc.dram_tensor("x", [T, D], F32, kind="ExternalInput")
    d_out = nc.dram_tensor("out", [T, D], F32, kind="ExternalOutput")
    d_dbg = {i: nc.dram_tensor(f"dbg{i}", [T, D], F32, kind="ExternalOutput")
             for i in dbg}
    dw = {}
    for name, shape in [
        ("a_w", [3 * D, D]), ("a_b", [3 * D]), ("a_ow", [D, D]), ("a_ob", [D]),
        ("b_wq", [D, D]), ("b_bq", [D]), ("b_wl", [D, LATENT]), ("b_bl", [LATENT]),
        ("b_wk", [LATENT, D]), ("b_bk", [D]), ("b_wv", [LATENT, D]), ("b_bv", [D]),
        ("b_wo", [D, D]), ("b_bo", [D]),
        ("f_sim", [D, N_EXP]), ("f_gates", [N_EXP]),
        ("f_q", [N_EXP, D, HDF]), ("f_k", [N_EXP, D, HDF]),
        ("f_v", [N_EXP, D, HDF]), ("f_o", [N_EXP, HDF, D]),
    ]:
        dw[name] = nc.dram_tensor(name, shape, F32, kind="ExternalInput")
    d_wta = nc.dram_tensor("wta_scr", [D, 3 * D], BF16, kind="Internal")
    d_rbc = nc.dram_tensor("rbc_scr", [4, 512], F32, kind="Internal")
    d_wota = nc.dram_tensor("wota_scr", [D, D], BF16, kind="Internal")
    d_wqb = nc.dram_tensor("wqb_scr", [128, 8, 8, 128], BF16, kind="Internal")
    d_kall = nc.dram_tensor("kall_scr", [8, 128, T], BF16, kind="Internal")
    d_vall = nc.dram_tensor("vall_scr", [8, 128, (T // 128) * 130], BF16, kind="Internal")
    d_wcat = nc.dram_tensor("wcat_scr", [128, 8, 16, 192], BF16, kind="Internal")
    d_wob = nc.dram_tensor("wob_scr", [128, 8, 1024], BF16, kind="Internal")

    x_sb = nc.alloc_sbuf_tensor("x_sb", [128, NT, D], F32)
    st_mu = nc.alloc_sbuf_tensor("st_mu", [128, NT], F32)
    st_rstd = nc.alloc_sbuf_tensor("st_rstd", [128, NT], F32)
    st_invy = nc.alloc_sbuf_tensor("st_invy", [128, NT], F32)
    st_t1 = nc.alloc_sbuf_tensor("st_t1", [128, NT], F32)
    st_t2 = nc.alloc_sbuf_tensor("st_t2", [128, NT], F32)
    y_stg = nc.alloc_sbuf_tensor("y_stg", [128, D], F32)
    y_stgb = nc.alloc_sbuf_tensor("y_stgb", [128, D], BF16)
    ones_b = nc.alloc_sbuf_tensor("ones_b", [33, 128], BF16)
    ones_f = nc.alloc_sbuf_tensor("ones_f", [1, 128], F32)
    ident_b = nc.alloc_sbuf_tensor("ident_b", [128, 128], BF16)
    ident_f = nc.alloc_sbuf_tensor("ident_f", [128, 128], F32)
    small = nc.alloc_sbuf_tensor("small", [128, 16], F32)

    r_x = [tk.R("x", t) for t in range(NT)]
    r_mu, r_rstd, r_invy = tk.R("mu"), tk.R("rstd"), tk.R("invy")
    r_t1, r_t2 = tk.R("t1"), tk.R("t2")
    r_ystg, r_ystgb = tk.R("ystg"), tk.R("ystgb")
    r_wstg, r_wstg2 = tk.R("wstg"), tk.R("wstg2")
    r_const, r_small = tk.R("const"), tk.R("small")
    r_wcd = tk.R("f_wcd")

    tk.dma("qsp", x_sb[:, :, :], d_x.ap().rearrange("(a p) d -> p a d", p=128),
           writes=r_x)

    tk.emit("dve", lambda v: v.memset(ones_b[:, :], 1.0), writes=[r_const])
    tk.emit("dve", lambda v: v.memset(ones_f[:, :], 1.0), writes=[r_const])
    prow = nc.alloc_sbuf_tensor("prow", [128, 1], F32)
    tk.emit("pool", lambda g: g.iota(prow[:, :], [[0, 1]], channel_multiplier=1,
                                     allow_small_or_imprecise_dtypes=True), writes=[r_const])
    tk.emit("pool", lambda g: g.iota(y_stg[:, 0:128], [[1, 128]], channel_multiplier=0,
                                     allow_small_or_imprecise_dtypes=True), writes=[r_ystg])
    tk.emit("dve", lambda v: v.tensor_scalar(ident_b[:, :], y_stg[:, 0:128], prow[:, :],
                                             None, ALU.is_equal),
            reads=[r_const, r_ystg], writes=[r_const])
    tk.emit("dve", lambda v: v.tensor_scalar(ident_f[:, :], y_stg[:, 0:128], prow[:, :],
                                             None, ALU.is_equal),
            reads=[r_const, r_ystg], writes=[r_const])

    with (nc.psum_tensor("warm_ps", [128, 512], F32) as warm_ps,
          nc.sbuf_tensor("warm_sb", [128, 512], BF16) as warm_sb):
        r_warm = tk.R("warmup")
        tk.emit("dve", lambda v: v.memset(warm_sb[:, :], 0.25), writes=[r_warm])
        def wburst(te):
            last = None
            for i in range(64):
                last = te.matmul(warm_ps[:, :], warm_sb[:, 0:128], warm_sb[:, :],
                                 start=True, stop=True)
            return last
        tk.emit("pe", wburst, reads=[r_warm], writes=[r_warm])
    tk.phase_end()

    def ln_stats(for_ffn):
        for t in range(NT):
            tk.emit("dve", lambda v, t=t: v.reduce_sum(st_mu[:, t:t + 1], x_sb[:, t, :], axis=AX.X),
                    reads=[r_x[t]], writes=[r_mu])
            tk.emit("act", lambda a, t=t: a.activation(y_stgb[:, :], x_sb[:, t, :], AF.Square,
                                                       accum_out=st_t1[:, t:t + 1]),
                    reads=[r_x[t]], writes=[r_ystgb, r_t1])
        tk.emit("dve", lambda v: v.tensor_scalar(st_mu[:, :], st_mu[:, :], 1.0 / D, None, ALU.mult),
                reads=[r_mu], writes=[r_mu])
        tk.emit("dve", lambda v: v.tensor_scalar(st_t1[:, :], st_t1[:, :], 1.0 / D, None, ALU.mult),
                reads=[r_t1], writes=[r_t1])
        tk.emit("dve", lambda v: v.tensor_mul(st_t2[:, :], st_mu[:, :], st_mu[:, :]),
                reads=[r_mu], writes=[r_t2])
        tk.emit("dve", lambda v: v.tensor_sub(st_t1[:, :], st_t1[:, :], st_t2[:, :]),
                reads=[r_t1, r_t2], writes=[r_t1])
        tk.emit("dve", lambda v: v.tensor_scalar(st_t2[:, :], st_t1[:, :], LN_EPS, None, ALU.add),
                reads=[r_t1], writes=[r_t2])
        tk.emit("act", lambda a: a.activation(st_rstd[:, :], st_t2[:, :], AF.Ln),
                reads=[r_t2], writes=[r_rstd])
        tk.emit("dve", lambda v: v.tensor_scalar(st_rstd[:, :], st_rstd[:, :], -0.5, None, ALU.mult),
                reads=[r_rstd], writes=[r_rstd])
        tk.emit("act", lambda a: a.activation(st_rstd[:, :], st_rstd[:, :], AF.Exp),
                reads=[r_rstd], writes=[r_rstd])
        if for_ffn:
            tk.emit("dve", lambda v: v.reciprocal(st_invy[:, :], st_t2[:, :]),
                    reads=[r_t2], writes=[r_invy])
            tk.emit("dve", lambda v: v.tensor_mul(st_invy[:, :], st_invy[:, :], st_t1[:, :]),
                    reads=[r_invy, r_t1], writes=[r_invy])
            tk.emit("dve", lambda v: v.tensor_scalar(st_invy[:, :], st_invy[:, :], float(D),
                                                     None, ALU.mult),
                    reads=[r_invy], writes=[r_invy])
            tk.emit("act", lambda a: a.activation(st_invy[:, :], st_invy[:, :], AF.Ln),
                    reads=[r_invy], writes=[r_invy])
            tk.emit("dve", lambda v: v.tensor_scalar(st_invy[:, :], st_invy[:, :], -0.5, None, ALU.mult),
                    reads=[r_invy], writes=[r_invy])
            tk.emit("act", lambda a: a.activation(st_invy[:, :], st_invy[:, :], AF.Exp),
                    reads=[r_invy], writes=[r_invy])

    def y_tile_bf(t, p_trb, r_ptr, dst, r_dst, col):
        tk.emit("dve", lambda v: v.tensor_scalar(y_stgb[:, :], x_sb[:, t, :],
                                                 st_mu[:, t:t + 1], st_rstd[:, t:t + 1],
                                                 ALU.subtract, ALU.mult),
                reads=[r_x[t], r_mu, r_rstd], writes=[r_ystgb])
        for rnd in range(2):
            def tr(te, rnd=rnd):
                last = None
                for c in range(4):
                    cc = rnd * 4 + c
                    last = te.transpose(p_trb[:, c * 128:(c + 1) * 128],
                                        y_stgb[:, cc * 128:(cc + 1) * 128], ident_b[:, :])
                return last
            tk.emit("pe", tr, reads=[r_ystgb, r_const], writes=[r_ptr])
            tk.emit("dve", lambda v, rnd=rnd: v.tensor_copy(
                        dst[:, rnd * 4:(rnd + 1) * 4, col:col + 128],
                        p_trb[:, :].rearrange("p (c n) -> p c n", c=4)),
                    reads=[r_ptr], writes=[r_dst])

    def y_tile_f32(t, p_tr, r_ptr, dst_bf, r_dstb, dst_f, r_dstf, col):
        tk.emit("dve", lambda v: v.tensor_scalar(y_stg[:, :], x_sb[:, t, :],
                                                 st_mu[:, t:t + 1], st_rstd[:, t:t + 1],
                                                 ALU.subtract, ALU.mult),
                reads=[r_x[t], r_mu, r_rstd], writes=[r_ystg])
        for rnd in range(2):
            def tr(te, rnd=rnd):
                last = None
                for c in range(4):
                    cc = rnd * 4 + c
                    last = te.transpose(p_tr[:, c * 128:(c + 1) * 128],
                                        y_stg[:, cc * 128:(cc + 1) * 128], ident_f[:, :])
                return last
            tk.emit("pe", tr, reads=[r_ystg, r_const], writes=[r_ptr])
            tk.emit("dve", lambda v, rnd=rnd: v.tensor_copy(
                        dst_bf[:, rnd * 4:(rnd + 1) * 4, col:col + 128],
                        p_tr[:, :].rearrange("p (c n) -> p c n", c=4)),
                    reads=[r_ptr], writes=[r_dstb])
            tk.emit("dve", lambda v, rnd=rnd: v.tensor_copy(
                        dst_f[:, rnd * 4:(rnd + 1) * 4, col:col + 128],
                        p_tr[:, :].rearrange("p (c n) -> p c n", c=4)),
                    reads=[r_ptr], writes=[r_dstf])

    def y_tile_fb(t, p_tr, r_ptr, dst_bf, r_dstb, col, extra_w=()):
        tk.emit("dve", lambda v: v.tensor_scalar(y_stg[:, :], x_sb[:, t, :],
                                                 st_mu[:, t:t + 1], st_rstd[:, t:t + 1],
                                                 ALU.subtract, ALU.mult),
                reads=[r_x[t], r_mu, r_rstd], writes=[r_ystg] + list(extra_w))
        for rnd in range(2):
            def tr(te, rnd=rnd):
                last = None
                for c in range(4):
                    cc = rnd * 4 + c
                    last = te.transpose(p_tr[:, c * 128:(c + 1) * 128],
                                        y_stg[:, cc * 128:(cc + 1) * 128], ident_f[:, :])
                return last
            tk.emit("pe", tr, reads=[r_ystg, r_const], writes=[r_ptr])
            tk.emit("dve", lambda v, rnd=rnd: v.tensor_copy(
                        dst_bf[:, rnd * 4:(rnd + 1) * 4, col:col + 128],
                        p_tr[:, :].rearrange("p (c n) -> p c n", c=4)),
                    reads=[r_ptr], writes=[r_dstb])

    def store_x(dst, rkey):
        r_st = tk.R("store", rkey)
        tk.dma("qsp", dst.ap().rearrange("(a p) d -> p a d", p=128), x_sb[:, :, :],
               reads=r_x, writes=[r_st])

    def attn_a_prep():
        with ExitStack() as _st:
            pp_w = _st.enter_context(nc.psum_tensor("pp_w", [128, 1024], BF16))
            w_stg = _st.enter_context(nc.sbuf_tensor("w_stg", [128, 1024], BF16))
            w_stg2 = _st.enter_context(nc.sbuf_tensor("w_stg2", [128, 8, 128], BF16))
            r_ppw = tk.R("pp_w")
            for src, dstT, nf in ((dw["a_w"], d_wta, 24), (dw["a_ow"], d_wota, 8)):
                for fr in range(nf):
                    tk.dma("qpool", w_stg[:, :], src.ap()[fr * 128:(fr + 1) * 128, :],
                           writes=[r_wstg])
                    def tr(te):
                        last = None
                        for dc in range(8):
                            last = te.transpose(pp_w[:, dc * 128:(dc + 1) * 128],
                                                w_stg[:, dc * 128:(dc + 1) * 128], ident_b[:, :])
                        return last
                    tk.emit("pe", tr, reads=[r_wstg, r_const], writes=[r_ppw])
                    tk.emit("dve", lambda v: v.tensor_copy(
                                w_stg2[:, :, :], pp_w[:, :].rearrange("p (c n) -> p c n", c=8)),
                            reads=[r_ppw], writes=[r_wstg2])
                    tk.dma("qsp", dstT.ap().rearrange("(c p) f -> p c f", p=128)[:, :, fr * 128:(fr + 1) * 128],
                           w_stg2[:, :, :], reads=[r_wstg2], writes=[tk.R("wta_dram")])
        tk.phase_end()

    def attn_a(after_bias=None):
        with ExitStack() as _st:
            ys_a = _st.enter_context(nc.sbuf_tensor("ys_a", [128, 8, 640], BF16))
            wqk_stg = _st.enter_context(nc.sbuf_tensor("wqk_stg", [128, 2, 8, 128], BF16))
            wv_stg = _st.enter_context(nc.sbuf_tensor("wv_stg", [128, 8, 1024], BF16))
            wot_stg = _st.enter_context(nc.sbuf_tensor("wot_stg", [128, 8, 1024], BF16))
            bfm_a = _st.enter_context(nc.sbuf_tensor("bfm_a", [128, 16], F32))
            rows_a = _st.enter_context(nc.sbuf_tensor("rows_a", [33, 1024], BF16))
            q_e = _st.enter_context(nc.sbuf_tensor("q_e", [128, 8, 512], BF16))
            k_e = _st.enter_context(nc.sbuf_tensor("k_e", [128, 8, 256], BF16))
            v_e = _st.enter_context(nc.sbuf_tensor("v_e", [128, 2, 1024], BF16))
            o_ring = _st.enter_context(nc.sbuf_tensor("o_ring", [128, 8, 3, 128], BF16))
            ptf_sb = _st.enter_context(nc.sbuf_tensor("ptf_sb", [128, 4, 128], BF16))
            sm_a = _st.enter_context(nc.sbuf_tensor("sm_a", [128, 4, 3], F32))
            pa_trb = _st.enter_context(nc.psum_tensor("pa_trb", [128, 512], BF16))
            pa_sx = [_st.enter_context(nc.psum_tensor(f"pa_sx{i}", [128, 192], F32))
                     for i in range(4)]
            pa_o0 = _st.enter_context(nc.psum_tensor("pa_o0", [128, 512], F32))
            pa_o1 = _st.enter_context(nc.psum_tensor("pa_o1", [128, 512], F32))
            pa_b = _st.enter_context(nc.psum_tensor("pa_b", [128, 16], F32))
            pa_oP = (pa_o0, pa_o1)
            pa_u, pa_u2 = pa_o0, pa_o1
            r_ys = tk.R("a_ys")
            r_wqk = [tk.R("a_wqk", i) for i in range(2)]
            r_wv, r_wot = tk.R("a_wv"), tk.R("a_wot")
            r_bfm, r_rows = tk.R("a_bfm"), tk.R("a_rows")
            r_qe, r_ke = tk.R("a_qe"), tk.R("a_ke")
            r_veP = [tk.R("a_ve", i) for i in range(2)]
            r_or = [tk.R("a_or", i) for i in range(3)]
            r_ot0 = tk.R("a_ot0")
            r_smP = [tk.R("a_sm", i) for i in range(4)]
            r_ptfP = [tk.R("a_ptf", i) for i in range(4)]
            r_ptrb = tk.R("pa_trb")
            r_sxP = [tk.R("pa_sx", i) for i in range(4)]
            r_paoP = [tk.R("pa_o", i) for i in range(2)]
            r_pau, r_pau2 = r_paoP[0], r_paoP[1]
            r_pab = tk.R("pa_b")

            tk.dma("qpool", rows_a[0:1, 0:1024],
                   dw["a_b"].ap().rearrange("(o n) -> o n", o=1)[:, 2 * D:3 * D], writes=[r_rows])
            tk.dma("qpool", rows_a[32:33, 0:1024],
                   dw["a_ob"].ap().rearrange("(o n) -> o n", o=1), writes=[r_rows])
            for typ in range(2):
                for fc in range(8):
                    tk.dma("qsp", y_stg[0:1, 0:128],
                           dw["a_b"].ap().rearrange("(o n) -> o n", o=1)[:, typ * D + fc * 128: typ * D + (fc + 1) * 128],
                           writes=[r_ystg])
                    tk.emit("pe", lambda te: te.transpose(pa_b[:, 0:1],
                                                          y_stg[0:1, 0:128], ident_f[0:1, 0:1]),
                            reads=[r_ystg, r_const], writes=[r_pab])
                    tk.emit("dve", lambda v, typ=typ, fc=fc: v.tensor_copy(
                                bfm_a[:, typ * 8 + fc: typ * 8 + fc + 1], pa_b[:, 0:1]),
                            reads=[r_pab], writes=[r_bfm])

            tk.emit("dve", lambda v: v.memset(ptf_sb[:, :, :], 0.0),
                    writes=r_ptfP)
            if after_bias is not None:
                after_bias()
            ln_stats(False)

            tk.dma("qsp", wv_stg[:, :, :],
                   d_wta.ap().rearrange("(c p) f -> p c f", p=128)[:, :, 2 * D:3 * D],
                   writes=[r_wv])
            tk.dma("qsp", wot_stg[:, :, :],
                   d_wota.ap().rearrange("(c p) f -> p c f", p=128), writes=[r_wot])
            pend_u = []
            for s in range(NT // 4):
                for j in range(5):
                    y_tile_bf((4 * s + j) % NT, pa_trb, r_ptrb, ys_a, r_ys, j * 128)
                for fc in range(8):
                    slot = fc % 2
                    tk.dma("qsp", wqk_stg[:, slot, :, :],
                           d_wta.ap().rearrange("(c p) f -> p c f", p=128)[
                               :, :, fc * 128:(fc + 1) * 128],
                           writes=[r_wqk[slot]])
                    pp = fc % 2
                    def mmq(te, slot=slot, pp=pp):
                        last = None
                        for dc in range(8):
                            last = te.matmul(pa_oP[pp][:, 0:512], wqk_stg[:, slot, dc, :],
                                             ys_a[:, dc, 32:544],
                                             start=(dc == 0), stop=(dc == 7))
                        return last
                    tk.emit("pe", mmq, reads=[r_wqk[slot], r_ys], writes=[r_paoP[pp]])
                    tk.emit("dve", lambda v, fc=fc, pp=pp: v.tensor_scalar(
                                q_e[:, fc, :], pa_oP[pp][:, 0:512],
                                bfm_a[:, fc: fc + 1], None, ALU.add),
                            reads=[r_paoP[pp], r_bfm], writes=[r_qe])

                def emit_k(h):
                    for fc in range(8):
                        slot = fc % 2
                        tk.dma("qsp", wqk_stg[:, slot, :, :],
                               d_wta.ap().rearrange("(c p) f -> p c f", p=128)[
                                   :, :, D + fc * 128: D + (fc + 1) * 128],
                               writes=[r_wqk[slot]])
                        pp = fc % 2
                        def mmk2(te, slot=slot, pp=pp, h=h):
                            last = None
                            for dc in range(8):
                                last = te.matmul(pa_oP[pp][:, 0:256], wqk_stg[:, slot, dc, :],
                                                 ys_a[:, dc, 32 + h * 256: 288 + h * 256],
                                                 start=(dc == 0), stop=(dc == 7))
                            return last
                        tk.emit("pe", mmk2, reads=[r_wqk[slot], r_ys], writes=[r_paoP[pp]])
                        tk.emit("dve", lambda v, fc=fc, pp=pp: v.tensor_scalar(
                                    k_e[:, fc, :], pa_oP[pp][:, 0:256],
                                    bfm_a[:, 8 + fc: 9 + fc], None, ALU.add),
                                reads=[r_paoP[pp], r_bfm], writes=[r_ke])
                emit_k(0)
                def emit_v(tt):
                    vs = tt % 2
                    for fh in range(2):
                        def mmv(te, tt=tt, fh=fh):
                            for dc in range(8):
                                te.matmul(pa_oP[fh][:, 0:512],
                                          ys_a[:, dc, 32 + tt * 128: 160 + tt * 128],
                                          wv_stg[:, dc, fh * 512:(fh + 1) * 512],
                                          start=(dc == 0), stop=False)
                            return te.matmul(pa_oP[fh][:, 0:512], ones_b[0:1, 0:128],
                                             rows_a[0:1, fh * 512:(fh + 1) * 512],
                                             start=False, stop=True)
                        tk.emit("pe", mmv, reads=[r_ys, r_wv, r_rows, r_const], writes=[r_paoP[fh]])
                        tk.emit("dve", lambda v, vs=vs, fh=fh: v.tensor_copy(
                                    v_e[:, vs, fh * 512:(fh + 1) * 512], pa_oP[fh][:, 0:512]),
                                reads=[r_paoP[fh]], writes=[r_veP[vs]])
                emit_v(0)
                emit_v(1)

                for gj in range(4):
                    g = 4 * s + gj
                    def mms_emit(hh, gj=gj):
                        sl = hh % 4
                        sx = pa_sx[sl]
                        def mms(te, hh=hh, gj=gj, sx=sx):
                            kc0 = (gj % 2) * 128
                            te.matmul(sx[0:64, 0:64],
                                      q_e[:, hh, gj * 128: gj * 128 + 64],
                                      k_e[:, hh, kc0: kc0 + 64],
                                      start=True, stop=True)
                            return te.matmul(sx[64:128, 0:64],
                                             q_e[:, hh, gj * 128 + 64: gj * 128 + 128],
                                             k_e[:, hh, kc0 + 64: kc0 + 128],
                                             start=True, stop=True)
                        tk.emit("pe", mms, reads=[r_qe, r_ke], writes=[r_sxP[sl]])
                    for hh in range(3):
                        mms_emit(hh)
                    if pend_u:
                        pend_u.pop(0)()

                    def stageAB(hh):
                        sl = hh % 4
                        sx, r_sx = pa_sx[sl], r_sxP[sl]
                        r_ptf, r_sm = r_ptfP[sl], r_smP[sl]
                        tk.emit("dve", lambda v, sl=sl, sx=sx: v.reduce_max(
                                    sm_a[:, sl, 0:1], sx[:, 0:64], axis=AX.X),
                                reads=[r_sx], writes=[r_sm])
                        tk.emit("dve", lambda v, sl=sl: v.tensor_scalar(
                                    sm_a[:, sl, 0:1], sm_a[:, sl, 0:1], -SCALE_A, None, ALU.mult),
                                reads=[r_sm], writes=[r_sm])
                        tk.emit("act", lambda a, sl=sl, sx=sx: a.activation(
                                    ptf_sb[0:64, sl, 0:64], sx[0:64, 0:64], AF.Exp,
                                    bias=sm_a[0:64, sl, 0:1], scale=SCALE_A,
                                    accum_out=sm_a[0:64, sl, 1:2]),
                                reads=[r_sx, r_sm], writes=[r_ptf, r_sm])
                        tk.emit("act", lambda a, sl=sl, sx=sx: a.activation(
                                    ptf_sb[64:128, sl, 64:128], sx[64:128, 0:64], AF.Exp,
                                    bias=sm_a[64:128, sl, 0:1], scale=SCALE_A,
                                    accum_out=sm_a[64:128, sl, 1:2]),
                                reads=[r_sx, r_sm], writes=[r_ptf, r_sm])

                    def stageC(hh):
                        sl = hh % 4
                        r_ptf, r_sm = r_ptfP[sl], r_smP[sl]
                        tk.emit("dve", lambda v, sl=sl: v.reciprocal(sm_a[:, sl, 2:3],
                                                                     sm_a[:, sl, 1:2]),
                                reads=[r_sm], writes=[r_sm])
                        tk.emit("dve", lambda v, sl=sl: v.tensor_scalar(
                                    ptf_sb[:, sl, :], ptf_sb[:, sl, :], sm_a[:, sl, 2:3],
                                    None, ALU.mult),
                                reads=[r_ptf, r_sm], writes=[r_ptf])

                    def stageD(hh, gj=gj, g=g):
                        sl = hh % 4
                        sx, r_sx = pa_sx[sl], r_sxP[sl]
                        P2 = hh % 2
                        pa_o, r_pao = pa_oP[P2], r_paoP[P2]
                        r_ptf = r_ptfP[sl]
                        tk.emit("pe", lambda te, sl=sl: te.transpose(
                                    pa_trb[:, sl * 128:(sl + 1) * 128],
                                    ptf_sb[:, sl, :], ident_b[:, :]),
                                reads=[r_ptf, r_const], writes=[r_ptrb])
                        tk.emit("dve", lambda v, sl=sl: v.tensor_copy(
                                    ptf_sb[:, sl, :], pa_trb[:, sl * 128:(sl + 1) * 128]),
                                reads=[r_ptrb], writes=[r_ptf])
                        tk.emit("pe", lambda te, gj=gj, hh=hh, sl=sl, pa_o=pa_o: te.matmul(
                                    pa_o[:, 0:128], v_e[:, gj % 2, hh * 128:(hh + 1) * 128],
                                    ptf_sb[:, sl, :], start=True, stop=True),
                                reads=[r_veP[gj % 2], r_ptf], writes=[r_pao])
                        if hh + 3 < 8:
                            mms_emit(hh + 3)
                        for w in range(2):
                            cstart = (32 + 64 * (2 * g + w)) % T
                            pieces = []
                            tau0 = cstart // 128
                            coff = cstart % 128
                            lenA = min(64, 128 - coff)
                            pieces.append((tau0, coff, 0, lenA))
                            if lenA < 64:
                                pieces.append(((tau0 + 1) % NT, 0, lenA, 64 - lenA))
                            for (tau, dco, sco, ln_) in pieces:
                                if tau == 0:
                                    rr = r_ot0
                                    def evw(v, dco=dco, sco=sco, ln_=ln_, hh=hh, w=w, pa_o=pa_o):
                                        return v.tensor_copy(
                                            o_ring[:, hh, 2, dco:dco + ln_],
                                            pa_o[:, w * 64 + sco: w * 64 + sco + ln_])
                                else:
                                    slot = tau % 2
                                    rr = r_or[slot]
                                    def evw(v, slot=slot, dco=dco, sco=sco, ln_=ln_, hh=hh, w=w, pa_o=pa_o):
                                        return v.tensor_copy(
                                            o_ring[:, hh, slot, dco:dco + ln_],
                                            pa_o[:, w * 64 + sco: w * 64 + sco + ln_])
                                tk.emit("dve", evw, reads=[r_pao], writes=[rr])

                    for hh_ in range(10):
                        if hh_ < 8:
                            stageAB(hh_)
                        if hh_ <= 1 and pend_u:
                            pend_u.pop(0)()
                        if 1 <= hh_ <= 8:
                            stageC(hh_ - 1)
                        if hh_ >= 2:
                            stageD(hh_ - 2)
                    if gj + 2 < 4:
                        emit_v(gj + 2)
                    if gj == 1:
                        emit_k(1)
                    tau = g
                    if tau == 0:
                        continue
                    slot = tau % 2

                    def emit_mmu(fh, slot=slot):
                        pu, r_pu = (pa_u, r_pau) if fh == 0 else (pa_u2, r_pau2)
                        def mmu(te, slot=slot, fh=fh, pu=pu):
                            for aj in range(8):
                                te.matmul(pu[:, :], o_ring[:, aj, slot, :],
                                          wot_stg[:, aj, fh * 512:(fh + 1) * 512],
                                          start=(aj == 0), stop=False)
                            return te.matmul(pu[:, :], ones_b[32:33, 0:128],
                                             rows_a[32:33, fh * 512:(fh + 1) * 512],
                                             start=False, stop=True)
                        tk.emit("pe", mmu, reads=[r_or[slot], r_wot, r_rows, r_const],
                                writes=[r_pu])

                    def emit_stt(fh, tau=tau):
                        pu, r_pu = (pa_u, r_pau) if fh == 0 else (pa_u2, r_pau2)
                        tk.emit("dve", lambda v, tau=tau, fh=fh, pu=pu: v.scalar_tensor_tensor(
                                    x_sb[:, tau, fh * 512:(fh + 1) * 512], pu[:, :], 2.0,
                                    x_sb[:, tau, fh * 512:(fh + 1) * 512], ALU.mult, ALU.add),
                                reads=[r_pu, r_x[tau]], writes=[r_x[tau]])
                    pend_u.append(lambda: emit_mmu(0))
                    pend_u.append(lambda: (emit_stt(0), emit_mmu(1)))
                    pend_u.append(lambda: emit_stt(1))
            for fn in pend_u:
                fn()
            pend_u.clear()
            for fh in range(2):
                pu, r_pu = (pa_u, r_pau) if fh == 0 else (pa_u2, r_pau2)
                def mmu0(te, fh=fh):
                    for aj in range(8):
                        te.matmul(pu[:, :], o_ring[:, aj, 2, :],
                                  wot_stg[:, aj, fh * 512:(fh + 1) * 512],
                                  start=(aj == 0), stop=False)
                    return te.matmul(pu[:, :], ones_b[32:33, 0:128],
                                     rows_a[32:33, fh * 512:(fh + 1) * 512],
                                     start=False, stop=True)
                tk.emit("pe", mmu0, reads=[r_ot0, r_wot, r_rows, r_const], writes=[r_pu])
                tk.emit("dve", lambda v, fh=fh, pu=pu: v.scalar_tensor_tensor(
                            x_sb[:, 0, fh * 512:(fh + 1) * 512], pu[:, :], 2.0,
                            x_sb[:, 0, fh * 512:(fh + 1) * 512], ALU.mult, ALU.add),
                        reads=[r_pu, r_x[0]], writes=[r_x[0]])
        tk.phase_end()

    NQT = T // 512

    def ffn(ci):
        with ExitStack() as _st:
            A = lambda nm, shape, dt: _st.enter_context(
                nc.sbuf_tensor(f"{nm}{ci}", shape, dt))
            simn32 = A("f_simn", [128, 8, N_EXP], F32)
            sgb = A("f_sgb", [128, N_EXP], F32)
            rw_buf = A("f_rw", [128, NT, N_EXP], F32)
            qf = A("f_qf", [64, T], BF16)
            kf = A("f_kf", [64, T], BF16)
            vaug = A("f_vaug", [128, NT, 65], BF16)
            gsc = A("f_gsc", [128, 8, N_EXP], F32)
            gcol = A("f_gcol", [128, 8], F32)
            nrm_q = A("f_nrmq", [128, NT], F32)
            nrm_k = A("f_nrmk", [128, NT], F32)
            ones2d = A("f_ones2d", [128, 64], F32)
            mbc = A("f_mbc", [128, 1], F32)
            sc1 = A("f_sc1", [1, 16], F32)
            srow = y_stg[:, 512:1024]
            R32 = y_stg[0:64, 0:512]

            r_simn, r_sgb, r_wcat = tk.R("f_simn"), tk.R("f_sgb"), tk.R("f_wcat")
            NTH = NT // 2
            r_rw = [tk.R("f_rw", t) for t in range(NT)]
            r_acc = [tk.R("f_acc", t) for t in range(NT)]
            r_qf, r_kf, r_vaug = tk.R("f_qf"), tk.R("f_kf"), tk.R("f_vaug")
            r_yq4, r_yq4f = tk.R("f_yq4"), tk.R("f_yq4f")
            r_gsc, r_gcol, r_nrm = tk.R("f_gsc"), tk.R("f_gcol"), tk.R("f_nrm")
            r_srow, r_R32, r_o2d = r_ystg, r_ystg, tk.R("f_o2d")
            r_mbc, r_sc1 = tk.R("f_mbc"), tk.R("f_sc1")

            tk.emit("dve", lambda v: v.memset(ones2d[:, :], 1.0), writes=[r_o2d])
            tk.emit("dve", lambda v: v.memset(vaug[:, :, 64:65], 1.0), writes=[r_vaug])
            with nc.sbuf_tensor(f"f_sim32_{ci}", [128, 8, N_EXP], F32) as sim32:
                r_sim32 = tk.R("f_sim32")
                tk.dma("qsp", sim32[:, :, :],
                       dw["f_sim"].ap().rearrange("(c p) e -> p c e", p=128),
                       writes=[r_sim32])
                with (nc.sbuf_tensor(f"f_smsq_{ci}", [128, 8, N_EXP], F32) as smsq,
                      nc.psum_tensor(f"f_pn_{ci}", [128, 512], F32) as pf_n):
                    r_smsq, r_pfn = tk.R("f_smsq"), tk.R("f_pn")
                    tk.emit("dve", lambda v: v.tensor_mul(smsq[:, :, :], sim32[:, :, :],
                                                          sim32[:, :, :]),
                            reads=[r_sim32], writes=[r_smsq])
                    def mmn(te):
                        last = None
                        for dc in range(8):
                            last = te.matmul(pf_n[0:1, 0:16], ones2d[:, 0:1],
                                             smsq[:, dc, :], start=(dc == 0), stop=(dc == 7))
                        return last
                    tk.emit("pe", mmn, reads=[r_smsq, r_o2d], writes=[r_pfn])
                    tk.emit("act", lambda a: a.activation(sc1[0:1, :], pf_n[0:1, 0:16], AF.Ln),
                            reads=[r_pfn], writes=[r_sc1])
                    tk.emit("dve", lambda v: v.tensor_scalar(sc1[0:1, :], sc1[0:1, :], -0.5,
                                                             None, ALU.mult),
                            reads=[r_sc1], writes=[r_sc1])
                    tk.emit("act", lambda a: a.activation(sc1[0:1, :], sc1[0:1, :], AF.Exp),
                            reads=[r_sc1], writes=[r_sc1])
                    tk.emit("pe", lambda te: te.matmul(pf_n[:, 32:48], ones_f[0:1, :],
                                                       sc1[0:1, :], start=True, stop=True),
                            reads=[r_sc1, r_const], writes=[r_pfn])
                    tk.emit("dve", lambda v: v.tensor_copy(sgb[:, :], pf_n[:, 32:48]),
                            reads=[r_pfn], writes=[r_sgb])
                    for dc in range(8):
                        tk.emit("dve", lambda v, dc=dc: v.tensor_mul(
                                    simn32[:, dc, :], sim32[:, dc, :], sgb[:, :]),
                                reads=[r_sim32, r_sgb], writes=[r_simn])
                    tk.dma("qsp", sc1[0:1, :], dw["f_gates"].ap().rearrange("(o n) -> o n", o=1),
                           writes=[r_sc1])
                    tk.emit("act", lambda a: a.activation(sc1[0:1, :], sc1[0:1, :], AF.Exp,
                                                          scale=-1.0),
                            reads=[r_sc1], writes=[r_sc1])
                    tk.emit("dve", lambda v: v.tensor_scalar(sc1[0:1, :], sc1[0:1, :], 1.0,
                                                             None, ALU.add),
                            reads=[r_sc1], writes=[r_sc1])
                    tk.emit("dve", lambda v: v.reciprocal(sc1[0:1, :], sc1[0:1, :]),
                            reads=[r_sc1], writes=[r_sc1])
                    tk.emit("pe", lambda te: te.matmul(pf_n[:, 64:80], ones_f[0:1, :],
                                                       sc1[0:1, :], start=True, stop=True),
                            reads=[r_sc1, r_const], writes=[r_pfn])
                    tk.emit("dve", lambda v: v.tensor_copy(sgb[:, :], pf_n[:, 64:80]),
                            reads=[r_pfn], writes=[r_sgb])
                    ln_stats(True)
            tk.phase_end()

            with ExitStack() as _s1:
                pf_tr2 = _s1.enter_context(nc.psum_tensor(f"f_ptr_{ci}", [128, 2, 512], F32))
                pf_raw = _s1.enter_context(nc.psum_tensor(f"f_praw_{ci}", [128, 16], F32))
                pf_gT = _s1.enter_context(nc.psum_tensor(f"f_pgT_{ci}", [128, 128], F32))
                praw_sb = _s1.enter_context(nc.sbuf_tensor(f"f_prs{ci}", [16, 128], F32))
                pf_q = _s1.enter_context(nc.psum_tensor(f"f_pq_{ci}", [128, 4, 512], F32))
                wcat = _s1.enter_context(nc.sbuf_tensor(f"f_wcat{ci}", [128, 8, 4, 384], BF16))
                acc = _s1.enter_context(nc.sbuf_tensor(f"f_acc{ci}", [128, NTH, 192], F32))
                yq42 = _s1.enter_context(nc.sbuf_tensor(f"f_yq4_{ci}", [128, 2, 8, 128], BF16))
                yq4f = _s1.enter_context(nc.sbuf_tensor(f"f_yq4f_{ci}", [128, 8, 128], F32))
                pf_tr = pf_tr2[:, 0, :]
                r_ptr2 = [tk.R("f_ptr", i) for i in range(2)]
                r_yq42 = [tk.R("f_yq4s", i) for i in range(2)]
                r_ptr, r_praw = r_ptr2[0], tk.R("f_praw")
                r_gT, r_prs = tk.R("f_gT"), tk.R("f_prs")
                r_pq = [tk.R("f_pq", i) for i in range(4)]
                def y_norm(t):
                    tk.emit("dve", lambda v, t=t: v.tensor_scalar(
                                y_stg[:, :], x_sb[:, t, :],
                                st_mu[:, t:t + 1], st_rstd[:, t:t + 1],
                                ALU.subtract, ALU.mult),
                            reads=[r_x[t], r_mu, r_rstd], writes=[r_ystg])

                def y_tr(t, with_f32):
                    sl = (t % NTH) % 2
                    for rnd in range(2):
                        def tr(te, rnd=rnd):
                            last = None
                            for c in range(4):
                                cc = rnd * 4 + c
                                last = te.transpose(pf_tr2[:, rnd, c * 128:(c + 1) * 128],
                                                    y_stg[:, cc * 128:(cc + 1) * 128],
                                                    ident_f[:, :])
                            return last
                        tk.emit("pe", tr, reads=[r_ystg, r_const], writes=[r_ptr2[rnd]])
                        tk.emit("dve", lambda v, rnd=rnd, sl=sl: v.tensor_copy(
                                    yq42[:, sl, rnd * 4:(rnd + 1) * 4, :],
                                    pf_tr2[:, rnd, :].rearrange("p (c n) -> p c n", c=4)),
                                reads=[r_ptr2[rnd]], writes=[r_yq42[sl]])
                        if with_f32:
                            tk.emit("dve", lambda v, rnd=rnd: v.tensor_copy(
                                        yq4f[:, rnd * 4:(rnd + 1) * 4, :],
                                        pf_tr2[:, rnd, :].rearrange("p (c n) -> p c n", c=4)),
                                    reads=[r_ptr2[rnd]], writes=[r_yq4f])

                for half, wave in [(h, w) for h in range(2) for w in range(2)]:
                    tk.dma("qpool", wcat[:, :, :, :],
                           d_wcat.ap()[:, :, wave * 8:(wave + 1) * 8, :].rearrange(
                               "p c (pr k) g -> p c pr (k g)", k=2),
                           reads=[r_wcd], writes=[r_wcat])
                    y_norm(half * NTH)
                    y_tr(half * NTH, wave == 0)
                    for tl in range(NTH):
                        t = half * NTH + tl
                        sl = tl % 2
                        yq4 = yq42[:, sl]
                        if wave == 0:
                            def mmr(te):
                                last = None
                                for dc in range(8):
                                    last = te.matmul(pf_gT[0:16, 0:128], simn32[:, dc, :],
                                                     yq4f[:, dc, :],
                                                     start=(dc == 0), stop=(dc == 7))
                                return last
                            tk.emit("pe", mmr, reads=[r_yq4f, r_simn], writes=[r_gT])
                            tk.emit("act", lambda a: a.copy(
                                        praw_sb[0:16, 0:128], pf_gT[0:16, 0:128]),
                                    reads=[r_gT], writes=[r_prs])
                            tk.emit("pe", lambda te: te.transpose(
                                        pf_raw[:, 0:16], praw_sb[0:16, 0:128],
                                        ident_f[0:16, 0:16]),
                                    reads=[r_prs, r_const], writes=[r_praw])
                            g0, g1, g2, g3, g4 = (gsc[:, i, :] for i in range(5))
                            m1, m2, iA, gs = (gcol[:, i:i + 1] for i in range(4))
                            tk.emit("dve", lambda v, t=t: v.tensor_scalar(
                                        g0, pf_raw[:, :], st_invy[:, t:t + 1], None, ALU.mult),
                                    reads=[r_praw, r_invy], writes=[r_gsc])
                            tk.emit("dve", lambda v: v.tensor_sub(g0, g0, sgb[:, :]),
                                    reads=[r_gsc, r_sgb], writes=[r_gsc])
                            tk.emit("dve", lambda v: v.tensor_scalar_max(g1, g0, 0.0),
                                    reads=[r_gsc], writes=[r_gsc])
                            tk.emit("dve", lambda v: v.tensor_single_scalar(g2, g0, 0.0, ALU.is_gt),
                                    reads=[r_gsc], writes=[r_gsc])
                            tk.emit("dve", lambda v: v.reduce_max(m1, g0, axis=AX.X),
                                    reads=[r_gsc], writes=[r_gcol])
                            tk.emit("dve", lambda v: v.tensor_single_scalar(iA, m1, 0.0, ALU.is_le),
                                    reads=[r_gcol], writes=[r_gcol])
                            tk.emit("dve", lambda v: v.tensor_scalar(g3, g0, m1, None, ALU.is_equal),
                                    reads=[r_gsc, r_gcol], writes=[r_gsc])
                            tk.emit("dve", lambda v: v.scalar_tensor_tensor(
                                        g4, g3, -1e30, g0, ALU.mult, ALU.add),
                                    reads=[r_gsc], writes=[r_gsc])
                            tk.emit("dve", lambda v: v.reduce_max(m2, g4, axis=AX.X),
                                    reads=[r_gsc], writes=[r_gcol])
                            tk.emit("dve", lambda v: v.tensor_scalar(g4, g4, m2, None, ALU.is_equal),
                                    reads=[r_gsc, r_gcol], writes=[r_gsc])
                            tk.emit("dve", lambda v: v.tensor_add(g3, g3, g4),
                                    reads=[r_gsc], writes=[r_gsc])
                            tk.emit("dve", lambda v: v.scalar_tensor_tensor(
                                        g2, g3, iA, g2, ALU.mult, ALU.add),
                                    reads=[r_gsc, r_gcol], writes=[r_gsc])
                            tk.emit("act", lambda a: a.activation(g1, g1, AF.Exp),
                                    reads=[r_gsc], writes=[r_gsc])
                            tk.emit("dve", lambda v: v.tensor_mul(g1, g1, g2),
                                    reads=[r_gsc], writes=[r_gsc])
                            tk.emit("dve", lambda v: v.reduce_sum(gs, g1, axis=AX.X),
                                    reads=[r_gsc], writes=[r_gcol])
                            tk.emit("dve", lambda v: v.reciprocal(gs, gs),
                                    reads=[r_gcol], writes=[r_gcol])
                            tk.emit("dve", lambda v, t=t: v.tensor_scalar(
                                        rw_buf[:, t, :], g1, gs, None, ALU.mult),
                                    reads=[r_gsc, r_gcol], writes=[r_rw[t]])
                        for pr in range(4):
                            def mmw(te, pr=pr, yq4=yq4):
                                last = None
                                for dc in range(8):
                                    last = te.matmul(pf_q[:, pr, 0:384], yq4[:, dc, :],
                                                     wcat[:, dc, pr, :],
                                                     start=(dc == 0), stop=(dc == 7))
                                return last
                            tk.emit("pe", mmw, reads=[r_yq42[sl], r_wcat], writes=[r_pq[pr]])
                        if tl + 1 < NTH:
                            y_norm(t + 1)
                        for pr in range(4):
                            for j in range(2):
                                e = wave * 8 + pr * 2 + j
                                if e == 0:
                                    tk.emit("dve", lambda v, t=t, tl=tl, pr=pr, j=j: v.tensor_scalar(
                                                acc[:, tl, :], pf_q[:, pr, j * 192:(j + 1) * 192],
                                                rw_buf[:, t, e:e + 1], None, ALU.mult),
                                            reads=[r_pq[pr], r_rw[t]], writes=[r_acc[t]])
                                else:
                                    tk.emit("dve", lambda v, t=t, tl=tl, pr=pr, j=j, e=e: v.scalar_tensor_tensor(
                                                acc[:, tl, :], pf_q[:, pr, j * 192:(j + 1) * 192],
                                                rw_buf[:, t, e:e + 1], acc[:, tl, :],
                                                ALU.mult, ALU.add),
                                            reads=[r_pq[pr], r_rw[t]], writes=[r_acc[t]])
                        if tl + 1 < NTH:
                            y_tr(t + 1, wave == 0)
                    if wave != 1:
                        continue
                    for tl2 in range(NTH):
                        t = half * NTH + tl2
                        tk.emit("pe", lambda te, tl2=tl2: te.transpose(
                                    pf_tr[0:64, 0:128], acc[:, tl2, 0:64], ident_f[:, :]),
                                reads=[r_acc[t], r_const], writes=[r_ptr])
                        tk.emit("dve", lambda v, t=t: v.tensor_copy(
                                    qf[0:64, t * 128:(t + 1) * 128], pf_tr[0:64, 0:128]),
                                reads=[r_ptr], writes=[r_qf])
                        tk.emit("pe", lambda te, tl2=tl2: te.transpose(
                                    pf_tr[0:64, 128:256], acc[:, tl2, 64:128], ident_f[:, :]),
                                reads=[r_acc[t], r_const], writes=[r_ptr])
                        tk.emit("dve", lambda v, t=t: v.tensor_copy(
                                    kf[0:64, t * 128:(t + 1) * 128], pf_tr[0:64, 128:256]),
                                reads=[r_ptr], writes=[r_kf])
                        tk.emit("dve", lambda v, t=t, tl2=tl2: v.tensor_copy(
                                    vaug[:, t, 0:64], acc[:, tl2, 128:192]),
                                reads=[r_acc[t]], writes=[r_vaug])
                        tk.emit("dve", lambda v, tl2=tl2: v.tensor_mul(
                                    yq4f[:, 0, 0:64], acc[:, tl2, 0:64], acc[:, tl2, 0:64]),
                                reads=[r_acc[t]], writes=[r_yq4f])
                        tk.emit("dve", lambda v, t=t: v.reduce_sum(
                                    nrm_q[:, t:t + 1], yq4f[:, 0, 0:64], axis=AX.X),
                                reads=[r_yq4f], writes=[r_nrm])
                        tk.emit("dve", lambda v, tl2=tl2: v.tensor_mul(
                                    yq4f[:, 0, 0:64], acc[:, tl2, 64:128], acc[:, tl2, 64:128]),
                                reads=[r_acc[t]], writes=[r_yq4f])
                        tk.emit("dve", lambda v, t=t: v.reduce_sum(
                                    nrm_k[:, t:t + 1], yq4f[:, 0, 0:64], axis=AX.X),
                                reads=[r_yq4f], writes=[r_nrm])
                tk.emit("dve", lambda v: v.reduce_max(gcol[:, 0:1], nrm_q[:, :], axis=AX.X),
                        reads=[r_nrm], writes=[r_gcol])
                tk.emit("dve", lambda v: v.reduce_max(gcol[:, 1:2], nrm_k[:, :], axis=AX.X),
                        reads=[r_nrm], writes=[r_gcol])
                tk.emit("dve", lambda v: v.tensor_add(gcol[:, 2:3], gcol[:, 0:1], gcol[:, 1:2]),
                        reads=[r_gcol], writes=[r_gcol])
                tk.emit("pe", lambda te: te.transpose(pf_tr[0:1, 0:128], gcol[:, 2:3],
                                                      ident_f[:, :]),
                        reads=[r_gcol, r_const], writes=[r_ptr])
                tk.emit("dve", lambda v: v.reduce_max(sc1[0:1, 0:1], pf_tr[0:1, 0:128], axis=AX.X),
                        reads=[r_ptr], writes=[r_sc1])
                tk.emit("dve", lambda v: v.tensor_scalar(sc1[0:1, 0:1], sc1[0:1, 0:1],
                                                         -0.525 * SCALE_F, None, ALU.mult),
                        reads=[r_sc1], writes=[r_sc1])
                tk.emit("pe", lambda te: te.matmul(pf_raw[:, 0:1], ones_f[0:1, :],
                                                   sc1[0:1, 0:1], start=True, stop=True),
                        reads=[r_sc1, r_const], writes=[r_praw])
                tk.emit("dve", lambda v: v.tensor_copy(mbc[:, :], pf_raw[:, 0:1]),
                        reads=[r_praw], writes=[r_mbc])
            tk.phase_end()

            af = _st.enter_context(nc.sbuf_tensor(f"f_af{ci}", [64, T], BF16))
            r_af = tk.R("f_af")
            with ExitStack() as _s3:
                pf_sc = _s3.enter_context(nc.psum_tensor(f"f_psc_{ci}", [128, 2, 1536], F32))
                pf_out = _s3.enter_context(nc.psum_tensor(f"f_pout_{ci}", [65, 512], F32))
                pf_msc = _s3.enter_context(nc.psum_tensor(f"f_pmsc_{ci}", [128, 512], F32))
                ptst = _s3.enter_context(nc.sbuf_tensor(f"f_ptst{ci}", [128, 2, 1536], BF16))
                masks = _s3.enter_context(nc.sbuf_tensor(f"f_masks{ci}", [128, 4, 512], BF16))
                r_psc = [tk.R("f_psc", i) for i in range(2)]
                r_pout, r_pmsc = tk.R("f_pout"), tk.R("f_pmsc")
                r_ptst = [tk.R("f_ptst", i) for i in range(2)]
                r_masks, r_io = tk.R("f_masks"), tk.R("f_io")
                tk.emit("pool", lambda g: g.iota(y_stg[:, 0:512], [[1, 512]],
                                                 channel_multiplier=0,
                                                 allow_small_or_imprecise_dtypes=True),
                        writes=[r_ystg])
                for r in range(4):
                    tk.emit("dve", lambda v, r=r: v.tensor_scalar(
                                y_stg[:, 512:1024], y_stg[:, 0:512], prow[:, :],
                                float(-128 * r), ALU.subtract, ALU.add),
                            reads=[r_ystg, r_const], writes=[r_ystg])
                    tk.emit("dve", lambda v, r=r: v.tensor_single_scalar(
                                y_stg[:, 512:1024], y_stg[:, 512:1024], 0.0, ALU.is_lt),
                            reads=[r_ystg], writes=[r_ystg])
                    tk.emit("dve", lambda v, r=r: v.tensor_scalar(
                                masks[:, r, :], y_stg[:, 512:1024], -1e30, None, ALU.mult),
                            reads=[r_ystg], writes=[r_masks])
                pend3 = []
                for qt in range(NQT):
                    nkc = 4 * (qt + 1)
                    groups = [list(range(g, min(g + 3, nkc))) for g in range(0, nkc, 3)]

                    def mmsc(te, grp, par, qt=qt):
                        last = None
                        for j, kc in enumerate(grp):
                            last = te.matmul(pf_sc[:, par, j * 512:(j + 1) * 512],
                                             kf[0:64, kc * 128:(kc + 1) * 128],
                                             qf[0:64, qt * 512:(qt + 1) * 512],
                                             start=True, stop=True)
                        return last

                    def emit_sc(gi):
                        par = gi % 2
                        grp = groups[gi]
                        tk.emit("pe", lambda te, grp=grp, par=par: mmsc(te, grp, par),
                                reads=[r_kf, r_qf], writes=[r_psc[par]])
                        for j, kc in enumerate(grp):
                            r = kc - 4 * qt
                            if 0 <= r <= 3:
                                tk.emit("dve", lambda v, par=par, j=j, r=r: v.tensor_tensor(
                                            pf_sc[:, par, j * 512:(j + 1) * 512],
                                            pf_sc[:, par, j * 512:(j + 1) * 512],
                                            masks[:, r, :], ALU.add),
                                        reads=[r_masks], writes=[r_psc[par]])

                    def mmpv(te, grp, par, nkc=nkc):
                        last = None
                        for j, kc in enumerate(grp):
                            last = te.matmul(pf_out[:, :], vaug[:, kc, :],
                                             ptst[:, par, j * 512:(j + 1) * 512],
                                             start=(kc == 0), stop=(kc == nkc - 1))
                        return last

                    emit_sc(0)
                    for fn in pend3:
                        fn()
                    pend3.clear()
                    for gi, grp in enumerate(groups):
                        par = gi % 2
                        ng = len(grp)
                        if gi > 0:
                            tk.emit("pe", lambda te, gi=gi: mmpv(te, groups[gi - 1], (gi - 1) % 2),
                                    reads=[r_vaug, r_ptst[(gi - 1) % 2]], writes=[r_pout])
                        tk.emit("act", lambda a, par=par, ng=ng: a.activation(
                                    ptst[:, par, 0:ng * 512], pf_sc[:, par, 0:ng * 512],
                                    AF.Exp, bias=mbc[:, :], scale=SCALE_F),
                                reads=[r_psc[par], r_mbc], writes=[r_ptst[par]])
                        if gi + 1 < len(groups):
                            emit_sc(gi + 1)
                    tk.emit("pe", lambda te: mmpv(te, groups[-1], (len(groups) - 1) % 2),
                            reads=[r_vaug, r_ptst[(len(groups) - 1) % 2]], writes=[r_pout])
                    tk.emit("dve", lambda v, qt=qt: v.tensor_copy(
                                af[0:64, qt * 512:(qt + 1) * 512], pf_out[0:64, :]),
                            reads=[r_pout], writes=[r_af])
                    tk.emit("dve", lambda v: v.tensor_copy(srow[64:65, :], pf_out[64:65, :]),
                            reads=[r_pout], writes=[r_srow])
                    tk.emit("act", lambda a: a.activation(srow[64:65, :], srow[64:65, :], AF.Ln),
                            reads=[r_srow], writes=[r_srow])
                    tk.emit("act", lambda a: a.activation(srow[64:65, :], srow[64:65, :],
                                                          AF.Exp, scale=-1.0),
                            reads=[r_srow], writes=[r_srow])
                    tk.emit("pe", lambda te: te.matmul(pf_msc[0:64, :], ones2d[64:65, :],
                                                       srow[64:65, :], start=True, stop=True),
                            reads=[r_srow, r_o2d], writes=[r_pmsc])
                    tk.emit("act", lambda a: a.copy(R32[:, :], pf_msc[0:64, :]),
                            reads=[r_pmsc], writes=[r_R32])

                    def tail3(qt=qt):
                        tk.emit("dve", lambda v, qt=qt: v.tensor_mul(
                                    af[0:64, qt * 512:(qt + 1) * 512],
                                    af[0:64, qt * 512:(qt + 1) * 512], R32[:, :]),
                                reads=[r_R32], writes=[r_af])
                    pend3.append(tail3)
                for fn in pend3:
                    fn()
            tk.phase_end()

            with ExitStack() as _s4:
                pf_trb = _s4.enter_context(nc.psum_tensor(f"f_ptrb_{ci}", [128, 2, 512], BF16))
                pf_ab = _s4.enter_context(nc.psum_tensor(f"f_pab_{ci}", [128, 2, 64], BF16))
                pf_u = _s4.enter_context(nc.psum_tensor(f"f_pu_{ci}", [128, 512], F32))
                pf_u2 = _s4.enter_context(nc.psum_tensor(f"f_pu2_{ci}", [128, 512], F32))
                atok = _s4.enter_context(nc.sbuf_tensor(f"f_atok{ci}", [128, NT, 64], BF16))
                opcat = _s4.enter_context(nc.sbuf_tensor(f"f_opcat{ci}", [128, 8, 1024], BF16))
                acat = _s4.enter_context(nc.sbuf_tensor(f"f_acat{ci}", [128, 2, D], BF16))
                afm = _s4.enter_context(nc.sbuf_tensor(f"f_afm{ci}", [128, 2, 8, 128], BF16))
                r_ptrb = [tk.R("f_ptrb", i) for i in range(2)]
                r_pab = [tk.R("f_pab", i) for i in range(2)]
                r_pu, r_pu2 = tk.R("f_pu"), tk.R("f_pu2")
                r_atok, r_opc = tk.R("f_atok"), tk.R("f_opc")
                r_acat = [tk.R("f_acat", i) for i in range(2)]
                r_afm = [tk.R("f_afm", i) for i in range(2)]
                tk.dma("qpool", opcat[:, :, :], d_wob.ap(),
                       reads=[r_wcd], writes=[r_opc])

                def trf_emit(t):
                    sl = t % 2
                    tk.emit("pe", lambda te, t=t, sl=sl: te.transpose(
                                pf_ab[:, sl, :], af[0:64, t * 128:(t + 1) * 128],
                                ident_b[0:64, 0:64]),
                            reads=[r_af, r_const], writes=[r_pab[sl]])
                    tk.emit("dve", lambda v, t=t, sl=sl: v.tensor_copy(atok[:, t, :],
                                                                       pf_ab[:, sl, :]),
                            reads=[r_pab[sl]], writes=[r_atok])

                def acat_emit(t):
                    sl = t % 2
                    for e in range(N_EXP):
                        tk.emit("dve", lambda v, t=t, e=e, sl=sl: v.tensor_scalar(
                                    acat[:, sl, e * 64:(e + 1) * 64], atok[:, t, :],
                                    rw_buf[:, t, e:e + 1], None, ALU.mult),
                                reads=[r_atok, r_rw[t]], writes=[r_acat[sl]])

                trf_emit(0)
                acat_emit(0)
                for t in range(NT):
                    sl = t % 2
                    if t + 1 < NT:
                        trf_emit(t + 1)
                    for rnd in range(2):
                        def tra(te, rnd=rnd, sl=sl):
                            last = None
                            for c in range(4):
                                cc = rnd * 4 + c
                                last = te.transpose(pf_trb[:, sl, c * 128:(c + 1) * 128],
                                                    acat[:, sl, cc * 128:(cc + 1) * 128],
                                                    ident_b[:, :])
                            return last
                        tk.emit("pe", tra, reads=[r_acat[sl], r_const], writes=[r_ptrb[sl]])
                        tk.emit("dve", lambda v, rnd=rnd, sl=sl: v.tensor_copy(
                                    afm[:, sl, rnd * 4:(rnd + 1) * 4, :],
                                    pf_trb[:, sl, :].rearrange("p (c n) -> p c n", c=4)),
                                reads=[r_ptrb[sl]], writes=[r_afm[sl]])
                    for fh in range(2):
                        pu, r_puu = (pf_u, r_pu) if fh == 0 else (pf_u2, r_pu2)
                        def mmu(te, fh=fh, pu=pu, sl=sl):
                            last = None
                            for c in range(8):
                                last = te.matmul(pu[:, :], afm[:, sl, c, :],
                                                 opcat[:, c, fh * 512:(fh + 1) * 512],
                                                 start=(c == 0), stop=(c == 7))
                            return last
                        tk.emit("pe", mmu, reads=[r_afm[sl], r_opc], writes=[r_puu])
                    if t + 1 < NT:
                        acat_emit(t + 1)
                    for fh in range(2):
                        pu, r_puu = (pf_u, r_pu) if fh == 0 else (pf_u2, r_pu2)
                        tk.emit("dve", lambda v, t=t, fh=fh, pu=pu: v.tensor_tensor(
                                    x_sb[:, t, fh * 512:(fh + 1) * 512], pu[:, :],
                                    x_sb[:, t, fh * 512:(fh + 1) * 512], ALU.add),
                                reads=[r_puu, r_x[t]], writes=[r_x[t]])
        tk.phase_end()

    def emit_attn_b_pair(qr, su, prl, pr, pb_sc, rb, pb_oA, pb_oB,
                         r_poA, r_poB, kp, r_kp, qq, r_qq, vp, r_vp,
                         ptstB, r_ptstB, mba, r_mba, o_q, r_oq,
                         ostg, r_ostg, QTS, NQTQ, NKC,
                         tail_st, stg_h0, r_tailr, r_tailb, r_tstg, r_rbcd, d_rbc,
                         r_qsq2):
        pend_tail = []
        last_tail = [None]
        for qt in range(NQTQ):
            qcol = qt * QTS
            groups = [list(range(g, min(g + 2, NKC))) for g in range(0, NKC, 2)]

            def mmsc(te, grp, h01, qcol=qcol):
                last = None
                for kc in grp:
                    jb = kc % 3
                    last = te.matmul(
                        pb_sc[:, h01, jb * 512:jb * 512 + QTS],
                        kp[h01 * 64:(h01 + 1) * 64, kc * 128:(kc + 1) * 128],
                        qq[h01 * 64:(h01 + 1) * 64, pr, qcol:qcol + QTS],
                        start=True, stop=True,
                        tile_position=(h01 * 64, 0))
                return last

            def mmpv(te, grp, h01, gpar):
                pb_o = pb_oA if h01 == 0 else pb_oB
                last = None
                for i, kc in enumerate(grp):
                    last = te.matmul(pb_o[0:65, 0:QTS],
                                     vp[:, kc, 65 * h01:65 * h01 + 65],
                                     ptstB[:, h01, gpar, i * QTS:(i + 1) * QTS],
                                     start=(kc == 0), stop=(kc == NKC - 1))
                return last

            def emit_sc(gi):
                grp = groups[gi]
                def both(te, grp=grp):
                    mmsc(te, grp, 0)
                    return mmsc(te, grp, 1)
                tk.emit("pe", both, reads=[r_kp, r_qq],
                        writes=[rb[h01][kc % 3] for kc in grp for h01 in range(2)])

            def emit_exp(gi):
                grp = groups[gi]
                gpar = gi % 2
                jbs = [kc % 3 for kc in grp]
                for h01 in range(2):
                    bias = mba[:, h01 * 8 + pr:h01 * 8 + pr + 1]
                    if QTS == 512 and len(grp) == 2 and jbs[1] == jbs[0] + 1:
                        tk.emit("act", lambda a, h01=h01, gpar=gpar, j0=jbs[0]: a.activation(
                                    ptstB[:, h01, gpar, 0:1024],
                                    pb_sc[:, h01, j0 * 512:(j0 + 2) * 512],
                                    AF.Exp, bias=mba[:, h01 * 8 + pr:h01 * 8 + pr + 1],
                                    scale=SCALE_B),
                                reads=[rb[h01][jbs[0]], rb[h01][jbs[1]], r_mba],
                                writes=[r_ptstB[h01][gpar]])
                    else:
                        for i, kc in enumerate(grp):
                            tk.emit("act", lambda a, h01=h01, gpar=gpar, i=i, jb=jbs[i]: a.activation(
                                        ptstB[:, h01, gpar, i * QTS:(i + 1) * QTS],
                                        pb_sc[:, h01, jb * 512:jb * 512 + QTS],
                                        AF.Exp, bias=mba[:, h01 * 8 + pr:h01 * 8 + pr + 1],
                                        scale=SCALE_B),
                                    reads=[rb[h01][jbs[i]], r_mba],
                                    writes=[r_ptstB[h01][gpar]])

            emit_sc(0)
            for fn in pend_tail:
                fn()
            pend_tail.clear()
            def emit_pv(gi):
                def both(te, gi=gi):
                    mmpv(te, groups[gi], 0, gi % 2)
                    return mmpv(te, groups[gi], 1, gi % 2)
                tk.emit("pe", both,
                        reads=[r_vp, r_ptstB[0][gi % 2], r_ptstB[1][gi % 2]],
                        writes=[r_poA, r_poB])

            for gi, grp in enumerate(groups):
                if gi > 0:
                    emit_pv(gi - 1)
                emit_exp(gi)
                if gi + 1 < len(groups):
                    emit_sc(gi + 1)
            emit_pv(len(groups) - 1)
            for h01 in range(2):
                pb_o = pb_oA if h01 == 0 else pb_oB
                r_po = r_poA if h01 == 0 else r_poB
                sl = h01
                stg = stg_h0[:, 0:QTS] if h01 == 0 else ostg[:, 0:QTS]
                wl_ = ([r_tstg[0], r_qsq2[1]] if h01 == 0
                       else [r_tstg[1], r_qsq2[0], r_ystgb])
                tk.emit("dve", lambda v, pb_o=pb_o, stg=stg: v.tensor_copy(
                            stg, pb_o[0:64, 0:QTS]),
                        reads=[r_po], writes=wl_)
                tk.emit("dve", lambda v, pb_o=pb_o, sl=sl: v.tensor_copy(
                            tail_st[64:65, sl * 512:sl * 512 + QTS], pb_o[64:65, 0:QTS]),
                        reads=[r_po], writes=[r_tailr[sl]])
            for sl in range(2):
                tk.emit("act", lambda a, sl=sl: a.activation(
                            tail_st[64:65, sl * 512:sl * 512 + QTS],
                            tail_st[64:65, sl * 512:sl * 512 + QTS], AF.Ln),
                        reads=[r_tailr[sl]], writes=[r_tailr[sl]])
                tk.emit("act", lambda a, sl=sl: a.activation(
                            tail_st[64:65, sl * 512:sl * 512 + QTS],
                            tail_st[64:65, sl * 512:sl * 512 + QTS], AF.Exp, scale=-1.0),
                        reads=[r_tailr[sl]], writes=[r_tailr[sl]])
                tk.dma("qsp", d_rbc.ap()[sl:sl + 1, 0:QTS],
                       tail_st[64:65, sl * 512:sl * 512 + QTS],
                       reads=[r_tailr[sl]], writes=[r_rbcd[sl]])
                dsrc = d_rbc.ap()[sl:sl + 1, 0:QTS]
                dsrc_b = bass.AP(dsrc.tensor, dsrc.offset, [[0, 64], [1, QTS]])
                tk.dma("qsp", tail_st[0:64, sl * 512:sl * 512 + QTS], dsrc_b,
                       reads=[r_rbcd[sl]], writes=[r_tailb[sl]])

            def tail_b(qt=qt, qcol=qcol):
                for h01 in range(2):
                    sl = h01
                    if h01 == 0:
                        tk.emit("dve", lambda v, prl=prl, qcol=qcol, sl=sl: v.tensor_mul(
                                    o_q[0:64, prl, qcol:qcol + QTS],
                                    stg_h0[:, 0:QTS],
                                    tail_st[0:64, sl * 512:sl * 512 + QTS]),
                                reads=[r_tstg[sl], r_tailb[sl]], writes=[r_oq])
                    else:
                        tk.emit("dve", lambda v, sl=sl: v.tensor_mul(
                                    ostg[:, 0:QTS], ostg[:, 0:QTS],
                                    tail_st[0:64, sl * 512:sl * 512 + QTS]),
                                reads=[r_tailb[sl]],
                                writes=[r_tstg[1]])
                        tk.dma("qpool", o_q[64:128, prl, qcol:qcol + QTS], ostg[:, 0:QTS],
                               reads=[r_tstg[1]], writes=[r_oq])
            if qt + 1 < NQTQ:
                pend_tail.append(tail_b)
            else:
                last_tail[0] = tail_b
        return last_tail[0]

    def attn_b():
        NT4 = NT // 4
        QL = T // 4
        NKC = NT
        QTS = min(512, QL)
        NQTQ = max(1, QL // QTS)
        GT = 4 if NT4 % 4 == 0 else (2 if NT4 % 2 == 0 else 1)
        npc = max(1, QL // 512)
        qplen = min(512, QL)
        with ExitStack() as _st:
            A = lambda nm, shape, dt: _st.enter_context(nc.sbuf_tensor(nm, shape, dt))
            blfm = A("b_blfm", [128, 2], F32)
            bqfm = A("b_bqfm", [128, 8], F32)
            bkfm = A("b_bkfm", [128, 8], F32)
            bvfm = A("b_bvfm", [128, 8], F32)
            rows_b = A("b_rows", [1, 640], BF16)
            ybq = A("b_ybq", [128, 8, 128 * GT], BF16)
            H2 = A("b_H2", [128, 2], BF16)
            mqk = A("b_mqk", [128, 12], F32)
            mks = A("b_mks", [128, 8], F32)
            kp = A("b_kp", [128, T], BF16)
            vp = A("b_vp", [128, NT, 130], BF16)
            wk_stg = A("b_wks", [128, 1, 2, 128], BF16)
            wv_stg = A("b_wvs", [128, 1, 2, 128], BF16)
            _lat_cm = nc.sbuf_tensor("b_lat", [128, 2, T], BF16)
            latent = _lat_cm.__enter__()
            mba = small
            ostg = y_stgb[0:64, 0:512]

            r_lat, r_wl = tk.R("b_lat"), tk.R("b_wl")
            r_blfm, r_bqfm, r_bkfm, r_rows = tk.R("b_blfm"), tk.R("b_bqfm"), tk.R("b_bkfm"), tk.R("b_rows")
            r_bvfm = tk.R("b_bvfm")
            r_kald, r_vald = tk.R("d_kall"), tk.R("d_vall")
            r_ybq = tk.R("b_ybq")
            r_wqs = [tk.R("b_wqs", i) for i in range(2)]
            r_wks = [tk.R("b_wks", i) for i in range(2)]
            r_wvs = [tk.R("b_wvs", i) for i in range(2)]
            r_wos = tk.R("b_wos")
            r_qq, r_kp, r_vp = tk.R("b_qq"), tk.R("b_kp"), tk.R("b_vp")
            r_oq = tk.R("b_oq")
            r_ptstB = [[tk.R("b_ptst", i, p) for p in range(2)] for i in range(2)]
            r_H2, r_mqk = tk.R("b_H2"), tk.R("b_mqk")
            r_mks, r_mba = tk.R("b_mks"), r_small
            r_qsq2 = [tk.R("b_qsqA"), tk.R("b_qsqB")]
            r_tailr = [tk.R("b_tailr", i) for i in range(2)]
            r_tailb = [tk.R("b_tailb", i) for i in range(2)]
            r_tstg = [tk.R("b_tstg", i) for i in range(2)]
            r_rbcd = [tk.R("b_rbcd", i) for i in range(2)]
            r_ostg = r_ystgb

            tk.emit("dve", lambda v: v.memset(H2[:, :], 0.0), writes=[r_H2])
            tk.emit("dve", lambda v: v.memset(H2[0:64, 0:1], 1.0), writes=[r_H2])
            tk.emit("dve", lambda v: v.memset(H2[64:128, 1:2], 1.0), writes=[r_H2])
            r_wqd = tk.R("b_wqd")
            tk.dma("qpool", d_wqb.ap(),
                   dw["b_wq"].ap().rearrange("(c p) (fc g) -> p fc c g", p=128, g=128),
                   writes=[r_wqd])
            with (nc.psum_tensor("b_ptr", [128, 512], BF16) as pb_tr,
                  nc.psum_tensor("b_pl", [128, 2, 512], F32) as pb_l,
                  nc.psum_tensor("b_pb", [128, 2], F32) as pb_b,
                  nc.sbuf_tensor("b_wl_sb", [128, 8, 256], BF16) as wl):
                tk.dma("qpool", wl[:, :, :],
                       dw["b_wl"].ap().rearrange("(c p) f -> p c f", p=128), writes=[r_wl])
                r_pbtr, r_pbb = tk.R("b_ptr"), tk.R("b_pb")
                r_pbl = [tk.R("b_pl", i) for i in range(2)]
                for dstb, r_dstb, src, n in ((blfm, r_blfm, dw["b_bl"], 2),
                                             (bqfm, r_bqfm, dw["b_bq"], 8),
                                             (bkfm, r_bkfm, dw["b_bk"], 8),
                                             (bvfm, r_bvfm, dw["b_bv"], 8)):
                    for c in range(n):
                        tk.dma("qsp", y_stg[0:1, 0:128],
                               src.ap().rearrange("(o n) -> o n", o=1)[:, c * 128:(c + 1) * 128],
                               writes=[r_ystg])
                        tk.emit("pe", lambda te: te.transpose(pb_b[:, 0:1], y_stg[0:1, 0:128],
                                                              ident_f[0:1, 0:1]),
                                reads=[r_ystg, r_const], writes=[r_pbb])
                        tk.emit("dve", lambda v, dstb=dstb, c=c: v.tensor_copy(
                                    dstb[:, c:c + 1], pb_b[:, 0:1]),
                                reads=[r_pbb], writes=[r_dstb])
                ln_stats(False)
                for g0 in range(0, NT, GT):
                    for j in range(GT):
                        y_tile_bf(g0 + j, pb_tr, r_pbtr, ybq, r_ybq, j * 128)
                    for lc in range(2):
                        def mml(te, lc=lc, n=128 * GT):
                            last = None
                            for dc in range(8):
                                last = te.matmul(pb_l[:, lc, 0:n], wl[:, dc, lc * 128:(lc + 1) * 128],
                                                 ybq[:, dc, 0:n], start=(dc == 0), stop=(dc == 7))
                            return last
                        tk.emit("pe", mml, reads=[r_wl, r_ybq], writes=[r_pbl[lc]])
                        tk.emit("dve", lambda v, lc=lc, g0=g0, n=128 * GT: v.tensor_scalar(
                                    latent[:, lc, g0 * 128: g0 * 128 + n], pb_l[:, lc, 0:n],
                                    blfm[:, lc:lc + 1], None, ALU.add),
                                reads=[r_pbl[lc], r_blfm], writes=[r_lat])
            tk.phase_end()

            tail_st = y_stg
            stg_h0 = y_stgb[0:64, 512:1024]

            with (nc.psum_tensor("kva_A", [128, 512], F32) as pvA,
                  nc.psum_tensor("kva_B", [128, 512], F32) as pvB,
                  nc.psum_tensor("kva_V0", [128, 512], F32) as pvV0,
                  nc.psum_tensor("kva_V1", [128, 512], F32) as pvV1,
                  nc.psum_tensor("kva_T", [128, 512], F32) as ptrK,
                  nc.named_scope("bKVall", notify=True)):
                r_pvK = [tk.R("kva_A"), tk.R("kva_B")]
                r_pvV = [tk.R("kva_V", i) for i in range(2)]
                r_ptrK = [tk.R("kva_T", j) for j in range(4)]
                r_vtmp = [tk.R("kva_vt", i) for i in range(2)]
                vfm2 = [y_stg[:, 0:512], y_stg[:, 512:1024]]
                tk.emit("dve", lambda v: v.memset(vp[:, :, 64:130:65], 1.0), writes=[r_vp])
                for pri, pr8 in enumerate([1, 2, 3, 4, 5, 6, 7, 0]):
                    tk.dma("qpool", wk_stg[:, 0, :, :],
                           dw["b_wk"].ap().rearrange("(c p) f -> p c f", p=128)[
                               :, :, pr8 * 128:(pr8 + 1) * 128],
                           writes=[r_wks[0]])
                    tk.dma("qpool", wv_stg[:, 0, :, :],
                           dw["b_wv"].ap().rearrange("(c p) f -> p c f", p=128)[
                               :, :, pr8 * 128:(pr8 + 1) * 128],
                           writes=[r_wvs[0]])
                    for ct in range(8):
                        pk, r_pk = (pvA, r_pvK[0]) if ct % 2 == 0 else (pvB, r_pvK[1])
                        def mmk(te, ct=ct, pk=pk):
                            last = None
                            for lc in range(2):
                                last = te.matmul(pk[:, :], wk_stg[:, 0, lc, :],
                                                 latent[:, lc, ct * 512:(ct + 1) * 512],
                                                 start=(lc == 0), stop=(lc == 1))
                            return last
                        tk.emit("pe", mmk, reads=[r_wks[0], r_lat], writes=[r_pk])
                        tk.emit("dve", lambda v, pr8=pr8, ct=ct, pk=pk: v.tensor_scalar(
                                    kp[:, ct * 512:(ct + 1) * 512], pk[:, :],
                                    bkfm[:, pr8:pr8 + 1], None, ALU.add),
                                reads=[r_pk, r_bkfm], writes=[r_kp])
                    tk.dma("qsp", d_kall.ap()[pr8], kp[:, :], reads=[r_kp], writes=[r_kald])
                    for ct in range(8):
                        par = ct % 2
                        pv, r_pv = (pvV0, r_pvV[0]) if par == 0 else (pvV1, r_pvV[1])
                        def mmv(te, ct=ct, pv=pv):
                            last = None
                            for lc in range(2):
                                last = te.matmul(pv[:, :], wv_stg[:, 0, lc, :],
                                                 latent[:, lc, ct * 512:(ct + 1) * 512],
                                                 start=(lc == 0), stop=(lc == 1))
                            return last
                        tk.emit("pe", mmv, reads=[r_wvs[0], r_lat], writes=[r_pv])
                        tk.emit("dve", lambda v, par=par, pv=pv, pr8=pr8: v.tensor_scalar(
                                    vfm2[par], pv[:, :],
                                    bvfm[:, pr8:pr8 + 1], None, ALU.add),
                                reads=[r_pv, r_bvfm], writes=[r_ystg])
                        def trv(te, par=par):
                            last = None
                            for j in range(4):
                                last = te.transpose(
                                    ptrK[:, j * 128:(j + 1) * 128],
                                    y_stg[:, par * 512 + j * 128: par * 512 + (j + 1) * 128],
                                    ident_f[:, :])
                            return last
                        tk.emit("pe", trv, reads=[r_ystg, r_const], writes=[r_ptrK[0]])
                        sl_v = par
                        tk.emit("dve", lambda v, sl_v=sl_v: v.tensor_copy(
                                    ybq[:, sl_v * 4:(sl_v + 1) * 4, 0:128],
                                    ptrK[:, :].rearrange("p (c n) -> p c n", c=4)),
                                reads=[r_ptrK[0]], writes=[r_vtmp[sl_v]])
                        for j in range(4):
                            tk.emit("dve", lambda v, ct=ct, j=j, sl_v=sl_v: v.tensor_copy(
                                        vp[:, ct * 4 + j, :].rearrange(
                                            "p (h c) -> p h c", h=2)[:, :, 0:64],
                                        ybq[:, sl_v * 4 + j, 0:128].rearrange(
                                            "p (h c) -> p h c", h=2)),
                                    reads=[r_vtmp[sl_v]], writes=[r_vp])
                    tk.dma("qsp", d_vall.ap()[pr8].rearrange("p (a c) -> p a c", a=NT),
                           vp[:, :, :], reads=[r_vp], writes=[r_vald])
                    tk.emit("dve", lambda v, pr8=pr8: v.memset(mks[0:2, pr8:pr8 + 1], 0.0),
                            writes=[r_mks])
                    npk = T // 512

                    def mk_mul(pc):
                        half = pc % 2
                        qv = y_stgb[:, half * 512:(half + 1) * 512]
                        wl_ = [r_qsq2[half]]
                        if pc < 2:
                            wl_ += [r_ystgb, r_tstg[1 - half]]
                        tk.emit("dve", lambda v, pc=pc, qv=qv: v.tensor_mul(
                                    qv, kp[:, pc * 512:(pc + 1) * 512],
                                    kp[:, pc * 512:(pc + 1) * 512]),
                                reads=[r_kp], writes=wl_)

                    def mk_mm(pc):
                        half = pc % 2
                        pk, r_pk = (pvA, r_pvK[0]) if pc % 2 == 0 else (pvB, r_pvK[1])
                        qv = y_stgb[:, half * 512:(half + 1) * 512]
                        tk.emit("pe", lambda te, pk=pk, qv=qv: te.matmul(
                                    pk[0:2, :], H2[:, :],
                                    qv, start=True, stop=True),
                                reads=[r_qsq2[half], r_H2], writes=[r_pk])

                    def mk_red(pc):
                        pk, r_pk = (pvA, r_pvK[0]) if pc % 2 == 0 else (pvB, r_pvK[1])
                        tk.emit("dve", lambda v, pk=pk: v.reduce_max(
                                    mqk[0:2, 10:11], pk[0:2, :], axis=AX.X),
                                reads=[r_pk], writes=[r_mqk])
                        tk.emit("dve", lambda v, pr8=pr8: v.tensor_max(
                                    mks[0:2, pr8:pr8 + 1], mks[0:2, pr8:pr8 + 1],
                                    mqk[0:2, 10:11]),
                                reads=[r_mqk, r_mks], writes=[r_mks])

                    for i in range(npk + 2):
                        if i < npk:
                            mk_mul(i)
                        if 1 <= i <= npk:
                            mk_mm(i - 1)
                        if i >= 2:
                            mk_red(i - 2)
            tk.phase_end()
            _lat_cm.__exit__(None, None, None)
            wq_stg = A("b_wqs", [128, 1, 8, 128], BF16)
            wo_stg = A("b_wos", [128, 2, 512], BF16)
            qq = A("b_qq", [128, 8, QL], BF16)
            o_q = A("b_oq", [128, 2, QL], BF16)
            ptstB = A("b_ptst", [128, 2, 2, 2 * QTS], BF16)

            with (nc.psum_tensor("b_psc", [128, 2, 1536], F32) as pb_sc,
                  nc.psum_tensor("b_poA", [128, 512], F32) as pb_oA,
                  nc.psum_tensor("b_poB", [128, 512], F32) as pb_oB):
                rb = [[tk.R("b_psc", h, j) for j in range(3)] for h in range(2)]
                r_poA, r_poB = tk.R("b_poA"), tk.R("b_poB")

                def qproj(qr):
                    _q = nc.named_scope(f"bQ{qr}", notify=True); _q.__enter__()
                    def dmah(fc, h):
                        tk.dma("qpool", wq_stg[:, 0, h * 4:(h + 1) * 4, :],
                               d_wqb.ap()[:, fc, h * 4:(h + 1) * 4, :],
                               reads=[r_wqd], writes=[r_wqs[h]])
                    for g0 in range(0, NT4, GT):
                        for j in range(GT):
                            y_tile_fb(qr * NT4 + g0 + j, pb_sc[:, 0, 0:512], rb[0][0],
                                      ybq, r_ybq, j * 128,
                                      extra_w=[r_tailr[0], r_tailr[1],
                                               r_tailb[0], r_tailb[1]])
                        dmah(0, 0)
                        dmah(0, 1)
                        for fc in range(8):
                            pq, r_pq = (pb_oA, r_poA) if fc % 2 == 0 else (pb_oB, r_poB)
                            def mmqh(te, h, pq=pq, n=128 * GT):
                                last = None
                                for dc in range(h * 4, h * 4 + 4):
                                    last = te.matmul(pq[:, 0:n],
                                                     wq_stg[:, 0, dc, :],
                                                     ybq[:, dc, 0:n],
                                                     start=(dc == 0), stop=(dc == 7))
                                return last
                            tk.emit("pe", lambda te, mmqh=mmqh: mmqh(te, 0),
                                    reads=[r_wqs[0], r_ybq], writes=[r_pq])
                            if fc + 1 < 8:
                                dmah(fc + 1, 0)
                            tk.emit("pe", lambda te, mmqh=mmqh: mmqh(te, 1),
                                    reads=[r_wqs[1], r_ybq], writes=[r_pq])
                            if fc + 1 < 8:
                                dmah(fc + 1, 1)
                            tk.emit("dve", lambda v, fc=fc, pq=pq, g0=g0, n=128 * GT: v.tensor_scalar(
                                        qq[:, fc, g0 * 128: g0 * 128 + n],
                                        pq[:, 0:n],
                                        bqfm[:, fc:fc + 1], None, ALU.add),
                                    reads=[r_pq, r_bqfm], writes=[r_qq])
                    _q.__exit__(None, None, None)

                def normbatch(qr):
                    _n = nc.named_scope(f"bN{qr}", notify=True); _n.__enter__()
                    nsi = 8 * npc

                    def nb_mul(si):
                        pr8, pc = si // npc, si % npc
                        half = si % 2
                        qv = y_stgb[:, half * 512: half * 512 + qplen]
                        wl_ = [r_qsq2[half]]
                        if si < 2:
                            wl_ += [r_ystgb, r_tstg[1 - half]]
                        tk.emit("dve", lambda v, pr8=pr8, pc=pc, qv=qv: v.tensor_mul(
                                    qv, qq[:, pr8, pc * 512: pc * 512 + qplen],
                                    qq[:, pr8, pc * 512: pc * 512 + qplen]),
                                reads=[r_qq], writes=wl_)

                    def nb_mm(si):
                        half = si % 2
                        s6 = si % 6
                        h_, j_ = s6 // 3, s6 % 3
                        qv = y_stgb[:, half * 512: half * 512 + qplen]
                        tk.emit("pe", lambda te, h_=h_, j_=j_, qv=qv: te.matmul(
                                    pb_sc[0:2, h_, j_ * 512: j_ * 512 + qplen], H2[:, :],
                                    qv, start=True, stop=True),
                                reads=[r_qsq2[half], r_H2], writes=[rb[h_][j_]])

                    def nb_red(si):
                        pr8, pc = si // npc, si % npc
                        s6 = si % 6
                        h_, j_ = s6 // 3, s6 % 3
                        dcol = (pr8, pr8 + 1) if pc == 0 else (8, 9)
                        tk.emit("dve", lambda v, h_=h_, j_=j_, dcol=dcol: v.reduce_max(
                                    mqk[0:2, dcol[0]:dcol[1]],
                                    pb_sc[0:2, h_, j_ * 512: j_ * 512 + qplen], axis=AX.X),
                                reads=[rb[h_][j_]], writes=[r_mqk])
                        if pc > 0:
                            tk.emit("dve", lambda v, pr8=pr8: v.tensor_max(
                                        mqk[0:2, pr8:pr8 + 1], mqk[0:2, pr8:pr8 + 1],
                                        mqk[0:2, 8:9]),
                                    reads=[r_mqk], writes=[r_mqk])

                    for i in range(nsi + 2):
                        if i < nsi:
                            nb_mul(i)
                        if 1 <= i <= nsi:
                            nb_mm(i - 1)
                        if i >= 2:
                            nb_red(i - 2)
                    tk.emit("dve", lambda v: v.memset(ybq[0:2, 0:2, 0:128], 1.0), writes=[r_ybq])
                    tk.emit("dve", lambda v: v.tensor_scalar(
                                ybq[0:2, 1, 0:128], ybq[0:2, 1, 0:128], prow[0:2, 0:1],
                                None, ALU.mult),
                            reads=[r_const], writes=[r_ybq])
                    tk.emit("dve", lambda v: v.tensor_sub(
                                ybq[0:2, 0, 0:128], ybq[0:2, 0, 0:128], ybq[0:2, 1, 0:128]),
                            reads=[r_ybq], writes=[r_ybq])
                    _n.__exit__(None, None, None)

                def mbc_combine_all():
                    tk.emit("dve", lambda v: v.tensor_add(mqk[0:2, 0:8], mqk[0:2, 0:8],
                                                          mks[0:2, 0:8]),
                            reads=[r_mqk, r_mks], writes=[r_mqk])
                    tk.emit("dve", lambda v: v.tensor_scalar(mqk[0:2, 0:8], mqk[0:2, 0:8],
                                                             -0.525 * SCALE_B, None, ALU.mult),
                            reads=[r_mqk], writes=[r_mqk])
                    tk.emit("dve", lambda v: v.tensor_copy(y_stgb[0:2, 0:8], mqk[0:2, 0:8]),
                            reads=[r_mqk], writes=[r_qsq2[0], r_ystgb, r_tstg[1]])
                    for h01 in range(2):
                        pq, r_pq = (pb_oA, r_poA) if h01 == 0 else (pb_oB, r_poB)
                        tk.emit("pe", lambda te, h01=h01, pq=pq: te.matmul(
                                    pq[:, 0:8], ybq[0:2, h01, 0:128], y_stgb[0:2, 0:8],
                                    start=True, stop=True),
                                reads=[r_ybq, r_qsq2[0]], writes=[r_pq])
                        tk.emit("dve", lambda v, h01=h01, pq=pq: v.tensor_copy(
                                    mba[:, h01 * 8:h01 * 8 + 8], pq[:, 0:8]),
                                reads=[r_pq], writes=[r_mba])

                def load_kv(pr8):
                    tk.dma("qsp", kp[:, :], d_kall.ap()[pr8],
                           reads=[r_kald], writes=[r_kp])
                    tk.dma("qpool", vp[:, :, :],
                           d_vall.ap()[pr8].rearrange("p (a c) -> p a c", a=NT),
                           reads=[r_vald], writes=[r_vp])

                def prefetch_wo(su, fh):
                    tk.dma("qpool", wo_stg[:, 0:2, :],
                           dw["b_wo"].ap().rearrange("(c p) f -> p c f", p=128)[
                               :, su * 2:(su + 1) * 2, fh * 512:(fh + 1) * 512],
                           writes=[r_wos])
                    if su == 0:
                        tk.dma("qpool", rows_b[0:1, 128:640],
                               dw["b_bo"].ap().rearrange("(o n) -> o n", o=1)[
                                   :, fh * 512:(fh + 1) * 512],
                               writes=[r_rows])

                def u_super(qr, su):
                    _u = nc.named_scope(f"bU{qr}_{su}", notify=True); _u.__enter__()
                    for fh in range(2):
                        if fh == 1:
                            prefetch_wo(su, 1)
                        for tci in range(NT4):
                            t = qr * NT4 + tci
                            def mmu(te, tci=tci, fh=fh, su=su):
                                last = None
                                for c in range(2):
                                    last = te.matmul(pb_sc[:, fh, 0:512],
                                                     o_q[:, c, tci * 128:(tci + 1) * 128],
                                                     wo_stg[:, c, :],
                                                     start=(c == 0),
                                                     stop=(c == 1 and su != 0))
                                if su == 0:
                                    last = te.matmul(pb_sc[:, fh, 0:512], ones_b[0:1, 0:128],
                                                     rows_b[0:1, 128:640],
                                                     start=False, stop=True)
                                return last
                            tk.emit("pe", mmu, reads=[r_oq, r_wos, r_rows, r_const],
                                    writes=[rb[fh][0]])
                            tk.emit("dve", lambda v, t=t, fh=fh: v.tensor_tensor(
                                        x_sb[:, t, fh * 512:(fh + 1) * 512],
                                        pb_sc[:, fh, 0:512],
                                        x_sb[:, t, fh * 512:(fh + 1) * 512], ALU.add),
                                    reads=[rb[fh][0], r_x[t]], writes=[r_x[t]])
                    _u.__exit__(None, None, None)

                for qr in range(4):
                    qproj(qr)
                    normbatch(qr)
                    mbc_combine_all()
                    for su in range(4):
                        for prl in range(2):
                            pr = su * 2 + prl
                            gidx = qr * 8 + pr
                            nxt_pr = (gidx + 1) % 8 if gidx + 1 < 32 else None
                            if prl == 1:
                                prefetch_wo(su, 0)
                            _at = nc.named_scope(f"bAT{qr}_{pr}", notify=True); _at.__enter__()
                            tail_fin = emit_attn_b_pair(
                                             qr, su, prl, pr, pb_sc, rb, pb_oA, pb_oB,
                                             r_poA, r_poB, kp, r_kp, qq, r_qq, vp, r_vp,
                                             ptstB, r_ptstB, mba, r_mba, o_q, r_oq,
                                             ostg, r_ostg, QTS, NQTQ, NKC,
                                             tail_st, stg_h0, r_tailr, r_tailb,
                                             r_tstg, r_rbcd, d_rbc, r_qsq2)
                            _at.__exit__(None, None, None)
                            if nxt_pr is not None:
                                load_kv(nxt_pr)
                            tail_fin()
                        u_super(qr, su)
            tk.phase_end()
        tk.phase_end()



    def ffn_prestage():
        for e in range(N_EXP):
            for typ, wsrc in enumerate((dw["f_q"], dw["f_k"], dw["f_v"])):
                tk.dma("qpool", d_wcat.ap()[:, :, e, typ * 64:(typ + 1) * 64],
                       wsrc.ap()[e].rearrange("(c p) d -> p c d", p=128),
                       writes=[r_wcd])
        tk.dma("qpool", d_wob.ap(),
               dw["f_o"].ap().rearrange("e h f -> (e h) f").rearrange(
                   "(c p) f -> p c f", p=128),
               writes=[r_wcd])

    if 0 in sublayers:
        with nc.named_scope("attn_a_prep", notify=True):
            attn_a_prep()
        with nc.named_scope("attn_a", notify=True):
            attn_a(after_bias=(ffn_prestage if (1 in sublayers or 3 in sublayers)
                               else None))
    elif 1 in sublayers or 3 in sublayers:
        ffn_prestage()
    if 1 in sublayers:
        with nc.named_scope("ffn1", notify=True):
            ffn(1)
    if 2 in sublayers:
        with nc.named_scope("attn_b", notify=True):
            attn_b()
    if 3 in sublayers:
        with nc.named_scope("ffn3", notify=True):
            ffn(3)
    if 0 in d_dbg:
        store_x(d_dbg[0], 0)
        tk.phase_end()

    store_x(d_out, 99)
    tk.phase_end()
    return nc


def np_ln(x):
    mu = x.mean(-1, keepdims=True)
    var = x.var(-1, keepdims=True)
    return (x - mu) / np.sqrt(var + LN_EPS)


def np_attn_a(x, w, b, ow, ob):
    T, Dm = x.shape
    y = np_ln(x)
    xs = np.roll(y, -SHIFT, axis=0)
    nw = T // 64
    xw = xs.reshape(nw, 64, Dm)
    qkv = xw @ w.T + b
    q, k, v = np.split(qkv, 3, axis=-1)
    H = 8
    hd = Dm // H
    def heads(t):
        return t.reshape(nw, 64, H, hd).transpose(0, 2, 1, 3)
    q, k, v = heads(q), heads(k), heads(v)
    s = np.einsum("nhqd,nhkd->nhqk", q, k) / np.sqrt(hd)
    s = s - s.max(-1, keepdims=True)
    p = np.exp(s)
    p /= p.sum(-1, keepdims=True)
    o = np.einsum("nhqk,nhkd->nhqd", p, v)
    o = o.transpose(0, 2, 1, 3).reshape(T, Dm)
    o = o @ ow.T + ob
    o = np.roll(o, SHIFT, axis=0)
    return x + 2 * o


def np_ffn(x, sim, gates, qp, kp, vp, op):
    T, Dm = x.shape
    h = np_ln(x)
    nf = h / np.maximum(np.linalg.norm(h, axis=-1, keepdims=True), 1e-12)
    ns = sim / np.maximum(np.linalg.norm(sim, axis=0, keepdims=True), 1e-12)
    logits = nf @ ns - 1.0 / (1.0 + np.exp(-gates))
    gated = np.maximum(logits, 0.0)
    mask = (gated > 0).astype(np.float32)
    inactive = mask.sum(1) == 0
    fb_idx = np.argsort(-logits, axis=1, kind="stable")[:, :2]
    fb = np.zeros_like(mask)
    np.put_along_axis(fb, fb_idx, 1.0, axis=1)
    mask = np.where(inactive[:, None] & (fb > 0), 1.0, mask)
    ml = np.where(mask > 0, gated, -np.inf)
    ml = ml - ml.max(1, keepdims=True)
    rw = np.exp(ml)
    rw /= rw.sum(1, keepdims=True)
    q = np.einsum("te,ted->td", rw, np.einsum("tc,ecd->ted", h, qp))
    k = np.einsum("te,ted->td", rw, np.einsum("tc,ecd->ted", h, kp))
    v = np.einsum("te,ted->td", rw, np.einsum("tc,ecd->ted", h, vp))
    s = (q @ k.T) / np.sqrt(HDF)
    causal = np.tril(np.ones((T, T), dtype=bool))
    s = np.where(causal, s, -np.inf)
    s = s - s.max(1, keepdims=True)
    p = np.exp(s)
    p /= p.sum(1, keepdims=True)
    a = p @ v
    mixed = a[:, None, :] * rw[:, :, None]
    return x + np.einsum("ted,edc->tc", mixed, op)


def np_attn_b(x, wq, bq, wl, bl, wk, bk, wv, bv, wo, bo):
    T, Dm = x.shape
    y = np_ln(x)
    H, Dh = 16, 64
    q = (y @ wq + bq).reshape(T, H, Dh)
    lat = y @ wl + bl
    k = (lat @ wk + bk).reshape(T, H, Dh)
    v = (lat @ wv + bv).reshape(T, H, Dh)
    s = np.einsum("ihd,jhd->hij", q, k) / np.sqrt(Dh)
    s = s - s.max(-1, keepdims=True)
    p = np.exp(s)
    p /= p.sum(-1, keepdims=True)
    o = np.einsum("hij,jhd->ihd", p, v).reshape(T, Dm)
    return x + o @ wo + bo


def np_reference(x, W, sublayers=(0, 1, 2, 3)):
    if 0 in sublayers:
        x = np_attn_a(x, W["a_w"], W["a_b"], W["a_ow"], W["a_ob"])
    if 1 in sublayers:
        x = np_ffn(x, W["f_sim"], W["f_gates"], W["f_q"], W["f_k"], W["f_v"], W["f_o"])
    if 2 in sublayers:
        x = np_attn_b(x, W["b_wq"], W["b_bq"], W["b_wl"], W["b_bl"], W["b_wk"],
                      W["b_bk"], W["b_wv"], W["b_bv"], W["b_wo"], W["b_bo"])
    if 3 in sublayers:
        x = np_ffn(x, W["f_sim"], W["f_gates"], W["f_q"], W["f_k"], W["f_v"], W["f_o"])
    return x


def make_weights(rng, scale=0.02):
    W = {
        "a_w": rng.standard_normal((3 * D, D), dtype=np.float32) * scale,
        "a_b": rng.standard_normal((3 * D,), dtype=np.float32) * 0.5,
        "a_ow": rng.standard_normal((D, D), dtype=np.float32) * scale,
        "a_ob": rng.standard_normal((D,), dtype=np.float32) * 0.5,
        "b_wq": rng.standard_normal((D, D), dtype=np.float32) * D ** -0.5,
        "b_bq": rng.standard_normal((D,), dtype=np.float32) * 0.5,
        "b_wl": rng.standard_normal((D, LATENT), dtype=np.float32) * D ** -0.5,
        "b_bl": rng.standard_normal((LATENT,), dtype=np.float32) * 0.5,
        "b_wk": rng.standard_normal((LATENT, D), dtype=np.float32) * LATENT ** -0.5,
        "b_bk": rng.standard_normal((D,), dtype=np.float32) * 0.5,
        "b_wv": rng.standard_normal((LATENT, D), dtype=np.float32) * LATENT ** -0.5,
        "b_bv": rng.standard_normal((D,), dtype=np.float32) * 0.5,
        "b_wo": rng.standard_normal((D, D), dtype=np.float32) * D ** -0.5,
        "b_bo": rng.standard_normal((D,), dtype=np.float32) * 0.5,
        "f_sim": rng.standard_normal((D, N_EXP), dtype=np.float32),
        "f_gates": rng.standard_normal((N_EXP,), dtype=np.float32),
        "f_q": (rng.random((N_EXP, D, HDF), dtype=np.float32) - 0.5) * 0.148,
        "f_k": (rng.random((N_EXP, D, HDF), dtype=np.float32) - 0.5) * 0.148,
        "f_v": (rng.random((N_EXP, D, HDF), dtype=np.float32) - 0.5) * 0.148,
        "f_o": (rng.random((N_EXP, HDF, D), dtype=np.float32) - 0.5) * 0.148,
    }
    return W


def kernel(**inputs):
    from concourse.bass_utils import run_bass_kernel_spmd
    x = np.ascontiguousarray(inputs["L0_attn_a_x"])
    B = x.shape[0]
    wmap = {
        "a_w": "L0_attn_a_in_proj_w", "a_b": "L0_attn_a_in_proj_b",
        "a_ow": "L0_attn_a_out_w", "a_ob": "L0_attn_a_out_b",
        "b_wq": "L0_attn_b_Wq", "b_bq": "L0_attn_b_bq",
        "b_wl": "L0_attn_b_Wl", "b_bl": "L0_attn_b_bl",
        "b_wk": "L0_attn_b_Wk", "b_bk": "L0_attn_b_bk",
        "b_wv": "L0_attn_b_Wv", "b_bv": "L0_attn_b_bv",
        "b_wo": "L0_attn_b_Wo", "b_bo": "L0_attn_b_bo",
        "f_sim": "L0_ffn_sim_matrix", "f_gates": "L0_ffn_gates",
        "f_q": "L0_ffn_q_proj", "f_k": "L0_ffn_k_proj",
        "f_v": "L0_ffn_v_proj", "f_o": "L0_ffn_o_proj",
    }
    weights = {k: np.ascontiguousarray(np.asarray(inputs[v], dtype=np.float32))
               for k, v in wmap.items()}
    nc = build_program(x.shape[1])
    in_maps = []
    for b in range(B):
        m = {"x": np.ascontiguousarray(x[b])}
        m.update(weights)
        in_maps.append(m)
    res = run_bass_kernel_spmd(nc, in_maps, core_ids=list(range(B)))
    return np.stack([np.asarray(res.results[b]["out"]) for b in range(B)], axis=0)


# version 29 (final)
# speedup vs baseline: 1.0560x; 1.0560x over previous
import numpy as np
import ml_dtypes
from contextlib import ExitStack
import concourse.bass as bass
import concourse.mybir as mybir

F32 = mybir.dt.float32
BF16 = mybir.dt.bfloat16
AF = mybir.ActivationFunctionType
ALU = mybir.AluOpType
AX = mybir.AxisListType

D = 1024
HDA = 128
N_EXP = 16
HDF = 64
DHB = 64
LATENT = 256
LN_EPS = 1e-5
SCALE_A = HDA ** -0.5
SCALE_F = HDF ** -0.5
SCALE_B = DHB ** -0.5
SHIFT = 32


class Tracker:
    ENGINES = ("pe", "act", "dve", "pool", "sp")

    def __init__(self, nc):
        self.nc = nc
        self.eng = {"pe": nc.tensor, "act": nc.scalar, "dve": nc.vector,
                    "pool": nc.gpsimd, "sp": nc.sync}
        self.sem = {e: nc.alloc_semaphore(f"prog_{e}") for e in self.ENGINES}
        self.cnt = {e: 0 for e in self.ENGINES}
        self.qeng = {"qsp": "sp", "qpool": "pool"}
        self.qpool_sems = {q: [nc.alloc_semaphore(f"{q}_{i}")
                               for i in range(24 if q == "qpool" else 12)]
                           for q in self.qeng}
        self.qsem_obj = {}
        self.quse = {}
        for q, sems in self.qpool_sems.items():
            for i, sm in enumerate(sems):
                self.qsem_obj[(q, i)] = sm
                self.quse[(q, i)] = 0
        self.qfree = {q: list(range(len(self.qpool_sems[q]))) for q in self.qeng}
        self.buf_sem = {}
        self.waited = {e: {} for e in self.ENGINES}
        self.res = {}

    def R(self, *key):
        r = self.res.get(key)
        if r is None:
            r = {"w": None, "readers": []}
            self.res[key] = r
        return r

    def _sem_obj(self, key):
        return self.sem[key] if key in self.sem else self.qsem_obj[key]

    def _collect(self, e, reads, writes):
        need = {}
        def add(dep):
            skey, val = dep
            if e == "pe" and skey == "pe":
                return
            if val > need.get(skey, 0):
                need[skey] = val
        for r in reads:
            if r["w"] is not None:
                add(r["w"])
        for w in writes:
            if w["w"] is not None:
                add(w["w"])
            for d in w["readers"]:
                add(d)
        return need

    def _emit_waits(self, e, need):
        for skey, val in need.items():
            if self.waited[e].get(skey, 0) >= val:
                continue
            self.eng[e].wait_ge(self._sem_obj(skey), val)
            self.waited[e][skey] = val

    def emit(self, e, fn, reads=(), writes=()):
        self._emit_waits(e, self._collect(e, reads, writes))
        last = fn(self.eng[e])
        assert last is not None
        last.then_inc(self.sem[e], 1)
        self.cnt[e] += 1
        tok = (e, self.cnt[e])
        for w in writes:
            w["w"] = tok
            w["readers"] = []
        for r in reads:
            r["readers"].append(tok)

    def dma(self, q, dst, src, reads=(), writes=(), **kw):
        e = self.qeng[q]
        assert writes, "dma needs a written resource for its sem key"
        bk = id(writes[0])
        skey = self.buf_sem.get(bk)
        if skey is None:
            assert self.qfree[q], f"dma sem pool exhausted on {q}"
            skey = (q, self.qfree[q].pop())
            self.buf_sem[bk] = skey
        self._emit_waits(e, self._collect(e, reads, writes))
        self.eng[e].dma_start(out=dst, in_=src, **kw).then_inc(self.qsem_obj[skey], 16)
        self.quse[skey] += 16
        tok = (skey, self.quse[skey])
        for w in writes:
            w["w"] = tok
            w["readers"] = []
        for r in reads:
            r["readers"].append(tok)

    def phase_end(self):
        for e in self.ENGINES:
            for skey, used in self.quse.items():
                if used > self.waited[e].get(skey, 0):
                    self.eng[e].wait_ge(self.qsem_obj[skey], used)
                    self.waited[e][skey] = used
        self.nc.all_engine_barrier()
        for e in self.ENGINES:
            for f in self.ENGINES:
                self.waited[e][f] = self.cnt[f]
        self.buf_sem = {}
        self.qfree = {q: list(range(len(self.qpool_sems[q]))) for q in self.qeng}


def build_program(T, sublayers=(0, 1, 2, 3), dbg=()):
    NT = T // 128
    NS = NT // 2
    nc = bass.Bass("TRN2", target_bir_lowering=False)
    tk = Tracker(nc)

    d_x = nc.dram_tensor("x", [T, D], F32, kind="ExternalInput")
    d_out = nc.dram_tensor("out", [T, D], F32, kind="ExternalOutput")
    d_dbg = {i: nc.dram_tensor(f"dbg{i}", [T, D], F32, kind="ExternalOutput")
             for i in dbg}
    dw = {}
    for name, shape in [
        ("a_w", [3 * D, D]), ("a_b", [3 * D]), ("a_ow", [D, D]), ("a_ob", [D]),
        ("b_wq", [D, D]), ("b_bq", [D]), ("b_wl", [D, LATENT]), ("b_bl", [LATENT]),
        ("b_wk", [LATENT, D]), ("b_bk", [D]), ("b_wv", [LATENT, D]), ("b_bv", [D]),
        ("b_wo", [D, D]), ("b_bo", [D]),
        ("f_sim", [D, N_EXP]), ("f_gates", [N_EXP]),
        ("f_q", [N_EXP, D, HDF]), ("f_k", [N_EXP, D, HDF]),
        ("f_v", [N_EXP, D, HDF]), ("f_o", [N_EXP, HDF, D]),
    ]:
        dw[name] = nc.dram_tensor(name, shape, F32, kind="ExternalInput")
    d_wta = nc.dram_tensor("wta_scr", [D, 3 * D], BF16, kind="Internal")
    d_rbc = nc.dram_tensor("rbc_scr", [4, 512], F32, kind="Internal")
    d_wota = nc.dram_tensor("wota_scr", [D, D], BF16, kind="Internal")
    d_wqb = nc.dram_tensor("wqb_scr", [128, 8, 8, 128], BF16, kind="Internal")
    d_kall = nc.dram_tensor("kall_scr", [8, 128, T], BF16, kind="Internal")
    d_vall = nc.dram_tensor("vall_scr", [8, 128, (T // 128) * 130], BF16, kind="Internal")
    d_wcat = nc.dram_tensor("wcat_scr", [128, 8, 16, 192], BF16, kind="Internal")
    d_wob = nc.dram_tensor("wob_scr", [128, 8, 1024], BF16, kind="Internal")

    x_sb = nc.alloc_sbuf_tensor("x_sb", [128, NT, D], F32)
    st_mu = nc.alloc_sbuf_tensor("st_mu", [128, NT], F32)
    st_rstd = nc.alloc_sbuf_tensor("st_rstd", [128, NT], F32)
    st_invy = nc.alloc_sbuf_tensor("st_invy", [128, NT], F32)
    st_t1 = nc.alloc_sbuf_tensor("st_t1", [128, NT], F32)
    st_t2 = nc.alloc_sbuf_tensor("st_t2", [128, NT], F32)
    y_stg = nc.alloc_sbuf_tensor("y_stg", [128, D], F32)
    y_stgb = nc.alloc_sbuf_tensor("y_stgb", [128, D], BF16)
    ones_b = nc.alloc_sbuf_tensor("ones_b", [33, 128], BF16)
    ones_f = nc.alloc_sbuf_tensor("ones_f", [1, 128], F32)
    ident_b = nc.alloc_sbuf_tensor("ident_b", [128, 128], BF16)
    ident_f = nc.alloc_sbuf_tensor("ident_f", [128, 128], F32)
    small = nc.alloc_sbuf_tensor("small", [128, 16], F32)

    r_x = [tk.R("x", t) for t in range(NT)]
    r_mu, r_rstd, r_invy = tk.R("mu"), tk.R("rstd"), tk.R("invy")
    r_t1, r_t2 = tk.R("t1"), tk.R("t2")
    r_ystg, r_ystgb = tk.R("ystg"), tk.R("ystgb")
    r_wstg, r_wstg2 = tk.R("wstg"), tk.R("wstg2")
    r_const, r_small = tk.R("const"), tk.R("small")
    r_wcd = tk.R("f_wcd")

    tk.dma("qsp", x_sb[:, :, :], d_x.ap().rearrange("(a p) d -> p a d", p=128),
           writes=r_x)

    tk.emit("dve", lambda v: v.memset(ones_b[:, :], 1.0), writes=[r_const])
    tk.emit("dve", lambda v: v.memset(ones_f[:, :], 1.0), writes=[r_const])
    prow = nc.alloc_sbuf_tensor("prow", [128, 1], F32)
    tk.emit("pool", lambda g: g.iota(prow[:, :], [[0, 1]], channel_multiplier=1,
                                     allow_small_or_imprecise_dtypes=True), writes=[r_const])
    tk.emit("pool", lambda g: g.iota(y_stg[:, 0:128], [[1, 128]], channel_multiplier=0,
                                     allow_small_or_imprecise_dtypes=True), writes=[r_ystg])
    tk.emit("dve", lambda v: v.tensor_scalar(ident_b[:, :], y_stg[:, 0:128], prow[:, :],
                                             None, ALU.is_equal),
            reads=[r_const, r_ystg], writes=[r_const])
    tk.emit("dve", lambda v: v.tensor_scalar(ident_f[:, :], y_stg[:, 0:128], prow[:, :],
                                             None, ALU.is_equal),
            reads=[r_const, r_ystg], writes=[r_const])

    with (nc.psum_tensor("warm_ps", [128, 512], F32) as warm_ps,
          nc.sbuf_tensor("warm_sb", [128, 512], BF16) as warm_sb):
        r_warm = tk.R("warmup")
        tk.emit("dve", lambda v: v.memset(warm_sb[:, :], 0.25), writes=[r_warm])
        def wburst(te):
            last = None
            for i in range(64):
                last = te.matmul(warm_ps[:, :], warm_sb[:, 0:128], warm_sb[:, :],
                                 start=True, stop=True)
            return last
        tk.emit("pe", wburst, reads=[r_warm], writes=[r_warm])
    tk.phase_end()

    def ln_stats(for_ffn):
        for t in range(NT):
            tk.emit("dve", lambda v, t=t: v.reduce_sum(st_mu[:, t:t + 1], x_sb[:, t, :], axis=AX.X),
                    reads=[r_x[t]], writes=[r_mu])
            tk.emit("act", lambda a, t=t: a.activation(y_stgb[:, :], x_sb[:, t, :], AF.Square,
                                                       accum_out=st_t1[:, t:t + 1]),
                    reads=[r_x[t]], writes=[r_ystgb, r_t1])
        tk.emit("dve", lambda v: v.tensor_scalar(st_mu[:, :], st_mu[:, :], 1.0 / D, None, ALU.mult),
                reads=[r_mu], writes=[r_mu])
        tk.emit("dve", lambda v: v.tensor_scalar(st_t1[:, :], st_t1[:, :], 1.0 / D, None, ALU.mult),
                reads=[r_t1], writes=[r_t1])
        tk.emit("dve", lambda v: v.tensor_mul(st_t2[:, :], st_mu[:, :], st_mu[:, :]),
                reads=[r_mu], writes=[r_t2])
        tk.emit("dve", lambda v: v.tensor_sub(st_t1[:, :], st_t1[:, :], st_t2[:, :]),
                reads=[r_t1, r_t2], writes=[r_t1])
        tk.emit("dve", lambda v: v.tensor_scalar(st_t2[:, :], st_t1[:, :], LN_EPS, None, ALU.add),
                reads=[r_t1], writes=[r_t2])
        tk.emit("act", lambda a: a.activation(st_rstd[:, :], st_t2[:, :], AF.Ln),
                reads=[r_t2], writes=[r_rstd])
        tk.emit("dve", lambda v: v.tensor_scalar(st_rstd[:, :], st_rstd[:, :], -0.5, None, ALU.mult),
                reads=[r_rstd], writes=[r_rstd])
        tk.emit("act", lambda a: a.activation(st_rstd[:, :], st_rstd[:, :], AF.Exp),
                reads=[r_rstd], writes=[r_rstd])
        if for_ffn:
            tk.emit("dve", lambda v: v.reciprocal(st_invy[:, :], st_t2[:, :]),
                    reads=[r_t2], writes=[r_invy])
            tk.emit("dve", lambda v: v.tensor_mul(st_invy[:, :], st_invy[:, :], st_t1[:, :]),
                    reads=[r_invy, r_t1], writes=[r_invy])
            tk.emit("dve", lambda v: v.tensor_scalar(st_invy[:, :], st_invy[:, :], float(D),
                                                     None, ALU.mult),
                    reads=[r_invy], writes=[r_invy])
            tk.emit("act", lambda a: a.activation(st_invy[:, :], st_invy[:, :], AF.Ln),
                    reads=[r_invy], writes=[r_invy])
            tk.emit("dve", lambda v: v.tensor_scalar(st_invy[:, :], st_invy[:, :], -0.5, None, ALU.mult),
                    reads=[r_invy], writes=[r_invy])
            tk.emit("act", lambda a: a.activation(st_invy[:, :], st_invy[:, :], AF.Exp),
                    reads=[r_invy], writes=[r_invy])

    def y_tile_bf(t, p_trb, r_ptr, dst, r_dst, col):
        tk.emit("dve", lambda v: v.tensor_scalar(y_stgb[:, :], x_sb[:, t, :],
                                                 st_mu[:, t:t + 1], st_rstd[:, t:t + 1],
                                                 ALU.subtract, ALU.mult),
                reads=[r_x[t], r_mu, r_rstd], writes=[r_ystgb])
        for rnd in range(2):
            def tr(te, rnd=rnd):
                last = None
                for c in range(4):
                    cc = rnd * 4 + c
                    last = te.transpose(p_trb[:, c * 128:(c + 1) * 128],
                                        y_stgb[:, cc * 128:(cc + 1) * 128], ident_b[:, :])
                return last
            tk.emit("pe", tr, reads=[r_ystgb, r_const], writes=[r_ptr])
            tk.emit("dve", lambda v, rnd=rnd: v.tensor_copy(
                        dst[:, rnd * 4:(rnd + 1) * 4, col:col + 128],
                        p_trb[:, :].rearrange("p (c n) -> p c n", c=4)),
                    reads=[r_ptr], writes=[r_dst])

    def y_tile_f32(t, p_tr, r_ptr, dst_bf, r_dstb, dst_f, r_dstf, col):
        tk.emit("dve", lambda v: v.tensor_scalar(y_stg[:, :], x_sb[:, t, :],
                                                 st_mu[:, t:t + 1], st_rstd[:, t:t + 1],
                                                 ALU.subtract, ALU.mult),
                reads=[r_x[t], r_mu, r_rstd], writes=[r_ystg])
        for rnd in range(2):
            def tr(te, rnd=rnd):
                last = None
                for c in range(4):
                    cc = rnd * 4 + c
                    last = te.transpose(p_tr[:, c * 128:(c + 1) * 128],
                                        y_stg[:, cc * 128:(cc + 1) * 128], ident_f[:, :])
                return last
            tk.emit("pe", tr, reads=[r_ystg, r_const], writes=[r_ptr])
            tk.emit("dve", lambda v, rnd=rnd: v.tensor_copy(
                        dst_bf[:, rnd * 4:(rnd + 1) * 4, col:col + 128],
                        p_tr[:, :].rearrange("p (c n) -> p c n", c=4)),
                    reads=[r_ptr], writes=[r_dstb])
            tk.emit("dve", lambda v, rnd=rnd: v.tensor_copy(
                        dst_f[:, rnd * 4:(rnd + 1) * 4, col:col + 128],
                        p_tr[:, :].rearrange("p (c n) -> p c n", c=4)),
                    reads=[r_ptr], writes=[r_dstf])

    def y_tile_fb(t, p_tr, r_ptr, dst_bf, r_dstb, col, extra_w=()):
        tk.emit("dve", lambda v: v.tensor_scalar(y_stg[:, :], x_sb[:, t, :],
                                                 st_mu[:, t:t + 1], st_rstd[:, t:t + 1],
                                                 ALU.subtract, ALU.mult),
                reads=[r_x[t], r_mu, r_rstd], writes=[r_ystg] + list(extra_w))
        for rnd in range(2):
            def tr(te, rnd=rnd):
                last = None
                for c in range(4):
                    cc = rnd * 4 + c
                    last = te.transpose(p_tr[:, c * 128:(c + 1) * 128],
                                        y_stg[:, cc * 128:(cc + 1) * 128], ident_f[:, :])
                return last
            tk.emit("pe", tr, reads=[r_ystg, r_const], writes=[r_ptr])
            tk.emit("dve", lambda v, rnd=rnd: v.tensor_copy(
                        dst_bf[:, rnd * 4:(rnd + 1) * 4, col:col + 128],
                        p_tr[:, :].rearrange("p (c n) -> p c n", c=4)),
                    reads=[r_ptr], writes=[r_dstb])

    def store_x(dst, rkey):
        r_st = tk.R("store", rkey)
        tk.dma("qsp", dst.ap().rearrange("(a p) d -> p a d", p=128), x_sb[:, :, :],
               reads=r_x, writes=[r_st])

    def attn_a_prep():
        with ExitStack() as _st:
            pp_w = _st.enter_context(nc.psum_tensor("pp_w", [128, 1024], BF16))
            w_stg = _st.enter_context(nc.sbuf_tensor("w_stg", [128, 1024], BF16))
            w_stg2 = _st.enter_context(nc.sbuf_tensor("w_stg2", [128, 8, 128], BF16))
            r_ppw = tk.R("pp_w")
            for src, dstT, nf in ((dw["a_w"], d_wta, 24), (dw["a_ow"], d_wota, 8)):
                for fr in range(nf):
                    tk.dma("qpool", w_stg[:, :], src.ap()[fr * 128:(fr + 1) * 128, :],
                           writes=[r_wstg])
                    def tr(te):
                        last = None
                        for dc in range(8):
                            last = te.transpose(pp_w[:, dc * 128:(dc + 1) * 128],
                                                w_stg[:, dc * 128:(dc + 1) * 128], ident_b[:, :])
                        return last
                    tk.emit("pe", tr, reads=[r_wstg, r_const], writes=[r_ppw])
                    tk.emit("dve", lambda v: v.tensor_copy(
                                w_stg2[:, :, :], pp_w[:, :].rearrange("p (c n) -> p c n", c=8)),
                            reads=[r_ppw], writes=[r_wstg2])
                    tk.dma("qsp", dstT.ap().rearrange("(c p) f -> p c f", p=128)[:, :, fr * 128:(fr + 1) * 128],
                           w_stg2[:, :, :], reads=[r_wstg2], writes=[tk.R("wta_dram")])
        tk.phase_end()

    def attn_a(after_bias=None):
        with ExitStack() as _st:
            ys_a = _st.enter_context(nc.sbuf_tensor("ys_a", [128, 8, 640], BF16))
            wqk_stg = _st.enter_context(nc.sbuf_tensor("wqk_stg", [128, 2, 8, 128], BF16))
            wv_stg = _st.enter_context(nc.sbuf_tensor("wv_stg", [128, 8, 1024], BF16))
            wot_stg = _st.enter_context(nc.sbuf_tensor("wot_stg", [128, 8, 1024], BF16))
            bfm_a = _st.enter_context(nc.sbuf_tensor("bfm_a", [128, 16], F32))
            rows_a = _st.enter_context(nc.sbuf_tensor("rows_a", [33, 1024], BF16))
            q_e = _st.enter_context(nc.sbuf_tensor("q_e", [128, 8, 512], BF16))
            k_e = _st.enter_context(nc.sbuf_tensor("k_e", [128, 8, 256], BF16))
            v_e = _st.enter_context(nc.sbuf_tensor("v_e", [128, 2, 1024], BF16))
            o_ring = _st.enter_context(nc.sbuf_tensor("o_ring", [128, 8, 3, 128], BF16))
            ptf_sb = _st.enter_context(nc.sbuf_tensor("ptf_sb", [128, 4, 128], BF16))
            sm_a = _st.enter_context(nc.sbuf_tensor("sm_a", [128, 4, 3], F32))
            pa_trb = _st.enter_context(nc.psum_tensor("pa_trb", [128, 512], BF16))
            pa_sx = [_st.enter_context(nc.psum_tensor(f"pa_sx{i}", [128, 192], F32))
                     for i in range(4)]
            pa_o0 = _st.enter_context(nc.psum_tensor("pa_o0", [128, 512], F32))
            pa_o1 = _st.enter_context(nc.psum_tensor("pa_o1", [128, 512], F32))
            pa_b = _st.enter_context(nc.psum_tensor("pa_b", [128, 16], F32))
            pa_oP = (pa_o0, pa_o1)
            pa_u, pa_u2 = pa_o0, pa_o1
            r_ys = tk.R("a_ys")
            r_wqk = [tk.R("a_wqk", i) for i in range(2)]
            r_wv, r_wot = tk.R("a_wv"), tk.R("a_wot")
            r_bfm, r_rows = tk.R("a_bfm"), tk.R("a_rows")
            r_qe, r_ke = tk.R("a_qe"), tk.R("a_ke")
            r_veP = [tk.R("a_ve", i) for i in range(2)]
            r_or = [tk.R("a_or", i) for i in range(3)]
            r_ot0 = tk.R("a_ot0")
            r_smP = [tk.R("a_sm", i) for i in range(4)]
            r_ptfP = [tk.R("a_ptf", i) for i in range(4)]
            r_ptrb = tk.R("pa_trb")
            r_sxP = [tk.R("pa_sx", i) for i in range(4)]
            r_paoP = [tk.R("pa_o", i) for i in range(2)]
            r_pau, r_pau2 = r_paoP[0], r_paoP[1]
            r_pab = tk.R("pa_b")

            tk.dma("qpool", rows_a[0:1, 0:1024],
                   dw["a_b"].ap().rearrange("(o n) -> o n", o=1)[:, 2 * D:3 * D], writes=[r_rows])
            tk.dma("qpool", rows_a[32:33, 0:1024],
                   dw["a_ob"].ap().rearrange("(o n) -> o n", o=1), writes=[r_rows])
            for typ in range(2):
                for fc in range(8):
                    tk.dma("qsp", y_stg[0:1, 0:128],
                           dw["a_b"].ap().rearrange("(o n) -> o n", o=1)[:, typ * D + fc * 128: typ * D + (fc + 1) * 128],
                           writes=[r_ystg])
                    tk.emit("pe", lambda te: te.transpose(pa_b[:, 0:1],
                                                          y_stg[0:1, 0:128], ident_f[0:1, 0:1]),
                            reads=[r_ystg, r_const], writes=[r_pab])
                    tk.emit("dve", lambda v, typ=typ, fc=fc: v.tensor_copy(
                                bfm_a[:, typ * 8 + fc: typ * 8 + fc + 1], pa_b[:, 0:1]),
                            reads=[r_pab], writes=[r_bfm])

            tk.emit("dve", lambda v: v.memset(ptf_sb[:, :, :], 0.0),
                    writes=r_ptfP)
            if after_bias is not None:
                after_bias()
            ln_stats(False)

            tk.dma("qsp", wv_stg[:, :, :],
                   d_wta.ap().rearrange("(c p) f -> p c f", p=128)[:, :, 2 * D:3 * D],
                   writes=[r_wv])
            tk.dma("qsp", wot_stg[:, :, :],
                   d_wota.ap().rearrange("(c p) f -> p c f", p=128), writes=[r_wot])
            pend_u = []
            for s in range(NT // 4):
                for j in range(5):
                    y_tile_bf((4 * s + j) % NT, pa_trb, r_ptrb, ys_a, r_ys, j * 128)
                for fc in range(8):
                    slot = fc % 2
                    tk.dma("qsp", wqk_stg[:, slot, :, :],
                           d_wta.ap().rearrange("(c p) f -> p c f", p=128)[
                               :, :, fc * 128:(fc + 1) * 128],
                           writes=[r_wqk[slot]])
                    pp = fc % 2
                    def mmq(te, slot=slot, pp=pp):
                        last = None
                        for dc in range(8):
                            last = te.matmul(pa_oP[pp][:, 0:512], wqk_stg[:, slot, dc, :],
                                             ys_a[:, dc, 32:544],
                                             start=(dc == 0), stop=(dc == 7))
                        return last
                    tk.emit("pe", mmq, reads=[r_wqk[slot], r_ys], writes=[r_paoP[pp]])
                    tk.emit("dve", lambda v, fc=fc, pp=pp: v.tensor_scalar(
                                q_e[:, fc, :], pa_oP[pp][:, 0:512],
                                bfm_a[:, fc: fc + 1], None, ALU.add),
                            reads=[r_paoP[pp], r_bfm], writes=[r_qe])

                def emit_k(h):
                    for fc in range(8):
                        slot = fc % 2
                        tk.dma("qsp", wqk_stg[:, slot, :, :],
                               d_wta.ap().rearrange("(c p) f -> p c f", p=128)[
                                   :, :, D + fc * 128: D + (fc + 1) * 128],
                               writes=[r_wqk[slot]])
                        pp = fc % 2
                        def mmk2(te, slot=slot, pp=pp, h=h):
                            last = None
                            for dc in range(8):
                                last = te.matmul(pa_oP[pp][:, 0:256], wqk_stg[:, slot, dc, :],
                                                 ys_a[:, dc, 32 + h * 256: 288 + h * 256],
                                                 start=(dc == 0), stop=(dc == 7))
                            return last
                        tk.emit("pe", mmk2, reads=[r_wqk[slot], r_ys], writes=[r_paoP[pp]])
                        tk.emit("dve", lambda v, fc=fc, pp=pp: v.tensor_scalar(
                                    k_e[:, fc, :], pa_oP[pp][:, 0:256],
                                    bfm_a[:, 8 + fc: 9 + fc], None, ALU.add),
                                reads=[r_paoP[pp], r_bfm], writes=[r_ke])
                emit_k(0)
                def emit_v(tt):
                    vs = tt % 2
                    for fh in range(2):
                        def mmv(te, tt=tt, fh=fh):
                            for dc in range(8):
                                te.matmul(pa_oP[fh][:, 0:512],
                                          ys_a[:, dc, 32 + tt * 128: 160 + tt * 128],
                                          wv_stg[:, dc, fh * 512:(fh + 1) * 512],
                                          start=(dc == 0), stop=False)
                            return te.matmul(pa_oP[fh][:, 0:512], ones_b[0:1, 0:128],
                                             rows_a[0:1, fh * 512:(fh + 1) * 512],
                                             start=False, stop=True)
                        tk.emit("pe", mmv, reads=[r_ys, r_wv, r_rows, r_const], writes=[r_paoP[fh]])
                        tk.emit("dve", lambda v, vs=vs, fh=fh: v.tensor_copy(
                                    v_e[:, vs, fh * 512:(fh + 1) * 512], pa_oP[fh][:, 0:512]),
                                reads=[r_paoP[fh]], writes=[r_veP[vs]])
                emit_v(0)
                emit_v(1)

                for gj in range(4):
                    g = 4 * s + gj
                    def mms_emit(hh, gj=gj):
                        sl = hh % 4
                        sx = pa_sx[sl]
                        def mms(te, hh=hh, gj=gj, sx=sx):
                            kc0 = (gj % 2) * 128
                            te.matmul(sx[0:64, 0:64],
                                      q_e[:, hh, gj * 128: gj * 128 + 64],
                                      k_e[:, hh, kc0: kc0 + 64],
                                      start=True, stop=True)
                            return te.matmul(sx[64:128, 0:64],
                                             q_e[:, hh, gj * 128 + 64: gj * 128 + 128],
                                             k_e[:, hh, kc0 + 64: kc0 + 128],
                                             start=True, stop=True)
                        tk.emit("pe", mms, reads=[r_qe, r_ke], writes=[r_sxP[sl]])
                    for hh in range(3):
                        mms_emit(hh)
                    if pend_u:
                        pend_u.pop(0)()

                    def stageAB(hh):
                        sl = hh % 4
                        sx, r_sx = pa_sx[sl], r_sxP[sl]
                        r_ptf, r_sm = r_ptfP[sl], r_smP[sl]
                        tk.emit("dve", lambda v, sl=sl, sx=sx: v.reduce_max(
                                    sm_a[:, sl, 0:1], sx[:, 0:64], axis=AX.X),
                                reads=[r_sx], writes=[r_sm])
                        tk.emit("dve", lambda v, sl=sl: v.tensor_scalar(
                                    sm_a[:, sl, 0:1], sm_a[:, sl, 0:1], -SCALE_A, None, ALU.mult),
                                reads=[r_sm], writes=[r_sm])
                        tk.emit("act", lambda a, sl=sl, sx=sx: a.activation(
                                    ptf_sb[0:64, sl, 0:64], sx[0:64, 0:64], AF.Exp,
                                    bias=sm_a[0:64, sl, 0:1], scale=SCALE_A,
                                    accum_out=sm_a[0:64, sl, 1:2]),
                                reads=[r_sx, r_sm], writes=[r_ptf, r_sm])
                        tk.emit("act", lambda a, sl=sl, sx=sx: a.activation(
                                    ptf_sb[64:128, sl, 64:128], sx[64:128, 0:64], AF.Exp,
                                    bias=sm_a[64:128, sl, 0:1], scale=SCALE_A,
                                    accum_out=sm_a[64:128, sl, 1:2]),
                                reads=[r_sx, r_sm], writes=[r_ptf, r_sm])

                    def stageC(hh):
                        sl = hh % 4
                        r_ptf, r_sm = r_ptfP[sl], r_smP[sl]
                        tk.emit("dve", lambda v, sl=sl: v.reciprocal(sm_a[:, sl, 2:3],
                                                                     sm_a[:, sl, 1:2]),
                                reads=[r_sm], writes=[r_sm])
                        tk.emit("dve", lambda v, sl=sl: v.tensor_scalar(
                                    ptf_sb[:, sl, :], ptf_sb[:, sl, :], sm_a[:, sl, 2:3],
                                    None, ALU.mult),
                                reads=[r_ptf, r_sm], writes=[r_ptf])

                    def stageD(hh, gj=gj, g=g):
                        sl = hh % 4
                        sx, r_sx = pa_sx[sl], r_sxP[sl]
                        P2 = hh % 2
                        pa_o, r_pao = pa_oP[P2], r_paoP[P2]
                        r_ptf = r_ptfP[sl]
                        tk.emit("pe", lambda te, sl=sl: te.transpose(
                                    pa_trb[:, sl * 128:(sl + 1) * 128],
                                    ptf_sb[:, sl, :], ident_b[:, :]),
                                reads=[r_ptf, r_const], writes=[r_ptrb])
                        tk.emit("dve", lambda v, sl=sl: v.tensor_copy(
                                    ptf_sb[:, sl, :], pa_trb[:, sl * 128:(sl + 1) * 128]),
                                reads=[r_ptrb], writes=[r_ptf])
                        tk.emit("pe", lambda te, gj=gj, hh=hh, sl=sl, pa_o=pa_o: te.matmul(
                                    pa_o[:, 0:128], v_e[:, gj % 2, hh * 128:(hh + 1) * 128],
                                    ptf_sb[:, sl, :], start=True, stop=True),
                                reads=[r_veP[gj % 2], r_ptf], writes=[r_pao])
                        if hh + 3 < 8:
                            mms_emit(hh + 3)
                        for w in range(2):
                            cstart = (32 + 64 * (2 * g + w)) % T
                            pieces = []
                            tau0 = cstart // 128
                            coff = cstart % 128
                            lenA = min(64, 128 - coff)
                            pieces.append((tau0, coff, 0, lenA))
                            if lenA < 64:
                                pieces.append(((tau0 + 1) % NT, 0, lenA, 64 - lenA))
                            for (tau, dco, sco, ln_) in pieces:
                                if tau == 0:
                                    rr = r_ot0
                                    def evw(v, dco=dco, sco=sco, ln_=ln_, hh=hh, w=w, pa_o=pa_o):
                                        return v.tensor_copy(
                                            o_ring[:, hh, 2, dco:dco + ln_],
                                            pa_o[:, w * 64 + sco: w * 64 + sco + ln_])
                                else:
                                    slot = tau % 2
                                    rr = r_or[slot]
                                    def evw(v, slot=slot, dco=dco, sco=sco, ln_=ln_, hh=hh, w=w, pa_o=pa_o):
                                        return v.tensor_copy(
                                            o_ring[:, hh, slot, dco:dco + ln_],
                                            pa_o[:, w * 64 + sco: w * 64 + sco + ln_])
                                tk.emit("dve", evw, reads=[r_pao], writes=[rr])

                    for hh_ in range(10):
                        if hh_ < 8:
                            stageAB(hh_)
                        if hh_ <= 1 and pend_u:
                            pend_u.pop(0)()
                        if 1 <= hh_ <= 8:
                            stageC(hh_ - 1)
                        if hh_ >= 2:
                            stageD(hh_ - 2)
                    if gj + 2 < 4:
                        emit_v(gj + 2)
                    if gj == 1:
                        emit_k(1)
                    tau = g
                    if tau == 0:
                        continue
                    slot = tau % 2

                    def emit_mmu(fh, slot=slot):
                        pu, r_pu = (pa_u, r_pau) if fh == 0 else (pa_u2, r_pau2)
                        def mmu(te, slot=slot, fh=fh, pu=pu):
                            for aj in range(8):
                                te.matmul(pu[:, :], o_ring[:, aj, slot, :],
                                          wot_stg[:, aj, fh * 512:(fh + 1) * 512],
                                          start=(aj == 0), stop=False)
                            return te.matmul(pu[:, :], ones_b[32:33, 0:128],
                                             rows_a[32:33, fh * 512:(fh + 1) * 512],
                                             start=False, stop=True)
                        tk.emit("pe", mmu, reads=[r_or[slot], r_wot, r_rows, r_const],
                                writes=[r_pu])

                    def emit_stt(fh, tau=tau):
                        pu, r_pu = (pa_u, r_pau) if fh == 0 else (pa_u2, r_pau2)
                        tk.emit("dve", lambda v, tau=tau, fh=fh, pu=pu: v.scalar_tensor_tensor(
                                    x_sb[:, tau, fh * 512:(fh + 1) * 512], pu[:, :], 2.0,
                                    x_sb[:, tau, fh * 512:(fh + 1) * 512], ALU.mult, ALU.add),
                                reads=[r_pu, r_x[tau]], writes=[r_x[tau]])
                    pend_u.append(lambda: emit_mmu(0))
                    pend_u.append(lambda: (emit_stt(0), emit_mmu(1)))
                    pend_u.append(lambda: emit_stt(1))
            for fn in pend_u:
                fn()
            pend_u.clear()
            for fh in range(2):
                pu, r_pu = (pa_u, r_pau) if fh == 0 else (pa_u2, r_pau2)
                def mmu0(te, fh=fh):
                    for aj in range(8):
                        te.matmul(pu[:, :], o_ring[:, aj, 2, :],
                                  wot_stg[:, aj, fh * 512:(fh + 1) * 512],
                                  start=(aj == 0), stop=False)
                    return te.matmul(pu[:, :], ones_b[32:33, 0:128],
                                     rows_a[32:33, fh * 512:(fh + 1) * 512],
                                     start=False, stop=True)
                tk.emit("pe", mmu0, reads=[r_ot0, r_wot, r_rows, r_const], writes=[r_pu])
                tk.emit("dve", lambda v, fh=fh, pu=pu: v.scalar_tensor_tensor(
                            x_sb[:, 0, fh * 512:(fh + 1) * 512], pu[:, :], 2.0,
                            x_sb[:, 0, fh * 512:(fh + 1) * 512], ALU.mult, ALU.add),
                        reads=[r_pu, r_x[0]], writes=[r_x[0]])
        tk.phase_end()

    NQT = T // 512

    def ffn(ci):
        with ExitStack() as _st:
            A = lambda nm, shape, dt: _st.enter_context(
                nc.sbuf_tensor(f"{nm}{ci}", shape, dt))
            simn32 = A("f_simn", [128, 8, N_EXP], F32)
            sgb = A("f_sgb", [128, N_EXP], F32)
            rw_buf = A("f_rw", [128, NT, N_EXP], F32)
            qf = A("f_qf", [64, T], BF16)
            kf = A("f_kf", [64, T], BF16)
            vaug = A("f_vaug", [128, NT, 65], BF16)
            gsc = A("f_gsc", [128, 8, N_EXP], F32)
            gcol = A("f_gcol", [128, 8], F32)
            nrm_q = A("f_nrmq", [128, NT], F32)
            nrm_k = A("f_nrmk", [128, NT], F32)
            ones2d = A("f_ones2d", [128, 64], F32)
            mbc = A("f_mbc", [128, 1], F32)
            sc1 = A("f_sc1", [1, 16], F32)
            srow = y_stg[:, 512:1024]
            R32 = y_stg[0:64, 0:512]

            r_simn, r_sgb, r_wcat = tk.R("f_simn"), tk.R("f_sgb"), tk.R("f_wcat")
            NTH = NT // 2
            r_rw = [tk.R("f_rw", t) for t in range(NT)]
            r_acc = [tk.R("f_acc", t) for t in range(NT)]
            r_qf, r_kf, r_vaug = tk.R("f_qf"), tk.R("f_kf"), tk.R("f_vaug")
            r_yq4, r_yq4f = tk.R("f_yq4"), tk.R("f_yq4f")
            r_gsc, r_gcol, r_nrm = tk.R("f_gsc"), tk.R("f_gcol"), tk.R("f_nrm")
            r_srow, r_R32, r_o2d = r_ystg, r_ystg, tk.R("f_o2d")
            r_mbc, r_sc1 = tk.R("f_mbc"), tk.R("f_sc1")

            tk.emit("dve", lambda v: v.memset(ones2d[:, :], 1.0), writes=[r_o2d])
            tk.emit("dve", lambda v: v.memset(vaug[:, :, 64:65], 1.0), writes=[r_vaug])
            with nc.sbuf_tensor(f"f_sim32_{ci}", [128, 8, N_EXP], F32) as sim32:
                r_sim32 = tk.R("f_sim32")
                tk.dma("qsp", sim32[:, :, :],
                       dw["f_sim"].ap().rearrange("(c p) e -> p c e", p=128),
                       writes=[r_sim32])
                with (nc.sbuf_tensor(f"f_smsq_{ci}", [128, 8, N_EXP], F32) as smsq,
                      nc.psum_tensor(f"f_pn_{ci}", [128, 512], F32) as pf_n):
                    r_smsq, r_pfn = tk.R("f_smsq"), tk.R("f_pn")
                    tk.emit("dve", lambda v: v.tensor_mul(smsq[:, :, :], sim32[:, :, :],
                                                          sim32[:, :, :]),
                            reads=[r_sim32], writes=[r_smsq])
                    def mmn(te):
                        last = None
                        for dc in range(8):
                            last = te.matmul(pf_n[0:1, 0:16], ones2d[:, 0:1],
                                             smsq[:, dc, :], start=(dc == 0), stop=(dc == 7))
                        return last
                    tk.emit("pe", mmn, reads=[r_smsq, r_o2d], writes=[r_pfn])
                    tk.emit("act", lambda a: a.activation(sc1[0:1, :], pf_n[0:1, 0:16], AF.Ln),
                            reads=[r_pfn], writes=[r_sc1])
                    tk.emit("dve", lambda v: v.tensor_scalar(sc1[0:1, :], sc1[0:1, :], -0.5,
                                                             None, ALU.mult),
                            reads=[r_sc1], writes=[r_sc1])
                    tk.emit("act", lambda a: a.activation(sc1[0:1, :], sc1[0:1, :], AF.Exp),
                            reads=[r_sc1], writes=[r_sc1])
                    tk.emit("pe", lambda te: te.matmul(pf_n[:, 32:48], ones_f[0:1, :],
                                                       sc1[0:1, :], start=True, stop=True),
                            reads=[r_sc1, r_const], writes=[r_pfn])
                    tk.emit("dve", lambda v: v.tensor_copy(sgb[:, :], pf_n[:, 32:48]),
                            reads=[r_pfn], writes=[r_sgb])
                    for dc in range(8):
                        tk.emit("dve", lambda v, dc=dc: v.tensor_mul(
                                    simn32[:, dc, :], sim32[:, dc, :], sgb[:, :]),
                                reads=[r_sim32, r_sgb], writes=[r_simn])
                    tk.dma("qsp", sc1[0:1, :], dw["f_gates"].ap().rearrange("(o n) -> o n", o=1),
                           writes=[r_sc1])
                    tk.emit("act", lambda a: a.activation(sc1[0:1, :], sc1[0:1, :], AF.Exp,
                                                          scale=-1.0),
                            reads=[r_sc1], writes=[r_sc1])
                    tk.emit("dve", lambda v: v.tensor_scalar(sc1[0:1, :], sc1[0:1, :], 1.0,
                                                             None, ALU.add),
                            reads=[r_sc1], writes=[r_sc1])
                    tk.emit("dve", lambda v: v.reciprocal(sc1[0:1, :], sc1[0:1, :]),
                            reads=[r_sc1], writes=[r_sc1])
                    tk.emit("pe", lambda te: te.matmul(pf_n[:, 64:80], ones_f[0:1, :],
                                                       sc1[0:1, :], start=True, stop=True),
                            reads=[r_sc1, r_const], writes=[r_pfn])
                    tk.emit("dve", lambda v: v.tensor_copy(sgb[:, :], pf_n[:, 64:80]),
                            reads=[r_pfn], writes=[r_sgb])
                    ln_stats(True)
            tk.phase_end()

            with ExitStack() as _s1:
                pf_tr2 = _s1.enter_context(nc.psum_tensor(f"f_ptr_{ci}", [128, 2, 512], F32))
                pf_raw = _s1.enter_context(nc.psum_tensor(f"f_praw_{ci}", [128, 16], F32))
                pf_gT = _s1.enter_context(nc.psum_tensor(f"f_pgT_{ci}", [128, 128], F32))
                praw_sb = _s1.enter_context(nc.sbuf_tensor(f"f_prs{ci}", [16, 128], F32))
                pf_q = _s1.enter_context(nc.psum_tensor(f"f_pq_{ci}", [128, 4, 512], F32))
                wcat = _s1.enter_context(nc.sbuf_tensor(f"f_wcat{ci}", [128, 8, 4, 384], BF16))
                acc = _s1.enter_context(nc.sbuf_tensor(f"f_acc{ci}", [128, NTH, 192], F32))
                yq42 = _s1.enter_context(nc.sbuf_tensor(f"f_yq4_{ci}", [128, 2, 8, 128], BF16))
                yq4f = _s1.enter_context(nc.sbuf_tensor(f"f_yq4f_{ci}", [128, 8, 128], F32))
                pf_tr = pf_tr2[:, 0, :]
                r_ptr2 = [tk.R("f_ptr", i) for i in range(2)]
                r_yq42 = [tk.R("f_yq4s", i) for i in range(2)]
                r_ptr, r_praw = r_ptr2[0], tk.R("f_praw")
                r_gT, r_prs = tk.R("f_gT"), tk.R("f_prs")
                r_pq = [tk.R("f_pq", i) for i in range(4)]
                def y_norm(t):
                    tk.emit("dve", lambda v, t=t: v.tensor_scalar(
                                y_stg[:, :], x_sb[:, t, :],
                                st_mu[:, t:t + 1], st_rstd[:, t:t + 1],
                                ALU.subtract, ALU.mult),
                            reads=[r_x[t], r_mu, r_rstd], writes=[r_ystg])

                def y_tr(t, with_f32):
                    sl = (t % NTH) % 2
                    for rnd in range(2):
                        def tr(te, rnd=rnd):
                            last = None
                            for c in range(4):
                                cc = rnd * 4 + c
                                last = te.transpose(pf_tr2[:, rnd, c * 128:(c + 1) * 128],
                                                    y_stg[:, cc * 128:(cc + 1) * 128],
                                                    ident_f[:, :])
                            return last
                        tk.emit("pe", tr, reads=[r_ystg, r_const], writes=[r_ptr2[rnd]])
                        tk.emit("dve", lambda v, rnd=rnd, sl=sl: v.tensor_copy(
                                    yq42[:, sl, rnd * 4:(rnd + 1) * 4, :],
                                    pf_tr2[:, rnd, :].rearrange("p (c n) -> p c n", c=4)),
                                reads=[r_ptr2[rnd]], writes=[r_yq42[sl]])
                        if with_f32:
                            tk.emit("dve", lambda v, rnd=rnd: v.tensor_copy(
                                        yq4f[:, rnd * 4:(rnd + 1) * 4, :],
                                        pf_tr2[:, rnd, :].rearrange("p (c n) -> p c n", c=4)),
                                    reads=[r_ptr2[rnd]], writes=[r_yq4f])

                for half, wave in [(h, w) for h in range(2) for w in range(2)]:
                    tk.dma("qpool", wcat[:, :, :, :],
                           d_wcat.ap()[:, :, wave * 8:(wave + 1) * 8, :].rearrange(
                               "p c (pr k) g -> p c pr (k g)", k=2),
                           reads=[r_wcd], writes=[r_wcat])
                    y_norm(half * NTH)
                    y_tr(half * NTH, wave == 0)
                    for tl in range(NTH):
                        t = half * NTH + tl
                        sl = tl % 2
                        yq4 = yq42[:, sl]
                        if wave == 0:
                            def mmr(te):
                                last = None
                                for dc in range(8):
                                    last = te.matmul(pf_gT[0:16, 0:128], simn32[:, dc, :],
                                                     yq4f[:, dc, :],
                                                     start=(dc == 0), stop=(dc == 7))
                                return last
                            tk.emit("pe", mmr, reads=[r_yq4f, r_simn], writes=[r_gT])
                            tk.emit("act", lambda a: a.copy(
                                        praw_sb[0:16, 0:128], pf_gT[0:16, 0:128]),
                                    reads=[r_gT], writes=[r_prs])
                            tk.emit("pe", lambda te: te.transpose(
                                        pf_raw[:, 0:16], praw_sb[0:16, 0:128],
                                        ident_f[0:16, 0:16]),
                                    reads=[r_prs, r_const], writes=[r_praw])
                            g0, g1, g2, g3, g4 = (gsc[:, i, :] for i in range(5))
                            m1, m2, iA, gs = (gcol[:, i:i + 1] for i in range(4))
                            tk.emit("dve", lambda v, t=t: v.tensor_scalar(
                                        g0, pf_raw[:, :], st_invy[:, t:t + 1], None, ALU.mult),
                                    reads=[r_praw, r_invy], writes=[r_gsc])
                            tk.emit("dve", lambda v: v.tensor_sub(g0, g0, sgb[:, :]),
                                    reads=[r_gsc, r_sgb], writes=[r_gsc])
                            tk.emit("dve", lambda v: v.tensor_scalar_max(g1, g0, 0.0),
                                    reads=[r_gsc], writes=[r_gsc])
                            tk.emit("dve", lambda v: v.tensor_single_scalar(g2, g0, 0.0, ALU.is_gt),
                                    reads=[r_gsc], writes=[r_gsc])
                            tk.emit("dve", lambda v: v.reduce_max(m1, g0, axis=AX.X),
                                    reads=[r_gsc], writes=[r_gcol])
                            tk.emit("dve", lambda v: v.tensor_single_scalar(iA, m1, 0.0, ALU.is_le),
                                    reads=[r_gcol], writes=[r_gcol])
                            tk.emit("dve", lambda v: v.tensor_scalar(g3, g0, m1, None, ALU.is_equal),
                                    reads=[r_gsc, r_gcol], writes=[r_gsc])
                            tk.emit("dve", lambda v: v.scalar_tensor_tensor(
                                        g4, g3, -1e30, g0, ALU.mult, ALU.add),
                                    reads=[r_gsc], writes=[r_gsc])
                            tk.emit("dve", lambda v: v.reduce_max(m2, g4, axis=AX.X),
                                    reads=[r_gsc], writes=[r_gcol])
                            tk.emit("dve", lambda v: v.tensor_scalar(g4, g4, m2, None, ALU.is_equal),
                                    reads=[r_gsc, r_gcol], writes=[r_gsc])
                            tk.emit("dve", lambda v: v.tensor_add(g3, g3, g4),
                                    reads=[r_gsc], writes=[r_gsc])
                            tk.emit("dve", lambda v: v.scalar_tensor_tensor(
                                        g2, g3, iA, g2, ALU.mult, ALU.add),
                                    reads=[r_gsc, r_gcol], writes=[r_gsc])
                            tk.emit("act", lambda a: a.activation(g1, g1, AF.Exp),
                                    reads=[r_gsc], writes=[r_gsc])
                            tk.emit("dve", lambda v: v.tensor_mul(g1, g1, g2),
                                    reads=[r_gsc], writes=[r_gsc])
                            tk.emit("dve", lambda v: v.reduce_sum(gs, g1, axis=AX.X),
                                    reads=[r_gsc], writes=[r_gcol])
                            tk.emit("dve", lambda v: v.reciprocal(gs, gs),
                                    reads=[r_gcol], writes=[r_gcol])
                            tk.emit("dve", lambda v, t=t: v.tensor_scalar(
                                        rw_buf[:, t, :], g1, gs, None, ALU.mult),
                                    reads=[r_gsc, r_gcol], writes=[r_rw[t]])
                        for pr in range(4):
                            def mmw(te, pr=pr, yq4=yq4):
                                last = None
                                for dc in range(8):
                                    last = te.matmul(pf_q[:, pr, 0:384], yq4[:, dc, :],
                                                     wcat[:, dc, pr, :],
                                                     start=(dc == 0), stop=(dc == 7))
                                return last
                            tk.emit("pe", mmw, reads=[r_yq42[sl], r_wcat], writes=[r_pq[pr]])
                        if tl + 1 < NTH:
                            y_norm(t + 1)
                        for pr in range(4):
                            for j in range(2):
                                e = wave * 8 + pr * 2 + j
                                if e == 0:
                                    tk.emit("dve", lambda v, t=t, tl=tl, pr=pr, j=j: v.tensor_scalar(
                                                acc[:, tl, :], pf_q[:, pr, j * 192:(j + 1) * 192],
                                                rw_buf[:, t, e:e + 1], None, ALU.mult),
                                            reads=[r_pq[pr], r_rw[t]], writes=[r_acc[t]])
                                else:
                                    tk.emit("dve", lambda v, t=t, tl=tl, pr=pr, j=j, e=e: v.scalar_tensor_tensor(
                                                acc[:, tl, :], pf_q[:, pr, j * 192:(j + 1) * 192],
                                                rw_buf[:, t, e:e + 1], acc[:, tl, :],
                                                ALU.mult, ALU.add),
                                            reads=[r_pq[pr], r_rw[t]], writes=[r_acc[t]])
                        if tl + 1 < NTH:
                            y_tr(t + 1, wave == 0)
                    if wave != 1:
                        continue
                    for tl2 in range(NTH):
                        t = half * NTH + tl2
                        tk.emit("pe", lambda te, tl2=tl2: te.transpose(
                                    pf_tr[0:64, 0:128], acc[:, tl2, 0:64], ident_f[:, :]),
                                reads=[r_acc[t], r_const], writes=[r_ptr])
                        tk.emit("dve", lambda v, t=t: v.tensor_copy(
                                    qf[0:64, t * 128:(t + 1) * 128], pf_tr[0:64, 0:128]),
                                reads=[r_ptr], writes=[r_qf])
                        tk.emit("pe", lambda te, tl2=tl2: te.transpose(
                                    pf_tr[0:64, 128:256], acc[:, tl2, 64:128], ident_f[:, :]),
                                reads=[r_acc[t], r_const], writes=[r_ptr])
                        tk.emit("dve", lambda v, t=t: v.tensor_copy(
                                    kf[0:64, t * 128:(t + 1) * 128], pf_tr[0:64, 128:256]),
                                reads=[r_ptr], writes=[r_kf])
                        tk.emit("dve", lambda v, t=t, tl2=tl2: v.tensor_copy(
                                    vaug[:, t, 0:64], acc[:, tl2, 128:192]),
                                reads=[r_acc[t]], writes=[r_vaug])
                        tk.emit("dve", lambda v, tl2=tl2: v.tensor_mul(
                                    yq4f[:, 0, 0:64], acc[:, tl2, 0:64], acc[:, tl2, 0:64]),
                                reads=[r_acc[t]], writes=[r_yq4f])
                        tk.emit("dve", lambda v, t=t: v.reduce_sum(
                                    nrm_q[:, t:t + 1], yq4f[:, 0, 0:64], axis=AX.X),
                                reads=[r_yq4f], writes=[r_nrm])
                        tk.emit("dve", lambda v, tl2=tl2: v.tensor_mul(
                                    yq4f[:, 0, 0:64], acc[:, tl2, 64:128], acc[:, tl2, 64:128]),
                                reads=[r_acc[t]], writes=[r_yq4f])
                        tk.emit("dve", lambda v, t=t: v.reduce_sum(
                                    nrm_k[:, t:t + 1], yq4f[:, 0, 0:64], axis=AX.X),
                                reads=[r_yq4f], writes=[r_nrm])
                tk.emit("dve", lambda v: v.reduce_max(gcol[:, 0:1], nrm_q[:, :], axis=AX.X),
                        reads=[r_nrm], writes=[r_gcol])
                tk.emit("dve", lambda v: v.reduce_max(gcol[:, 1:2], nrm_k[:, :], axis=AX.X),
                        reads=[r_nrm], writes=[r_gcol])
                tk.emit("dve", lambda v: v.tensor_add(gcol[:, 2:3], gcol[:, 0:1], gcol[:, 1:2]),
                        reads=[r_gcol], writes=[r_gcol])
                tk.emit("pe", lambda te: te.transpose(pf_tr[0:1, 0:128], gcol[:, 2:3],
                                                      ident_f[:, :]),
                        reads=[r_gcol, r_const], writes=[r_ptr])
                tk.emit("dve", lambda v: v.reduce_max(sc1[0:1, 0:1], pf_tr[0:1, 0:128], axis=AX.X),
                        reads=[r_ptr], writes=[r_sc1])
                tk.emit("dve", lambda v: v.tensor_scalar(sc1[0:1, 0:1], sc1[0:1, 0:1],
                                                         -0.525 * SCALE_F, None, ALU.mult),
                        reads=[r_sc1], writes=[r_sc1])
                tk.emit("pe", lambda te: te.matmul(pf_raw[:, 0:1], ones_f[0:1, :],
                                                   sc1[0:1, 0:1], start=True, stop=True),
                        reads=[r_sc1, r_const], writes=[r_praw])
                tk.emit("dve", lambda v: v.tensor_copy(mbc[:, :], pf_raw[:, 0:1]),
                        reads=[r_praw], writes=[r_mbc])
            tk.phase_end()

            af = _st.enter_context(nc.sbuf_tensor(f"f_af{ci}", [64, T], BF16))
            r_af = tk.R("f_af")
            with ExitStack() as _s3:
                pf_sc = _s3.enter_context(nc.psum_tensor(f"f_psc_{ci}", [128, 2, 1536], F32))
                pf_out = _s3.enter_context(nc.psum_tensor(f"f_pout_{ci}", [65, 512], F32))
                pf_msc = _s3.enter_context(nc.psum_tensor(f"f_pmsc_{ci}", [128, 512], F32))
                ptst = _s3.enter_context(nc.sbuf_tensor(f"f_ptst{ci}", [128, 2, 1536], BF16))
                masks = _s3.enter_context(nc.sbuf_tensor(f"f_masks{ci}", [128, 4, 512], BF16))
                r_psc = [tk.R("f_psc", i) for i in range(2)]
                r_pout, r_pmsc = tk.R("f_pout"), tk.R("f_pmsc")
                r_ptst = [tk.R("f_ptst", i) for i in range(2)]
                r_masks, r_io = tk.R("f_masks"), tk.R("f_io")
                tk.emit("pool", lambda g: g.iota(y_stg[:, 0:512], [[1, 512]],
                                                 channel_multiplier=0,
                                                 allow_small_or_imprecise_dtypes=True),
                        writes=[r_ystg])
                for r in range(4):
                    tk.emit("dve", lambda v, r=r: v.tensor_scalar(
                                y_stg[:, 512:1024], y_stg[:, 0:512], prow[:, :],
                                float(-128 * r), ALU.subtract, ALU.add),
                            reads=[r_ystg, r_const], writes=[r_ystg])
                    tk.emit("dve", lambda v, r=r: v.tensor_single_scalar(
                                y_stg[:, 512:1024], y_stg[:, 512:1024], 0.0, ALU.is_lt),
                            reads=[r_ystg], writes=[r_ystg])
                    tk.emit("dve", lambda v, r=r: v.tensor_scalar(
                                masks[:, r, :], y_stg[:, 512:1024], -1e30, None, ALU.mult),
                            reads=[r_ystg], writes=[r_masks])
                pend3 = []
                for qt in range(NQT):
                    nkc = 4 * (qt + 1)
                    groups = [list(range(g, min(g + 3, nkc))) for g in range(0, nkc, 3)]

                    def mmsc(te, grp, par, qt=qt):
                        last = None
                        for j, kc in enumerate(grp):
                            last = te.matmul(pf_sc[:, par, j * 512:(j + 1) * 512],
                                             kf[0:64, kc * 128:(kc + 1) * 128],
                                             qf[0:64, qt * 512:(qt + 1) * 512],
                                             start=True, stop=True)
                        return last

                    def emit_sc(gi):
                        par = gi % 2
                        grp = groups[gi]
                        tk.emit("pe", lambda te, grp=grp, par=par: mmsc(te, grp, par),
                                reads=[r_kf, r_qf], writes=[r_psc[par]])
                        for j, kc in enumerate(grp):
                            r = kc - 4 * qt
                            if 0 <= r <= 3:
                                tk.emit("dve", lambda v, par=par, j=j, r=r: v.tensor_tensor(
                                            pf_sc[:, par, j * 512:(j + 1) * 512],
                                            pf_sc[:, par, j * 512:(j + 1) * 512],
                                            masks[:, r, :], ALU.add),
                                        reads=[r_masks], writes=[r_psc[par]])

                    def mmpv(te, grp, par, nkc=nkc):
                        last = None
                        for j, kc in enumerate(grp):
                            last = te.matmul(pf_out[:, :], vaug[:, kc, :],
                                             ptst[:, par, j * 512:(j + 1) * 512],
                                             start=(kc == 0), stop=(kc == nkc - 1))
                        return last

                    emit_sc(0)
                    for fn in pend3:
                        fn()
                    pend3.clear()
                    for gi, grp in enumerate(groups):
                        par = gi % 2
                        ng = len(grp)
                        if gi > 0:
                            tk.emit("pe", lambda te, gi=gi: mmpv(te, groups[gi - 1], (gi - 1) % 2),
                                    reads=[r_vaug, r_ptst[(gi - 1) % 2]], writes=[r_pout])
                        tk.emit("act", lambda a, par=par, ng=ng: a.activation(
                                    ptst[:, par, 0:ng * 512], pf_sc[:, par, 0:ng * 512],
                                    AF.Exp, bias=mbc[:, :], scale=SCALE_F),
                                reads=[r_psc[par], r_mbc], writes=[r_ptst[par]])
                        if gi + 1 < len(groups):
                            emit_sc(gi + 1)
                    tk.emit("pe", lambda te: mmpv(te, groups[-1], (len(groups) - 1) % 2),
                            reads=[r_vaug, r_ptst[(len(groups) - 1) % 2]], writes=[r_pout])
                    tk.emit("dve", lambda v, qt=qt: v.tensor_copy(
                                af[0:64, qt * 512:(qt + 1) * 512], pf_out[0:64, :]),
                            reads=[r_pout], writes=[r_af])
                    tk.emit("dve", lambda v: v.tensor_copy(srow[64:65, :], pf_out[64:65, :]),
                            reads=[r_pout], writes=[r_srow])
                    tk.emit("act", lambda a: a.activation(srow[64:65, :], srow[64:65, :], AF.Ln),
                            reads=[r_srow], writes=[r_srow])
                    tk.emit("act", lambda a: a.activation(srow[64:65, :], srow[64:65, :],
                                                          AF.Exp, scale=-1.0),
                            reads=[r_srow], writes=[r_srow])
                    tk.emit("pe", lambda te: te.matmul(pf_msc[0:64, :], ones2d[64:65, :],
                                                       srow[64:65, :], start=True, stop=True),
                            reads=[r_srow, r_o2d], writes=[r_pmsc])
                    tk.emit("act", lambda a: a.copy(R32[:, :], pf_msc[0:64, :]),
                            reads=[r_pmsc], writes=[r_R32])

                    def tail3(qt=qt):
                        tk.emit("dve", lambda v, qt=qt: v.tensor_mul(
                                    af[0:64, qt * 512:(qt + 1) * 512],
                                    af[0:64, qt * 512:(qt + 1) * 512], R32[:, :]),
                                reads=[r_R32], writes=[r_af])
                    pend3.append(tail3)
                for fn in pend3:
                    fn()
            tk.phase_end()

            with ExitStack() as _s4:
                pf_trb = _s4.enter_context(nc.psum_tensor(f"f_ptrb_{ci}", [128, 2, 512], BF16))
                pf_ab = _s4.enter_context(nc.psum_tensor(f"f_pab_{ci}", [128, 2, 64], BF16))
                pf_u = _s4.enter_context(nc.psum_tensor(f"f_pu_{ci}", [128, 512], F32))
                pf_u2 = _s4.enter_context(nc.psum_tensor(f"f_pu2_{ci}", [128, 512], F32))
                atok = _s4.enter_context(nc.sbuf_tensor(f"f_atok{ci}", [128, NT, 64], BF16))
                opcat = _s4.enter_context(nc.sbuf_tensor(f"f_opcat{ci}", [128, 8, 1024], BF16))
                acat = _s4.enter_context(nc.sbuf_tensor(f"f_acat{ci}", [128, 2, D], BF16))
                afm = _s4.enter_context(nc.sbuf_tensor(f"f_afm{ci}", [128, 2, 8, 128], BF16))
                r_ptrb = [tk.R("f_ptrb", i) for i in range(2)]
                r_pab = [tk.R("f_pab", i) for i in range(2)]
                r_pu, r_pu2 = tk.R("f_pu"), tk.R("f_pu2")
                r_atok, r_opc = tk.R("f_atok"), tk.R("f_opc")
                r_acat = [tk.R("f_acat", i) for i in range(2)]
                r_afm = [tk.R("f_afm", i) for i in range(2)]
                tk.dma("qpool", opcat[:, :, :], d_wob.ap(),
                       reads=[r_wcd], writes=[r_opc])

                def trf_emit(t):
                    sl = t % 2
                    tk.emit("pe", lambda te, t=t, sl=sl: te.transpose(
                                pf_ab[:, sl, :], af[0:64, t * 128:(t + 1) * 128],
                                ident_b[0:64, 0:64]),
                            reads=[r_af, r_const], writes=[r_pab[sl]])
                    tk.emit("dve", lambda v, t=t, sl=sl: v.tensor_copy(atok[:, t, :],
                                                                       pf_ab[:, sl, :]),
                            reads=[r_pab[sl]], writes=[r_atok])

                def acat_emit(t):
                    sl = t % 2
                    for e in range(N_EXP):
                        tk.emit("dve", lambda v, t=t, e=e, sl=sl: v.tensor_scalar(
                                    acat[:, sl, e * 64:(e + 1) * 64], atok[:, t, :],
                                    rw_buf[:, t, e:e + 1], None, ALU.mult),
                                reads=[r_atok, r_rw[t]], writes=[r_acat[sl]])

                trf_emit(0)
                acat_emit(0)
                for t in range(NT):
                    sl = t % 2
                    if t + 1 < NT:
                        trf_emit(t + 1)
                    for rnd in range(2):
                        def tra(te, rnd=rnd, sl=sl):
                            last = None
                            for c in range(4):
                                cc = rnd * 4 + c
                                last = te.transpose(pf_trb[:, sl, c * 128:(c + 1) * 128],
                                                    acat[:, sl, cc * 128:(cc + 1) * 128],
                                                    ident_b[:, :])
                            return last
                        tk.emit("pe", tra, reads=[r_acat[sl], r_const], writes=[r_ptrb[sl]])
                        tk.emit("dve", lambda v, rnd=rnd, sl=sl: v.tensor_copy(
                                    afm[:, sl, rnd * 4:(rnd + 1) * 4, :],
                                    pf_trb[:, sl, :].rearrange("p (c n) -> p c n", c=4)),
                                reads=[r_ptrb[sl]], writes=[r_afm[sl]])
                    for fh in range(2):
                        pu, r_puu = (pf_u, r_pu) if fh == 0 else (pf_u2, r_pu2)
                        def mmu(te, fh=fh, pu=pu, sl=sl):
                            last = None
                            for c in range(8):
                                last = te.matmul(pu[:, :], afm[:, sl, c, :],
                                                 opcat[:, c, fh * 512:(fh + 1) * 512],
                                                 start=(c == 0), stop=(c == 7))
                            return last
                        tk.emit("pe", mmu, reads=[r_afm[sl], r_opc], writes=[r_puu])
                    if t + 1 < NT:
                        acat_emit(t + 1)
                    for fh in range(2):
                        pu, r_puu = (pf_u, r_pu) if fh == 0 else (pf_u2, r_pu2)
                        tk.emit("dve", lambda v, t=t, fh=fh, pu=pu: v.tensor_tensor(
                                    x_sb[:, t, fh * 512:(fh + 1) * 512], pu[:, :],
                                    x_sb[:, t, fh * 512:(fh + 1) * 512], ALU.add),
                                reads=[r_puu, r_x[t]], writes=[r_x[t]])
        tk.phase_end()

    def emit_attn_b_pair(qr, su, prl, pr, pb_sc, rb, pb_oA, pb_oB,
                         r_poA, r_poB, kp, r_kp, qq, r_qq, vp, r_vp,
                         ptstB, r_ptstB, mba, r_mba, o_q, r_oq,
                         ostg, r_ostg, QTS, NQTQ, NKC,
                         tail_st, stg_h0, r_tailr, r_tailb, r_tstg, r_rbcd, d_rbc,
                         r_qsq2):
        pend_tail = []
        last_tail = [None]
        for qt in range(NQTQ):
            qcol = qt * QTS
            groups = [list(range(g, min(g + 2, NKC))) for g in range(0, NKC, 2)]

            def mmsc(te, grp, h01, qcol=qcol):
                last = None
                for kc in grp:
                    jb = kc % 3
                    last = te.matmul(
                        pb_sc[:, h01, jb * 512:jb * 512 + QTS],
                        kp[h01 * 64:(h01 + 1) * 64, kc * 128:(kc + 1) * 128],
                        qq[h01 * 64:(h01 + 1) * 64, pr, qcol:qcol + QTS],
                        start=True, stop=True,
                        tile_position=(h01 * 64, 0))
                return last

            def mmpv(te, grp, h01, gpar):
                pb_o = pb_oA if h01 == 0 else pb_oB
                last = None
                for i, kc in enumerate(grp):
                    last = te.matmul(pb_o[0:65, 0:QTS],
                                     vp[:, kc, 65 * h01:65 * h01 + 65],
                                     ptstB[:, h01, gpar, i * QTS:(i + 1) * QTS],
                                     start=(kc == 0), stop=(kc == NKC - 1))
                return last

            def emit_sc(gi):
                grp = groups[gi]
                for h01 in range(2):
                    tk.emit("pe", lambda te, h01=h01, grp=grp: mmsc(te, grp, h01),
                            reads=[r_kp, r_qq],
                            writes=[rb[h01][kc % 3] for kc in grp])

            def emit_exp(gi):
                grp = groups[gi]
                gpar = gi % 2
                jbs = [kc % 3 for kc in grp]
                for h01 in range(2):
                    bias = mba[:, h01 * 8 + pr:h01 * 8 + pr + 1]
                    if QTS == 512 and len(grp) == 2 and jbs[1] == jbs[0] + 1:
                        tk.emit("act", lambda a, h01=h01, gpar=gpar, j0=jbs[0]: a.activation(
                                    ptstB[:, h01, gpar, 0:1024],
                                    pb_sc[:, h01, j0 * 512:(j0 + 2) * 512],
                                    AF.Exp, bias=mba[:, h01 * 8 + pr:h01 * 8 + pr + 1],
                                    scale=SCALE_B),
                                reads=[rb[h01][jbs[0]], rb[h01][jbs[1]], r_mba],
                                writes=[r_ptstB[h01][gpar]])
                    else:
                        for i, kc in enumerate(grp):
                            tk.emit("act", lambda a, h01=h01, gpar=gpar, i=i, jb=jbs[i]: a.activation(
                                        ptstB[:, h01, gpar, i * QTS:(i + 1) * QTS],
                                        pb_sc[:, h01, jb * 512:jb * 512 + QTS],
                                        AF.Exp, bias=mba[:, h01 * 8 + pr:h01 * 8 + pr + 1],
                                        scale=SCALE_B),
                                    reads=[rb[h01][jbs[i]], r_mba],
                                    writes=[r_ptstB[h01][gpar]])

            emit_sc(0)
            for fn in pend_tail:
                fn()
            pend_tail.clear()
            for gi, grp in enumerate(groups):
                if gi > 0:
                    for h01 in range(2):
                        r_po = r_poA if h01 == 0 else r_poB
                        tk.emit("pe", lambda te, h01=h01, gi=gi: mmpv(
                                    te, groups[gi - 1], h01, (gi - 1) % 2),
                                reads=[r_vp, r_ptstB[h01][(gi - 1) % 2]], writes=[r_po])
                emit_exp(gi)
                if gi + 1 < len(groups):
                    emit_sc(gi + 1)
            for h01 in range(2):
                r_po = r_poA if h01 == 0 else r_poB
                tk.emit("pe", lambda te, h01=h01: mmpv(
                            te, groups[-1], h01, (len(groups) - 1) % 2),
                        reads=[r_vp, r_ptstB[h01][(len(groups) - 1) % 2]], writes=[r_po])
            for h01 in range(2):
                pb_o = pb_oA if h01 == 0 else pb_oB
                r_po = r_poA if h01 == 0 else r_poB
                sl = h01
                stg = stg_h0[:, 0:QTS] if h01 == 0 else ostg[:, 0:QTS]
                wl_ = ([r_tstg[0], r_qsq2[1]] if h01 == 0
                       else [r_tstg[1], r_qsq2[0], r_ystgb])
                tk.emit("dve", lambda v, pb_o=pb_o, stg=stg: v.tensor_copy(
                            stg, pb_o[0:64, 0:QTS]),
                        reads=[r_po], writes=wl_)
                tk.emit("dve", lambda v, pb_o=pb_o, sl=sl: v.tensor_copy(
                            tail_st[64:65, sl * 512:sl * 512 + QTS], pb_o[64:65, 0:QTS]),
                        reads=[r_po], writes=[r_tailr[sl]])
            for sl in range(2):
                tk.emit("act", lambda a, sl=sl: a.activation(
                            tail_st[64:65, sl * 512:sl * 512 + QTS],
                            tail_st[64:65, sl * 512:sl * 512 + QTS], AF.Ln),
                        reads=[r_tailr[sl]], writes=[r_tailr[sl]])
                tk.emit("act", lambda a, sl=sl: a.activation(
                            tail_st[64:65, sl * 512:sl * 512 + QTS],
                            tail_st[64:65, sl * 512:sl * 512 + QTS], AF.Exp, scale=-1.0),
                        reads=[r_tailr[sl]], writes=[r_tailr[sl]])
                tk.dma("qsp", d_rbc.ap()[sl:sl + 1, 0:QTS],
                       tail_st[64:65, sl * 512:sl * 512 + QTS],
                       reads=[r_tailr[sl]], writes=[r_rbcd[sl]])
                dsrc = d_rbc.ap()[sl:sl + 1, 0:QTS]
                dsrc_b = bass.AP(dsrc.tensor, dsrc.offset, [[0, 64], [1, QTS]])
                tk.dma("qsp", tail_st[0:64, sl * 512:sl * 512 + QTS], dsrc_b,
                       reads=[r_rbcd[sl]], writes=[r_tailb[sl]])

            def tail_b(qt=qt, qcol=qcol):
                for h01 in range(2):
                    sl = h01
                    if h01 == 0:
                        tk.emit("dve", lambda v, prl=prl, qcol=qcol, sl=sl: v.tensor_mul(
                                    o_q[0:64, prl, qcol:qcol + QTS],
                                    stg_h0[:, 0:QTS],
                                    tail_st[0:64, sl * 512:sl * 512 + QTS]),
                                reads=[r_tstg[sl], r_tailb[sl]], writes=[r_oq])
                    else:
                        tk.emit("dve", lambda v, sl=sl: v.tensor_mul(
                                    ostg[:, 0:QTS], ostg[:, 0:QTS],
                                    tail_st[0:64, sl * 512:sl * 512 + QTS]),
                                reads=[r_tailb[sl]],
                                writes=[r_tstg[1]])
                        tk.dma("qpool", o_q[64:128, prl, qcol:qcol + QTS], ostg[:, 0:QTS],
                               reads=[r_tstg[1]], writes=[r_oq])
            if qt + 1 < NQTQ:
                pend_tail.append(tail_b)
            else:
                last_tail[0] = tail_b
        return last_tail[0]

    def attn_b():
        NT4 = NT // 4
        QL = T // 4
        NKC = NT
        QTS = min(512, QL)
        NQTQ = max(1, QL // QTS)
        GT = 4 if NT4 % 4 == 0 else (2 if NT4 % 2 == 0 else 1)
        npc = max(1, QL // 512)
        qplen = min(512, QL)
        with ExitStack() as _st:
            A = lambda nm, shape, dt: _st.enter_context(nc.sbuf_tensor(nm, shape, dt))
            blfm = A("b_blfm", [128, 2], F32)
            bqfm = A("b_bqfm", [128, 8], F32)
            bkfm = A("b_bkfm", [128, 8], F32)
            bvfm = A("b_bvfm", [128, 8], F32)
            rows_b = A("b_rows", [1, 640], BF16)
            ybq = A("b_ybq", [128, 8, 128 * GT], BF16)
            H2 = A("b_H2", [128, 2], BF16)
            mqk = A("b_mqk", [128, 12], F32)
            mks = A("b_mks", [128, 8], F32)
            kp = A("b_kp", [128, T], BF16)
            vp = A("b_vp", [128, NT, 130], BF16)
            wk_stg = A("b_wks", [128, 1, 2, 128], BF16)
            wv_stg = A("b_wvs", [128, 1, 2, 128], BF16)
            _lat_cm = nc.sbuf_tensor("b_lat", [128, 2, T], BF16)
            latent = _lat_cm.__enter__()
            mba = small
            ostg = y_stgb[0:64, 0:512]

            r_lat, r_wl = tk.R("b_lat"), tk.R("b_wl")
            r_blfm, r_bqfm, r_bkfm, r_rows = tk.R("b_blfm"), tk.R("b_bqfm"), tk.R("b_bkfm"), tk.R("b_rows")
            r_bvfm = tk.R("b_bvfm")
            r_kald, r_vald = tk.R("d_kall"), tk.R("d_vall")
            r_ybq = tk.R("b_ybq")
            r_wqs = [tk.R("b_wqs", i) for i in range(2)]
            r_wks = [tk.R("b_wks", i) for i in range(2)]
            r_wvs = [tk.R("b_wvs", i) for i in range(2)]
            r_wos = tk.R("b_wos")
            r_qq, r_kp, r_vp = tk.R("b_qq"), tk.R("b_kp"), tk.R("b_vp")
            r_oq = tk.R("b_oq")
            r_ptstB = [[tk.R("b_ptst", i, p) for p in range(2)] for i in range(2)]
            r_H2, r_mqk = tk.R("b_H2"), tk.R("b_mqk")
            r_mks, r_mba = tk.R("b_mks"), r_small
            r_qsq2 = [tk.R("b_qsqA"), tk.R("b_qsqB")]
            r_tailr = [tk.R("b_tailr", i) for i in range(2)]
            r_tailb = [tk.R("b_tailb", i) for i in range(2)]
            r_tstg = [tk.R("b_tstg", i) for i in range(2)]
            r_rbcd = [tk.R("b_rbcd", i) for i in range(2)]
            r_ostg = r_ystgb

            tk.emit("dve", lambda v: v.memset(H2[:, :], 0.0), writes=[r_H2])
            tk.emit("dve", lambda v: v.memset(H2[0:64, 0:1], 1.0), writes=[r_H2])
            tk.emit("dve", lambda v: v.memset(H2[64:128, 1:2], 1.0), writes=[r_H2])
            r_wqd = tk.R("b_wqd")
            tk.dma("qpool", d_wqb.ap(),
                   dw["b_wq"].ap().rearrange("(c p) (fc g) -> p fc c g", p=128, g=128),
                   writes=[r_wqd])
            with (nc.psum_tensor("b_ptr", [128, 512], BF16) as pb_tr,
                  nc.psum_tensor("b_pl", [128, 2, 512], F32) as pb_l,
                  nc.psum_tensor("b_pb", [128, 2], F32) as pb_b,
                  nc.sbuf_tensor("b_wl_sb", [128, 8, 256], BF16) as wl):
                tk.dma("qpool", wl[:, :, :],
                       dw["b_wl"].ap().rearrange("(c p) f -> p c f", p=128), writes=[r_wl])
                r_pbtr, r_pbb = tk.R("b_ptr"), tk.R("b_pb")
                r_pbl = [tk.R("b_pl", i) for i in range(2)]
                for dstb, r_dstb, src, n in ((blfm, r_blfm, dw["b_bl"], 2),
                                             (bqfm, r_bqfm, dw["b_bq"], 8),
                                             (bkfm, r_bkfm, dw["b_bk"], 8),
                                             (bvfm, r_bvfm, dw["b_bv"], 8)):
                    for c in range(n):
                        tk.dma("qsp", y_stg[0:1, 0:128],
                               src.ap().rearrange("(o n) -> o n", o=1)[:, c * 128:(c + 1) * 128],
                               writes=[r_ystg])
                        tk.emit("pe", lambda te: te.transpose(pb_b[:, 0:1], y_stg[0:1, 0:128],
                                                              ident_f[0:1, 0:1]),
                                reads=[r_ystg, r_const], writes=[r_pbb])
                        tk.emit("dve", lambda v, dstb=dstb, c=c: v.tensor_copy(
                                    dstb[:, c:c + 1], pb_b[:, 0:1]),
                                reads=[r_pbb], writes=[r_dstb])
                ln_stats(False)
                for g0 in range(0, NT, GT):
                    for j in range(GT):
                        y_tile_bf(g0 + j, pb_tr, r_pbtr, ybq, r_ybq, j * 128)
                    for lc in range(2):
                        def mml(te, lc=lc, n=128 * GT):
                            last = None
                            for dc in range(8):
                                last = te.matmul(pb_l[:, lc, 0:n], wl[:, dc, lc * 128:(lc + 1) * 128],
                                                 ybq[:, dc, 0:n], start=(dc == 0), stop=(dc == 7))
                            return last
                        tk.emit("pe", mml, reads=[r_wl, r_ybq], writes=[r_pbl[lc]])
                        tk.emit("dve", lambda v, lc=lc, g0=g0, n=128 * GT: v.tensor_scalar(
                                    latent[:, lc, g0 * 128: g0 * 128 + n], pb_l[:, lc, 0:n],
                                    blfm[:, lc:lc + 1], None, ALU.add),
                                reads=[r_pbl[lc], r_blfm], writes=[r_lat])
            tk.phase_end()

            tail_st = y_stg
            stg_h0 = y_stgb[0:64, 512:1024]

            with (nc.psum_tensor("kva_A", [128, 512], F32) as pvA,
                  nc.psum_tensor("kva_B", [128, 512], F32) as pvB,
                  nc.psum_tensor("kva_V0", [128, 512], F32) as pvV0,
                  nc.psum_tensor("kva_V1", [128, 512], F32) as pvV1,
                  nc.psum_tensor("kva_T", [128, 512], F32) as ptrK,
                  nc.named_scope("bKVall", notify=True)):
                r_pvK = [tk.R("kva_A"), tk.R("kva_B")]
                r_pvV = [tk.R("kva_V", i) for i in range(2)]
                r_ptrK = [tk.R("kva_T", j) for j in range(4)]
                r_vtmp = [tk.R("kva_vt", i) for i in range(2)]
                vfm2 = [y_stg[:, 0:512], y_stg[:, 512:1024]]
                tk.emit("dve", lambda v: v.memset(vp[:, :, 64:130:65], 1.0), writes=[r_vp])
                for pri, pr8 in enumerate([1, 2, 3, 4, 5, 6, 7, 0]):
                    tk.dma("qpool", wk_stg[:, 0, :, :],
                           dw["b_wk"].ap().rearrange("(c p) f -> p c f", p=128)[
                               :, :, pr8 * 128:(pr8 + 1) * 128],
                           writes=[r_wks[0]])
                    tk.dma("qpool", wv_stg[:, 0, :, :],
                           dw["b_wv"].ap().rearrange("(c p) f -> p c f", p=128)[
                               :, :, pr8 * 128:(pr8 + 1) * 128],
                           writes=[r_wvs[0]])
                    for ct in range(8):
                        pk, r_pk = (pvA, r_pvK[0]) if ct % 2 == 0 else (pvB, r_pvK[1])
                        def mmk(te, ct=ct, pk=pk):
                            last = None
                            for lc in range(2):
                                last = te.matmul(pk[:, :], wk_stg[:, 0, lc, :],
                                                 latent[:, lc, ct * 512:(ct + 1) * 512],
                                                 start=(lc == 0), stop=(lc == 1))
                            return last
                        tk.emit("pe", mmk, reads=[r_wks[0], r_lat], writes=[r_pk])
                        tk.emit("dve", lambda v, pr8=pr8, ct=ct, pk=pk: v.tensor_scalar(
                                    kp[:, ct * 512:(ct + 1) * 512], pk[:, :],
                                    bkfm[:, pr8:pr8 + 1], None, ALU.add),
                                reads=[r_pk, r_bkfm], writes=[r_kp])
                    tk.dma("qsp", d_kall.ap()[pr8], kp[:, :], reads=[r_kp], writes=[r_kald])
                    for ct in range(8):
                        par = ct % 2
                        pv, r_pv = (pvV0, r_pvV[0]) if par == 0 else (pvV1, r_pvV[1])
                        def mmv(te, ct=ct, pv=pv):
                            last = None
                            for lc in range(2):
                                last = te.matmul(pv[:, :], wv_stg[:, 0, lc, :],
                                                 latent[:, lc, ct * 512:(ct + 1) * 512],
                                                 start=(lc == 0), stop=(lc == 1))
                            return last
                        tk.emit("pe", mmv, reads=[r_wvs[0], r_lat], writes=[r_pv])
                        tk.emit("dve", lambda v, par=par, pv=pv, pr8=pr8: v.tensor_scalar(
                                    vfm2[par], pv[:, :],
                                    bvfm[:, pr8:pr8 + 1], None, ALU.add),
                                reads=[r_pv, r_bvfm], writes=[r_ystg])
                        def trv(te, par=par):
                            last = None
                            for j in range(4):
                                last = te.transpose(
                                    ptrK[:, j * 128:(j + 1) * 128],
                                    y_stg[:, par * 512 + j * 128: par * 512 + (j + 1) * 128],
                                    ident_f[:, :])
                            return last
                        tk.emit("pe", trv, reads=[r_ystg, r_const], writes=[r_ptrK[0]])
                        sl_v = par
                        tk.emit("dve", lambda v, sl_v=sl_v: v.tensor_copy(
                                    ybq[:, sl_v * 4:(sl_v + 1) * 4, 0:128],
                                    ptrK[:, :].rearrange("p (c n) -> p c n", c=4)),
                                reads=[r_ptrK[0]], writes=[r_vtmp[sl_v]])
                        for j in range(4):
                            tk.emit("dve", lambda v, ct=ct, j=j, sl_v=sl_v: v.tensor_copy(
                                        vp[:, ct * 4 + j, :].rearrange(
                                            "p (h c) -> p h c", h=2)[:, :, 0:64],
                                        ybq[:, sl_v * 4 + j, 0:128].rearrange(
                                            "p (h c) -> p h c", h=2)),
                                    reads=[r_vtmp[sl_v]], writes=[r_vp])
                    tk.dma("qsp", d_vall.ap()[pr8].rearrange("p (a c) -> p a c", a=NT),
                           vp[:, :, :], reads=[r_vp], writes=[r_vald])
                    tk.emit("dve", lambda v, pr8=pr8: v.memset(mks[0:2, pr8:pr8 + 1], 0.0),
                            writes=[r_mks])
                    npk = T // 512

                    def mk_mul(pc):
                        half = pc % 2
                        qv = y_stgb[:, half * 512:(half + 1) * 512]
                        wl_ = [r_qsq2[half]]
                        if pc < 2:
                            wl_ += [r_ystgb, r_tstg[1 - half]]
                        tk.emit("dve", lambda v, pc=pc, qv=qv: v.tensor_mul(
                                    qv, kp[:, pc * 512:(pc + 1) * 512],
                                    kp[:, pc * 512:(pc + 1) * 512]),
                                reads=[r_kp], writes=wl_)

                    def mk_mm(pc):
                        half = pc % 2
                        pk, r_pk = (pvA, r_pvK[0]) if pc % 2 == 0 else (pvB, r_pvK[1])
                        qv = y_stgb[:, half * 512:(half + 1) * 512]
                        tk.emit("pe", lambda te, pk=pk, qv=qv: te.matmul(
                                    pk[0:2, :], H2[:, :],
                                    qv, start=True, stop=True),
                                reads=[r_qsq2[half], r_H2], writes=[r_pk])

                    def mk_red(pc):
                        pk, r_pk = (pvA, r_pvK[0]) if pc % 2 == 0 else (pvB, r_pvK[1])
                        tk.emit("dve", lambda v, pk=pk: v.reduce_max(
                                    mqk[0:2, 10:11], pk[0:2, :], axis=AX.X),
                                reads=[r_pk], writes=[r_mqk])
                        tk.emit("dve", lambda v, pr8=pr8: v.tensor_max(
                                    mks[0:2, pr8:pr8 + 1], mks[0:2, pr8:pr8 + 1],
                                    mqk[0:2, 10:11]),
                                reads=[r_mqk, r_mks], writes=[r_mks])

                    for i in range(npk + 2):
                        if i < npk:
                            mk_mul(i)
                        if 1 <= i <= npk:
                            mk_mm(i - 1)
                        if i >= 2:
                            mk_red(i - 2)
            tk.phase_end()
            _lat_cm.__exit__(None, None, None)
            wq_stg = A("b_wqs", [128, 1, 8, 128], BF16)
            wo_stg = A("b_wos", [128, 2, 512], BF16)
            qq = A("b_qq", [128, 8, QL], BF16)
            o_q = A("b_oq", [128, 2, QL], BF16)
            ptstB = A("b_ptst", [128, 2, 2, 2 * QTS], BF16)
            vpB = A("b_vpB", [128, NT, 130], BF16)
            r_vpB = tk.R("b_vpB")

            with (nc.psum_tensor("b_psc", [128, 2, 1536], F32) as pb_sc,
                  nc.psum_tensor("b_poA", [128, 512], F32) as pb_oA,
                  nc.psum_tensor("b_poB", [128, 512], F32) as pb_oB):
                rb = [[tk.R("b_psc", h, j) for j in range(3)] for h in range(2)]
                r_poA, r_poB = tk.R("b_poA"), tk.R("b_poB")

                def qproj(qr):
                    _q = nc.named_scope(f"bQ{qr}", notify=True); _q.__enter__()
                    def dmah(fc, h):
                        tk.dma("qpool", wq_stg[:, 0, h * 4:(h + 1) * 4, :],
                               d_wqb.ap()[:, fc, h * 4:(h + 1) * 4, :],
                               reads=[r_wqd], writes=[r_wqs[h]])
                    for g0 in range(0, NT4, GT):
                        for j in range(GT):
                            y_tile_fb(qr * NT4 + g0 + j, pb_sc[:, 0, 0:512], rb[0][0],
                                      ybq, r_ybq, j * 128,
                                      extra_w=[r_tailr[0], r_tailr[1],
                                               r_tailb[0], r_tailb[1]])
                        dmah(0, 0)
                        dmah(0, 1)
                        for fc in range(8):
                            pq, r_pq = (pb_oA, r_poA) if fc % 2 == 0 else (pb_oB, r_poB)
                            def mmqh(te, h, pq=pq, n=128 * GT):
                                last = None
                                for dc in range(h * 4, h * 4 + 4):
                                    last = te.matmul(pq[:, 0:n],
                                                     wq_stg[:, 0, dc, :],
                                                     ybq[:, dc, 0:n],
                                                     start=(dc == 0), stop=(dc == 7))
                                return last
                            tk.emit("pe", lambda te, mmqh=mmqh: mmqh(te, 0),
                                    reads=[r_wqs[0], r_ybq], writes=[r_pq])
                            if fc + 1 < 8:
                                dmah(fc + 1, 0)
                            tk.emit("pe", lambda te, mmqh=mmqh: mmqh(te, 1),
                                    reads=[r_wqs[1], r_ybq], writes=[r_pq])
                            if fc + 1 < 8:
                                dmah(fc + 1, 1)
                            tk.emit("dve", lambda v, fc=fc, pq=pq, g0=g0, n=128 * GT: v.tensor_scalar(
                                        qq[:, fc, g0 * 128: g0 * 128 + n],
                                        pq[:, 0:n],
                                        bqfm[:, fc:fc + 1], None, ALU.add),
                                    reads=[r_pq, r_bqfm], writes=[r_qq])
                    _q.__exit__(None, None, None)

                def normbatch(qr):
                    _n = nc.named_scope(f"bN{qr}", notify=True); _n.__enter__()
                    nsi = 8 * npc

                    def nb_mul(si):
                        pr8, pc = si // npc, si % npc
                        half = si % 2
                        qv = y_stgb[:, half * 512: half * 512 + qplen]
                        wl_ = [r_qsq2[half]]
                        if si < 2:
                            wl_ += [r_ystgb, r_tstg[1 - half]]
                        tk.emit("dve", lambda v, pr8=pr8, pc=pc, qv=qv: v.tensor_mul(
                                    qv, qq[:, pr8, pc * 512: pc * 512 + qplen],
                                    qq[:, pr8, pc * 512: pc * 512 + qplen]),
                                reads=[r_qq], writes=wl_)

                    def nb_mm(si):
                        half = si % 2
                        s6 = si % 6
                        h_, j_ = s6 // 3, s6 % 3
                        qv = y_stgb[:, half * 512: half * 512 + qplen]
                        tk.emit("pe", lambda te, h_=h_, j_=j_, qv=qv: te.matmul(
                                    pb_sc[0:2, h_, j_ * 512: j_ * 512 + qplen], H2[:, :],
                                    qv, start=True, stop=True),
                                reads=[r_qsq2[half], r_H2], writes=[rb[h_][j_]])

                    def nb_red(si):
                        pr8, pc = si // npc, si % npc
                        s6 = si % 6
                        h_, j_ = s6 // 3, s6 % 3
                        dcol = (pr8, pr8 + 1) if pc == 0 else (8, 9)
                        tk.emit("dve", lambda v, h_=h_, j_=j_, dcol=dcol: v.reduce_max(
                                    mqk[0:2, dcol[0]:dcol[1]],
                                    pb_sc[0:2, h_, j_ * 512: j_ * 512 + qplen], axis=AX.X),
                                reads=[rb[h_][j_]], writes=[r_mqk])
                        if pc > 0:
                            tk.emit("dve", lambda v, pr8=pr8: v.tensor_max(
                                        mqk[0:2, pr8:pr8 + 1], mqk[0:2, pr8:pr8 + 1],
                                        mqk[0:2, 8:9]),
                                    reads=[r_mqk], writes=[r_mqk])

                    for i in range(nsi + 2):
                        if i < nsi:
                            nb_mul(i)
                        if 1 <= i <= nsi:
                            nb_mm(i - 1)
                        if i >= 2:
                            nb_red(i - 2)
                    tk.emit("dve", lambda v: v.memset(ybq[0:2, 0:2, 0:128], 1.0), writes=[r_ybq])
                    tk.emit("dve", lambda v: v.tensor_scalar(
                                ybq[0:2, 1, 0:128], ybq[0:2, 1, 0:128], prow[0:2, 0:1],
                                None, ALU.mult),
                            reads=[r_const], writes=[r_ybq])
                    tk.emit("dve", lambda v: v.tensor_sub(
                                ybq[0:2, 0, 0:128], ybq[0:2, 0, 0:128], ybq[0:2, 1, 0:128]),
                            reads=[r_ybq], writes=[r_ybq])
                    _n.__exit__(None, None, None)

                def mbc_combine_all():
                    tk.emit("dve", lambda v: v.tensor_add(mqk[0:2, 0:8], mqk[0:2, 0:8],
                                                          mks[0:2, 0:8]),
                            reads=[r_mqk, r_mks], writes=[r_mqk])
                    tk.emit("dve", lambda v: v.tensor_scalar(mqk[0:2, 0:8], mqk[0:2, 0:8],
                                                             -0.525 * SCALE_B, None, ALU.mult),
                            reads=[r_mqk], writes=[r_mqk])
                    tk.emit("dve", lambda v: v.tensor_copy(y_stgb[0:2, 0:8], mqk[0:2, 0:8]),
                            reads=[r_mqk], writes=[r_qsq2[0], r_ystgb, r_tstg[1]])
                    for h01 in range(2):
                        pq, r_pq = (pb_oA, r_poA) if h01 == 0 else (pb_oB, r_poB)
                        tk.emit("pe", lambda te, h01=h01, pq=pq: te.matmul(
                                    pq[:, 0:8], ybq[0:2, h01, 0:128], y_stgb[0:2, 0:8],
                                    start=True, stop=True),
                                reads=[r_ybq, r_qsq2[0]], writes=[r_pq])
                        tk.emit("dve", lambda v, h01=h01, pq=pq: v.tensor_copy(
                                    mba[:, h01 * 8:h01 * 8 + 8], pq[:, 0:8]),
                                reads=[r_pq], writes=[r_mba])

                def load_kp(pr8):
                    tk.dma("qsp", kp[:, :], d_kall.ap()[pr8],
                           reads=[r_kald], writes=[r_kp])

                def load_vp(pr8, vpd, r_vpd):
                    tk.dma("qpool", vpd[:, :, :],
                           d_vall.ap()[pr8].rearrange("p (a c) -> p a c", a=NT),
                           reads=[r_vald], writes=[r_vpd])

                def prefetch_wo(su, fh):
                    tk.dma("qpool", wo_stg[:, 0:2, :],
                           dw["b_wo"].ap().rearrange("(c p) f -> p c f", p=128)[
                               :, su * 2:(su + 1) * 2, fh * 512:(fh + 1) * 512],
                           writes=[r_wos])
                    if su == 0:
                        tk.dma("qpool", rows_b[0:1, 128:640],
                               dw["b_bo"].ap().rearrange("(o n) -> o n", o=1)[
                                   :, fh * 512:(fh + 1) * 512],
                               writes=[r_rows])

                def u_super(qr, su):
                    _u = nc.named_scope(f"bU{qr}_{su}", notify=True); _u.__enter__()
                    for fh in range(2):
                        if fh == 1:
                            prefetch_wo(su, 1)
                        for tci in range(NT4):
                            t = qr * NT4 + tci
                            def mmu(te, tci=tci, fh=fh, su=su):
                                last = None
                                for c in range(2):
                                    last = te.matmul(pb_sc[:, fh, 0:512],
                                                     o_q[:, c, tci * 128:(tci + 1) * 128],
                                                     wo_stg[:, c, :],
                                                     start=(c == 0),
                                                     stop=(c == 1 and su != 0))
                                if su == 0:
                                    last = te.matmul(pb_sc[:, fh, 0:512], ones_b[0:1, 0:128],
                                                     rows_b[0:1, 128:640],
                                                     start=False, stop=True)
                                return last
                            tk.emit("pe", mmu, reads=[r_oq, r_wos, r_rows, r_const],
                                    writes=[rb[fh][0]])
                            tk.emit("dve", lambda v, t=t, fh=fh: v.tensor_tensor(
                                        x_sb[:, t, fh * 512:(fh + 1) * 512],
                                        pb_sc[:, fh, 0:512],
                                        x_sb[:, t, fh * 512:(fh + 1) * 512], ALU.add),
                                    reads=[rb[fh][0], r_x[t]], writes=[r_x[t]])
                    _u.__exit__(None, None, None)

                for qr in range(4):
                    qproj(qr)
                    normbatch(qr)
                    mbc_combine_all()
                    for su in range(4):
                        for prl in range(2):
                            pr = su * 2 + prl
                            gidx = qr * 8 + pr
                            nxt_pr = (gidx + 1) % 8 if gidx + 1 < 32 else None
                            if prl == 1:
                                prefetch_wo(su, 0)
                            vp_c, r_vp_c = (vp, r_vp) if gidx % 2 == 0 else (vpB, r_vpB)
                            if nxt_pr is not None:
                                vp_n, r_vp_n = (vpB, r_vpB) if gidx % 2 == 0 else (vp, r_vp)
                                load_vp(nxt_pr, vp_n, r_vp_n)
                            _at = nc.named_scope(f"bAT{qr}_{pr}", notify=True); _at.__enter__()
                            tail_fin = emit_attn_b_pair(
                                             qr, su, prl, pr, pb_sc, rb, pb_oA, pb_oB,
                                             r_poA, r_poB, kp, r_kp, qq, r_qq, vp_c, r_vp_c,
                                             ptstB, r_ptstB, mba, r_mba, o_q, r_oq,
                                             ostg, r_ostg, QTS, NQTQ, NKC,
                                             tail_st, stg_h0, r_tailr, r_tailb,
                                             r_tstg, r_rbcd, d_rbc, r_qsq2)
                            _at.__exit__(None, None, None)
                            if nxt_pr is not None:
                                load_kp(nxt_pr)
                            tail_fin()
                        u_super(qr, su)
            tk.phase_end()
        tk.phase_end()



    def ffn_prestage():
        for e in range(N_EXP):
            for typ, wsrc in enumerate((dw["f_q"], dw["f_k"], dw["f_v"])):
                tk.dma("qpool", d_wcat.ap()[:, :, e, typ * 64:(typ + 1) * 64],
                       wsrc.ap()[e].rearrange("(c p) d -> p c d", p=128),
                       writes=[r_wcd])
        tk.dma("qpool", d_wob.ap(),
               dw["f_o"].ap().rearrange("e h f -> (e h) f").rearrange(
                   "(c p) f -> p c f", p=128),
               writes=[r_wcd])

    if 0 in sublayers:
        with nc.named_scope("attn_a_prep", notify=True):
            attn_a_prep()
        with nc.named_scope("attn_a", notify=True):
            attn_a(after_bias=(ffn_prestage if (1 in sublayers or 3 in sublayers)
                               else None))
    elif 1 in sublayers or 3 in sublayers:
        ffn_prestage()
    if 1 in sublayers:
        with nc.named_scope("ffn1", notify=True):
            ffn(1)
    if 2 in sublayers:
        with nc.named_scope("attn_b", notify=True):
            attn_b()
    if 3 in sublayers:
        with nc.named_scope("ffn3", notify=True):
            ffn(3)
    if 0 in d_dbg:
        store_x(d_dbg[0], 0)
        tk.phase_end()

    store_x(d_out, 99)
    tk.phase_end()
    return nc


def np_ln(x):
    mu = x.mean(-1, keepdims=True)
    var = x.var(-1, keepdims=True)
    return (x - mu) / np.sqrt(var + LN_EPS)


def np_attn_a(x, w, b, ow, ob):
    T, Dm = x.shape
    y = np_ln(x)
    xs = np.roll(y, -SHIFT, axis=0)
    nw = T // 64
    xw = xs.reshape(nw, 64, Dm)
    qkv = xw @ w.T + b
    q, k, v = np.split(qkv, 3, axis=-1)
    H = 8
    hd = Dm // H
    def heads(t):
        return t.reshape(nw, 64, H, hd).transpose(0, 2, 1, 3)
    q, k, v = heads(q), heads(k), heads(v)
    s = np.einsum("nhqd,nhkd->nhqk", q, k) / np.sqrt(hd)
    s = s - s.max(-1, keepdims=True)
    p = np.exp(s)
    p /= p.sum(-1, keepdims=True)
    o = np.einsum("nhqk,nhkd->nhqd", p, v)
    o = o.transpose(0, 2, 1, 3).reshape(T, Dm)
    o = o @ ow.T + ob
    o = np.roll(o, SHIFT, axis=0)
    return x + 2 * o


def np_ffn(x, sim, gates, qp, kp, vp, op):
    T, Dm = x.shape
    h = np_ln(x)
    nf = h / np.maximum(np.linalg.norm(h, axis=-1, keepdims=True), 1e-12)
    ns = sim / np.maximum(np.linalg.norm(sim, axis=0, keepdims=True), 1e-12)
    logits = nf @ ns - 1.0 / (1.0 + np.exp(-gates))
    gated = np.maximum(logits, 0.0)
    mask = (gated > 0).astype(np.float32)
    inactive = mask.sum(1) == 0
    fb_idx = np.argsort(-logits, axis=1, kind="stable")[:, :2]
    fb = np.zeros_like(mask)
    np.put_along_axis(fb, fb_idx, 1.0, axis=1)
    mask = np.where(inactive[:, None] & (fb > 0), 1.0, mask)
    ml = np.where(mask > 0, gated, -np.inf)
    ml = ml - ml.max(1, keepdims=True)
    rw = np.exp(ml)
    rw /= rw.sum(1, keepdims=True)
    q = np.einsum("te,ted->td", rw, np.einsum("tc,ecd->ted", h, qp))
    k = np.einsum("te,ted->td", rw, np.einsum("tc,ecd->ted", h, kp))
    v = np.einsum("te,ted->td", rw, np.einsum("tc,ecd->ted", h, vp))
    s = (q @ k.T) / np.sqrt(HDF)
    causal = np.tril(np.ones((T, T), dtype=bool))
    s = np.where(causal, s, -np.inf)
    s = s - s.max(1, keepdims=True)
    p = np.exp(s)
    p /= p.sum(1, keepdims=True)
    a = p @ v
    mixed = a[:, None, :] * rw[:, :, None]
    return x + np.einsum("ted,edc->tc", mixed, op)


def np_attn_b(x, wq, bq, wl, bl, wk, bk, wv, bv, wo, bo):
    T, Dm = x.shape
    y = np_ln(x)
    H, Dh = 16, 64
    q = (y @ wq + bq).reshape(T, H, Dh)
    lat = y @ wl + bl
    k = (lat @ wk + bk).reshape(T, H, Dh)
    v = (lat @ wv + bv).reshape(T, H, Dh)
    s = np.einsum("ihd,jhd->hij", q, k) / np.sqrt(Dh)
    s = s - s.max(-1, keepdims=True)
    p = np.exp(s)
    p /= p.sum(-1, keepdims=True)
    o = np.einsum("hij,jhd->ihd", p, v).reshape(T, Dm)
    return x + o @ wo + bo


def np_reference(x, W, sublayers=(0, 1, 2, 3)):
    if 0 in sublayers:
        x = np_attn_a(x, W["a_w"], W["a_b"], W["a_ow"], W["a_ob"])
    if 1 in sublayers:
        x = np_ffn(x, W["f_sim"], W["f_gates"], W["f_q"], W["f_k"], W["f_v"], W["f_o"])
    if 2 in sublayers:
        x = np_attn_b(x, W["b_wq"], W["b_bq"], W["b_wl"], W["b_bl"], W["b_wk"],
                      W["b_bk"], W["b_wv"], W["b_bv"], W["b_wo"], W["b_bo"])
    if 3 in sublayers:
        x = np_ffn(x, W["f_sim"], W["f_gates"], W["f_q"], W["f_k"], W["f_v"], W["f_o"])
    return x


def make_weights(rng, scale=0.02):
    W = {
        "a_w": rng.standard_normal((3 * D, D), dtype=np.float32) * scale,
        "a_b": rng.standard_normal((3 * D,), dtype=np.float32) * 0.5,
        "a_ow": rng.standard_normal((D, D), dtype=np.float32) * scale,
        "a_ob": rng.standard_normal((D,), dtype=np.float32) * 0.5,
        "b_wq": rng.standard_normal((D, D), dtype=np.float32) * D ** -0.5,
        "b_bq": rng.standard_normal((D,), dtype=np.float32) * 0.5,
        "b_wl": rng.standard_normal((D, LATENT), dtype=np.float32) * D ** -0.5,
        "b_bl": rng.standard_normal((LATENT,), dtype=np.float32) * 0.5,
        "b_wk": rng.standard_normal((LATENT, D), dtype=np.float32) * LATENT ** -0.5,
        "b_bk": rng.standard_normal((D,), dtype=np.float32) * 0.5,
        "b_wv": rng.standard_normal((LATENT, D), dtype=np.float32) * LATENT ** -0.5,
        "b_bv": rng.standard_normal((D,), dtype=np.float32) * 0.5,
        "b_wo": rng.standard_normal((D, D), dtype=np.float32) * D ** -0.5,
        "b_bo": rng.standard_normal((D,), dtype=np.float32) * 0.5,
        "f_sim": rng.standard_normal((D, N_EXP), dtype=np.float32),
        "f_gates": rng.standard_normal((N_EXP,), dtype=np.float32),
        "f_q": (rng.random((N_EXP, D, HDF), dtype=np.float32) - 0.5) * 0.148,
        "f_k": (rng.random((N_EXP, D, HDF), dtype=np.float32) - 0.5) * 0.148,
        "f_v": (rng.random((N_EXP, D, HDF), dtype=np.float32) - 0.5) * 0.148,
        "f_o": (rng.random((N_EXP, HDF, D), dtype=np.float32) - 0.5) * 0.148,
    }
    return W


def kernel(**inputs):
    from concourse.bass_utils import run_bass_kernel_spmd
    x = np.ascontiguousarray(inputs["L0_attn_a_x"])
    B = x.shape[0]
    wmap = {
        "a_w": "L0_attn_a_in_proj_w", "a_b": "L0_attn_a_in_proj_b",
        "a_ow": "L0_attn_a_out_w", "a_ob": "L0_attn_a_out_b",
        "b_wq": "L0_attn_b_Wq", "b_bq": "L0_attn_b_bq",
        "b_wl": "L0_attn_b_Wl", "b_bl": "L0_attn_b_bl",
        "b_wk": "L0_attn_b_Wk", "b_bk": "L0_attn_b_bk",
        "b_wv": "L0_attn_b_Wv", "b_bv": "L0_attn_b_bv",
        "b_wo": "L0_attn_b_Wo", "b_bo": "L0_attn_b_bo",
        "f_sim": "L0_ffn_sim_matrix", "f_gates": "L0_ffn_gates",
        "f_q": "L0_ffn_q_proj", "f_k": "L0_ffn_k_proj",
        "f_v": "L0_ffn_v_proj", "f_o": "L0_ffn_o_proj",
    }
    weights = {k: np.ascontiguousarray(np.asarray(inputs[v], dtype=np.float32))
               for k, v in wmap.items()}
    nc = build_program(x.shape[1])
    in_maps = []
    for b in range(B):
        m = {"x": np.ascontiguousarray(x[b])}
        m.update(weights)
        in_maps.append(m)
    res = run_bass_kernel_spmd(nc, in_maps, core_ids=list(range(B)))
    return np.stack([np.asarray(res.results[b]["out"]) for b in range(B)], axis=0)
